# version 12
# baseline (speedup 1.0000x reference)
import numpy as np
import ml_dtypes
from contextlib import ExitStack

import concourse.bass as bass
import concourse.mybir as mybir
import bass_rust
import concourse.library_config as libcfg
from concourse.bass import IndirectOffsetOnAxis
from concourse.bass_utils import run_bass_kernel_spmd

F32 = mybir.dt.float32
BF16 = mybir.dt.bfloat16
I16 = mybir.dt.int16
I32 = mybir.dt.int32
U8 = mybir.dt.uint8
ALU = mybir.AluOpType
AF = mybir.ActivationFunctionType
AX = mybir.AxisListType.X

P = 128
DM = 1024
NTOK = 4096
NBLK = 32
EXP = 8
CAP = 5120
DFF = 4096
H = 4
DK = 128
DV = 256
NCHUNK = 32
NSPAN = 4
SSPAN = 256
NSSPAN = CAP // SSPAN
NCORE = 8
EPS_LN = 1e-5
EPS_ON = 1e-5
EPS_K = 1e-12
SCALE_Q = float(DK) ** -0.5
PAD_NEG = -3.0e38
PERT = 1e-6


class Prog:

    def __init__(self, nc, es):
        self.nc = nc
        self.es = es
        self.ops = {e: [] for e in "TVSGY"}
        self.sems = {}
        self.cnt = {}
        self.last = {}

    def sem(self, name):
        if name not in self.sems:
            self.sems[name] = self.es.enter_context(self.nc.semaphore(name))
            self.cnt[name] = 0
        return self.sems[name]

    BULK = ("q_y", "q_g", "q_s")

    def op(self, eng, fn, waits=(), inc=None, amt=1):
        ws = []
        for w in waits:
            if w is None:
                continue
            s, v = w
            if s in self.BULK:
                v = self.cnt.get(s, 0)
            ws.append((s, v))
        if eng in "VSG" and amt == 1 and self.last.get(eng) is not None:
            ws.append(self.last[eng])
        tok = None
        if inc is not None:
            self.sem(inc)
            self.cnt[inc] += amt
            tok = (inc, self.cnt[inc])
            if eng in "VSG" and amt == 1:
                self.last[eng] = tok
        self.ops[eng].append((fn, ws, inc, amt))
        return tok

    def dma(self, eng, out, in_, waits=(), sem=None, **kw):
        q = sem or {"Y": "q_y", "G": "q_g", "S": "q_s"}[eng]
        return self.op(eng, lambda e: e.dma_start(out=out, in_=in_, **kw),
                       waits, inc=q, amt=16)

    def fence(self):
        return [(s, self.cnt[s]) for s in self.cnt if self.cnt[s] > 0]

    def emit(self):
        nc = self.nc
        with nc.Block() as blk:
            decos = {"T": blk.tensor, "V": blk.vector, "S": blk.scalar,
                     "G": blk.gpsimd, "Y": blk.sync}
            for eng in "TVSGY":
                if not self.ops[eng]:
                    continue
                ops = self.ops[eng]

                def run(e, ops=ops):
                    seen = {}
                    for fn, ws, inc, amt in ops:
                        for (s, v) in ws:
                            if seen.get(s, -1) < v:
                                e.wait_ge(self.sems[s], v)
                                seen[s] = v
                        inst = fn(e)
                        if inc is not None:
                            inst.then_inc(self.sems[inc], amt)
                decos[eng](run)


def build(mode="full"):
    nc = bass.Bass("TRN2", target_bir_lowering=False)

    x_in = nc.dram_tensor("x", [NTOK, DM], F32, kind="ExternalInput")
    Wqd = nc.dram_tensor("Wq", [DM, H * DK], F32, kind="ExternalInput")
    Wkd = nc.dram_tensor("Wk", [DM, H * DK], F32, kind="ExternalInput")
    Wvd = nc.dram_tensor("Wv", [DM, H * DV], F32, kind="ExternalInput")
    Wbd = nc.dram_tensor("Wb", [DM, H], F32, kind="ExternalInput")
    Wgd = nc.dram_tensor("Wg", [DM, H * DV], F32, kind="ExternalInput")
    Wod = nc.dram_tensor("Wo", [H * DV, DM], F32, kind="ExternalInput")
    NWd = nc.dram_tensor("NW", [DV], F32, kind="ExternalInput")
    Wrd = nc.dram_tensor("Wr", [DM, EXP], F32, kind="ExternalInput")
    W1d = nc.dram_tensor("W1e", [DM, DFF], F32, kind="ExternalInput")
    W2d = nc.dram_tensor("W2e", [DFF, DM], F32, kind="ExternalInput")
    identd = nc.dram_tensor("ident", [P, P], BF16, kind="ExternalInput")
    slmaskd = nc.dram_tensor("slmask", [P, P], BF16, kind="ExternalInput")
    utmaskd = nc.dram_tensor("utmask", [P, P], BF16, kind="ExternalInput")
    ones8fd = nc.dram_tensor("ones8f", [EXP, 1], F32, kind="ExternalInput")
    ones8bd = nc.dram_tensor("ones8b", [EXP, 1], BF16, kind="ExternalInput")
    ones1bd = nc.dram_tensor("ones1b", [1, P], BF16, kind="ExternalInput")
    onescbd = nc.dram_tensor("onescb", [P, 1], BF16, kind="ExternalInput")
    pert8d = nc.dram_tensor("pert8", [EXP, 1], F32, kind="ExternalInput")
    eoff8d = nc.dram_tensor("eoff8", [EXP, 1], F32, kind="ExternalInput")
    selmatd = nc.dram_tensor("selmat", [NCORE * EXP, EXP], F32, kind="ExternalInput")
    lselmatd = nc.dram_tensor("lselmat", [NCORE * EXP, NCORE], F32, kind="ExternalInput")
    selcold = nc.dram_tensor("selcol", [EXP, P], F32, kind="ExternalInput")
    tokidd = nc.dram_tensor("tokid", [P, 256], I16, kind="ExternalInput")
    ident4d = nc.dram_tensor("ident4", [P, 512], BF16, kind="ExternalInput")
    ut4d = nc.dram_tensor("ut4c", [P, 512], BF16, kind="ExternalInput")
    coreoffd = nc.dram_tensor("coreoff", [1, 1], I32, kind="ExternalInput")
    out_t = nc.dram_tensor("out", [NTOK, DM], F32, kind="ExternalOutput")

    resid1 = nc.dram_tensor("resid1", [NTOK, DM], F32)
    resid2 = nc.dram_tensor("resid2", [NTOK, DM], F32)
    xln_in = nc.dram_tensor("xln_in", [NTOK, DM], BF16)
    xln_all = nc.dram_tensor("xln_all", [NCORE * NTOK, DM], BF16, addr_space="Shared")
    spm_in = nc.dram_tensor("spm_in", [EXP, NTOK], F32)
    spm_all = nc.dram_tensor("spm_all", [NCORE * EXP, NTOK], F32, addr_space="Shared")
    eout_in = nc.dram_tensor("eout_in", [CAP, DM], BF16)
    QCAP = CAP // 4
    eout_all = nc.dram_tensor("eout_all", [4 * NCORE * QCAP, DM], BF16, addr_space="Shared")
    tblrow = nc.dram_tensor("tblrow", [1, 5376], I16)
    tblD = nc.dram_tensor("tblD", [P, 5376], I16)
    pinvrow = nc.dram_tensor("pinvrow", [1, NTOK], F32)
    RG = [list(range(NCORE))]

    es_top = ExitStack()
    es = es_top.enter_context
    pr = Prog(nc, es_top)
    greg = es(nc.gpsimd.register("greg"))

    identb = es(nc.sbuf_tensor("identb", [P, P], BF16))
    slm = es(nc.sbuf_tensor("slm", [P, P], BF16))
    utm = es(nc.sbuf_tensor("utm", [P, P], BF16))
    on8f = es(nc.sbuf_tensor("on8f", [EXP, 1], F32))
    on8b = es(nc.sbuf_tensor("on8b", [EXP, 1], BF16))
    on1b = es(nc.sbuf_tensor("on1b", [1, P], BF16))
    oncb = es(nc.sbuf_tensor("oncb", [P, 1], BF16))
    p8 = es(nc.sbuf_tensor("p8", [EXP, 1], F32))
    e8 = es(nc.sbuf_tensor("e8", [EXP, 1], F32))
    selm = es(nc.sbuf_tensor("selm", [NCORE * EXP, EXP], F32))
    lselm = es(nc.sbuf_tensor("lselm", [NCORE * EXP, NCORE], F32))
    selc = es(nc.sbuf_tensor("selc", [EXP, P], F32))
    tok16 = es(nc.sbuf_tensor("tok16", [P, 256], I16))
    id4 = es(nc.sbuf_tensor("id4", [P, 512], BF16))
    ut4 = es(nc.sbuf_tensor("ut4", [P, 512], BF16))
    wrsb = es(nc.sbuf_tensor("wrsb", [P, EXP * EXP], BF16))
    nwsb = es(nc.sbuf_tensor("nwsb", [P, 2], F32))
    co32 = es(nc.sbuf_tensor("co32", [1, 1], I32))
    flat32 = es(nc.sbuf_tensor("flat32", [P, NBLK], I32))
    af_t = es(nc.sbuf_tensor("af_t", [P, NBLK], F32))
    bf_t = es(nc.sbuf_tensor("bf_t", [P, NBLK], F32))
    msum = es(nc.sbuf_tensor("msum", [P, NBLK], F32))
    sqsum = es(nc.sbuf_tensor("sqsum", [P, NBLK], F32))
    mval_t = es(nc.sbuf_tensor("mval_t", [P, NBLK], F32))
    var_t = es(nc.sbuf_tensor("var_t", [P, NBLK], F32))
    rcp_t = es(nc.sbuf_tensor("rcp_t", [P, NBLK], F32))
    rstd_t = es(nc.sbuf_tensor("rstd_t", [P, NBLK], F32))
    nm_t = es(nc.sbuf_tensor("nm_t", [P, NBLK], F32))
    lntrash = es(nc.sbuf_tensor("lntrash", [P, DM], BF16))
    xringT = [es(nc.sbuf_tensor(f"xringT{i}", [P, DM], F32)) for i in range(2)]
    xlnbT = [es(nc.sbuf_tensor(f"xlnbT{i}", [P, DM], BF16)) for i in range(2)]
    xafT = es(nc.sbuf_tensor("xafT", [P, EXP * 512], BF16))
    psTRg = es(nc.psum_tensor("psTRg", [P, 1024], BF16))
    tblT32 = es(nc.sbuf_tensor("tblT32", [P, CAP // P], I32))

    tcon = []
    for dst, src in [(identb, identd), (slm, slmaskd), (utm, utmaskd),
                     (on8f, ones8fd), (on8b, ones8bd), (on1b, ones1bd),
                     (oncb, onescbd),
                     (p8, pert8d), (e8, eoff8d), (selm, selmatd),
                     (lselm, lselmatd), (selc, selcold), (tok16, tokidd),
                     (co32, coreoffd), (id4, ident4d), (ut4, ut4d)]:
        tcon.append(pr.dma("Y", dst[:, :], src[:, :]))
    t_wr = pr.dma("G", wrsb[:, :].rearrange("p (k c) -> p k c", c=EXP),
                  bass.AP(Wrd, 0, [[EXP, P], [P * EXP, EXP], [1, EXP]]),
                  sem="s_wr")
    tcon.append(t_wr)
    t_nw = pr.dma("Y", nwsb[:, :], bass.AP(NWd, 0, [[1, P], [P, 2]]),
                  allow_slow_non_contiguous=True)
    tcon.append(t_nw)
    import os as _os
    if _os.environ.get("SKIPLIB0"):
        t_lib0 = None
    else:
        t_lib0 = pr.op("G", lambda g: (g.load_library(libcfg.attn),
                                       g.nop())[1],
                       pr.fence(), inc="slib")
        tcon.append(t_lib0)
    tok16_tok = tcon[11]
    co_tok = tcon[12]

    def emit_ln(src, deps, sink, xring, xlnb, sC, sS, blk0=0, nblk=NBLK,
                ring_tok=None, blk_deps=None):
        GRP = 2
        toks = [None] * (blk0 + nblk)
        t_s = [None] * (blk0 + nblk)
        t_x = [None] * (blk0 + nblk)
        t_v = [None] * (blk0 + nblk)
        for g in range(blk0 // GRP, (blk0 + nblk) // GRP):
            for j in range(GRP * g, GRP * g + GRP):
                rb = xring[j % GRP]
                w = list(deps) if blk_deps is None else list(blk_deps[j])
                if j - GRP >= blk0:
                    w += [toks[j - GRP], t_s[j - GRP]]
                elif ring_tok is not None and ring_tok[j % GRP] is not None:
                    w += [ring_tok[j % GRP]]
                t_x[j] = pr.dma("Y", rb[:, :], src[j * P:(j + 1) * P, :],
                                waits=w, sem=f"lnx{j % GRP}")
                t_v[j] = pr.op("V", lambda v, j=j, rb=rb: v.reduce_sum(
                    msum[:, j:j + 1], rb[:, :], AX), [t_x[j]], inc=sC)
                t_s[j] = pr.op("S", lambda s, j=j, rb=rb: s.activation(
                    lntrash[:, :], rb[:, :], AF.Square,
                    accum_out=sqsum[:, j:j + 1]),
                    [t_x[j], t_s[j - 1] if j else None], inc=sS)
            j = GRP * g + GRP - 1
            sl = slice(j - GRP + 1, j + 1)
            pr.op("V", lambda v, sl=sl: v.tensor_scalar(
                mval_t[:, sl], msum[:, sl], 1.0 / DM, None, ALU.mult),
                [t_v[j]], inc=sC)
            pr.op("V", lambda v, sl=sl: v.tensor_scalar(
                var_t[:, sl], sqsum[:, sl], 1.0 / DM, EPS_LN, ALU.mult,
                ALU.add), [t_s[j]], inc=sC)
            pr.op("V", lambda v, sl=sl: v.tensor_mul(
                rcp_t[:, sl], mval_t[:, sl], mval_t[:, sl]), inc=sC)
            pr.op("V", lambda v, sl=sl: v.tensor_sub(
                var_t[:, sl], var_t[:, sl], rcp_t[:, sl]), inc=sC)
            tv2 = pr.op("V", lambda v, sl=sl: v.reciprocal(
                rcp_t[:, sl], var_t[:, sl]), inc=sC)
            tq = pr.op("S", lambda s, sl=sl: s.activation(
                rstd_t[:, sl], rcp_t[:, sl], AF.Sqrt), [tv2], inc=sS)
            pr.op("V", lambda v, sl=sl: v.tensor_mul(
                nm_t[:, sl], mval_t[:, sl], rstd_t[:, sl]), [tq], inc=sC)
            tgp = pr.op("V", lambda v, sl=sl: v.tensor_scalar(
                nm_t[:, sl], nm_t[:, sl], -1.0, None, ALU.mult), inc=sC)
            for jj in range(j - GRP + 1, j + 1):
                xb = xlnb[jj % 2]
                w2 = [tgp, t_x[jj]]
                if jj - 2 >= blk0 and toks[jj - 2] is not None:
                    w2.append(toks[jj - 2])
                tx = pr.op("V", lambda v, jj=jj, xb=xb: v.tensor_scalar(
                    xb[:, :], xring[jj % GRP][:, :], rstd_t[:, jj:jj + 1],
                    nm_t[:, jj:jj + 1], ALU.mult, ALU.add), w2, inc=sC)
                toks[jj] = sink(jj, xb, tx)
        return toks, t_s

    def ffn_phase(src, dst, tag, deps, stop=99, src_tok=None):
        sC, sS, sT, sG, cc = f"v{tag}", f"s{tag}", f"t{tag}", f"g{tag}", "cc"

        def dummy_out():
            tz = pr.op("V", lambda v: v.memset(mval_t[:, :], 0.0), pr.fence(),
                       inc=sC)
            touts = []
            for j in range(NBLK):
                touts.append(pr.dma("Y", dst[j * P:(j + 1) * P, 0:NBLK],
                                    mval_t[:, :], waits=[tz]))
            fes.close()
            return touts, None
        fes = ExitStack()
        fe = fes.enter_context
        w1sb = fe(nc.sbuf_tensor(f"w1sb{tag}", [P, EXP * DFF], BF16))
        xring, xlnb = xringT, xlnbT
        maskA = fe(nc.sbuf_tensor(f"maskA{tag}", [EXP, NTOK], BF16))
        posA = fe(nc.sbuf_tensor(f"posA{tag}", [EXP, NTOK], I16))
        carry8 = fe(nc.sbuf_tensor(f"carry8{tag}", [EXP, 1], F32))
        pinv1 = fe(nc.sbuf_tensor(f"pinv1{tag}", [1, 1024], F32))
        cnt64 = fe(nc.sbuf_tensor(f"cnt64{tag}", [NCORE * EXP, 1], F32))
        ctmp = fe(nc.sbuf_tensor(f"ctmp{tag}", [NCORE * EXP, 1], F32))
        o8pf = fe(nc.sbuf_tensor(f"o8pf{tag}", [EXP, 2], F32))
        o8m1 = fe(nc.sbuf_tensor(f"o8m1{tag}", [EXP, 1], F32))
        kept_f = fe(nc.sbuf_tensor(f"keptf{tag}", [P, NBLK], F32))
        ptok = fe(nc.sbuf_tensor(f"ptok{tag}", [P, NBLK], F32))
        prefm1 = fe(nc.sbuf_tensor(f"prefm1{tag}", [P, 1], F32))
        psL = fe(nc.psum_tensor(f"psL{tag}", [P, 512], F32))
        psTR = psTRg
        psW1 = [fe(nc.psum_tensor(f"psW1{tag}{i}", [P, 512], F32)) for i in range(2)]
        psW2 = [fe(nc.psum_tensor(f"psW2{tag}{i}", [P, 512], F32)) for i in range(2)]

        libw = []
        if tag != "A":
            libw = [pr.op("G", lambda g: (g.load_library(libcfg.attn),
                                          g.nop())[1],
                          pr.fence(), inc=sG)]
        tW1 = pr.dma("G", w1sb[:, :].rearrange("p (k c) -> p k c", c=DFF),
                     bass.AP(W1d, 0, [[DFF, P], [P * DFF, EXP], [1, DFF]]),
                     waits=deps + libw, sem=f"sw1{tag}")
        pr.op("V", lambda v: v.memset(carry8[:, :], 0.0), deps, inc=sC)
        pr.op("V", lambda v: v.memset(cnt64[:, :], 0.0), inc=sC)

        def ln_sink(j, xb, tx):
            return pr.dma("Y", xln_in[j * P:(j + 1) * P, :], xb[:, :],
                          waits=[tx], sem=f"xw{tag}{j % 2}")
        bdeps = None if src_tok is None else [[src_tok[j]] for j in range(NBLK)]
        t_xlnw, _lns = emit_ln(src, deps, ln_sink, xring, xlnb, sC, sS,
                               blk_deps=bdeps)

        if stop <= 1:
            return dummy_out()
        if stop == 15:
            pass
        tAG1 = None
        if stop == 15:
            return dummy_out()

        res = ExitStack()
        re_ = res.enter_context
        xlf = re_(nc.sbuf_tensor(f"xlf{tag}", [P, EXP * 1024], BF16))
        l16 = re_(nc.sbuf_tensor(f"l16{tag}", [2 * EXP, 1024], F32))
        e16 = re_(nc.sbuf_tensor(f"e16{tag}", [2 * EXP, 1024], F32))
        ar16 = re_(nc.sbuf_tensor(f"ar16{tag}", [2 * EXP, 1024], F32))
        cbuf = re_(nc.sbuf_tensor(f"cbuf{tag}", [NCORE * EXP, 1024], F32))
        pr.op("V", lambda v: v.memset(l16[:, :], PAD_NEG), deps, inc=sC)
        pr.op("V", lambda v: v.memset(e16[:, :], 0.0), inc=sC)
        tglib = t_lib0 if tag == "A" else libw[0]
        if stop == 25:
            res.close()
            return dummy_out()
        t_spm = []
        t_pinv = []
        tcp = None
        tex_prev = None
        tpv_prev = None
        tspm_prev = None
        xblk = [re_(nc.sbuf_tensor(f"xblk{tag}{i}", [P, DM], BF16))
                for i in range(2)]
        t_xbe = [None, None]
        tAG1s = []
        for sp in range(NSPAN):
            cols = slice(sp * 1024, (sp + 1) * 1024)
            tAG1s.append(pr.op("G", lambda g, sp=sp: g.collective_compute(
                "AllGather", ALU.bypass, replica_groups=RG,
                ins=[xln_in[sp * 1024:(sp + 1) * 1024, :].opt()],
                outs=[xln_all[sp * NCORE * 1024:
                              (sp + 1) * NCORE * 1024, :].opt()]),
                waits=t_xlnw[sp * 8:(sp + 1) * 8], inc=f"cA{tag}"))
            ttp = []
            for b in range(8):
                i = b % 2
                w = [t_xlnw[sp * 8 + b]] + list(deps)
                if t_xbe[i] is not None:
                    w.append(t_xbe[i])
                if sp > 0 and b < 2:
                    w.append(tcp)
                txb = pr.dma("Y", xblk[i][:, :],
                             xln_in[(sp * 8 + b) * P:(sp * 8 + b + 1) * P, :],
                             waits=w, sem=f"sxb{tag}{i}")
                if stop in (21, 26):
                    t_xbe[i] = txb
                    ttp.append(txb)
                    continue
                ttr = None
                for c in range(EXP):
                    wtr = [txb]
                    if c == 0 and ttp:
                        wtr.append(ttp[-1])
                    ttr = pr.op("T", lambda t, i=i, c=c: t.transpose(
                        psTR[:, c * P:(c + 1) * P],
                        xblk[i][:, c * P:(c + 1) * P], identb[:, :]),
                        wtr, inc=sT)
                if stop == 22:
                    tev = pr.op("V", lambda v: v.tensor_copy(
                        lntrash[:, 0:P], psTR[:, 0:P]), [ttr], inc=sC)
                else:
                    tev = pr.op("V", lambda v, b=b: v.tensor_copy(
                        xlf[:, :].rearrange("p (c f) -> p c f", f=1024)
                        [:, :, b * P:(b + 1) * P],
                        psTR[:, :].rearrange("p (c f) -> p c f", f=P)),
                        [ttr], inc=sC)
                t_xbe[i] = tev
                ttp.append(tev)
            if stop in (16, 21, 22, 26):
                tcp = ttp[-1]
                continue
            for half in range(2):
                tm = None
                for k in range(EXP):
                    tm = pr.op("T", lambda t, k=k, half=half: t.matmul(
                        psL[0:EXP, 0:512],
                        wrsb[:, k * EXP:(k + 1) * EXP],
                        xlf[:, k * 1024 + half * 512:k * 1024 + (half + 1) * 512],
                        start=(k == 0), stop=(k == EXP - 1)),
                        [t_wr, ttp[-1]] + ([tcp] if tcp else []), inc=sT)
                tcp = pr.op("V", lambda v, half=half: v.tensor_copy(
                    l16[0:EXP, half * 512:(half + 1) * 512], psL[0:EXP, 0:512]),
                    [tm] + ([tex_prev] if (half == 0 and tex_prev) else []),
                    inc=sC)
            if stop == 17:
                continue
            tpe = pr.op("V", lambda v: v.tensor_scalar(
                l16[0:EXP, :], l16[0:EXP, :], p8[:, 0:1], None, ALU.subtract),
                [tcp], inc=sC)
            tmx = pr.op("G", lambda g: g.partition_all_reduce(
                ar16[:, :], l16[:, :], 2 * EXP, bass_rust.ReduceOp.max),
                [tpe, tglib] + ([tpv_prev] if tpv_prev else []), inc=sG)
            tsh = pr.op("V", lambda v: v.tensor_sub(
                l16[0:EXP, :], l16[0:EXP, :], ar16[0:EXP, :]), [tmx], inc=sC)
            tmk = pr.op("V", lambda v, cols=cols: v.tensor_single_scalar(
                maskA[:, cols], l16[0:EXP, :], 0.0, ALU.is_equal), [tsh], inc=sC)
            if stop == 18:
                continue
            tex = pr.op("S", lambda s: s.activation(
                e16[0:EXP, :], l16[0:EXP, :], AF.Exp),
                [tsh] + ([tspm_prev] if tspm_prev else []), inc=sS)
            tex_prev = tex
            tse = pr.op("G", lambda g: g.partition_all_reduce(
                ar16[:, :], e16[:, :], 2 * EXP, bass_rust.ReduceOp.add),
                [tex], inc=sG)
            tpv = pr.op("V", lambda v: v.reciprocal(pinv1[:, :], ar16[0:1, :]),
                        [tse] + (t_pinv[-1:] if t_pinv else []), inc=sC)
            tpv_prev = tpv
            t_pinv.append(pr.dma("Y", pinvrow[0:1, sp * 1024:(sp + 1) * 1024],
                                 pinv1[:, :], waits=[tpv], sem=f"spv{tag}"))
            if stop == 19:
                continue
            tsc = pr.op("V", lambda v, cols=cols: v.tensor_tensor_scan(
                posA[:, cols], maskA[:, cols], maskA[:, cols], carry8[:, 0:1],
                ALU.add, ALU.max), [tmk], inc=sC)
            pr.op("V", lambda v, sp=sp: v.tensor_copy(
                carry8[:, :], posA[:, (sp + 1) * 1024 - 1:(sp + 1) * 1024]),
                [tsc], inc=sC)
            tsp = pr.op("V", lambda v, cols=cols: v.tensor_mul(
                e16[0:EXP, :], posA[:, cols], maskA[:, cols]),
                [tsc, tse, tpv], inc=sC)
            tspm_prev = pr.dma("Y", spm_in[:, sp * 1024:(sp + 1) * 1024],
                               e16[0:EXP, :], waits=[tsp], sem=f"ssp{tag}")
            t_spm.append(tspm_prev)

        if stop <= 2 or stop in (16, 17, 18, 19, 21, 22, 26):
            res.close()
            return dummy_out()
        tAG2 = pr.op("G", lambda g: g.collective_compute(
            "AllGather", ALU.bypass, replica_groups=RG,
            ins=[spm_in.ap().opt()], outs=[spm_all.ap().opt()]),
            waits=t_spm, inc=f"cB{tag}")

        tcn = None
        for q in range(4):
            tq = pr.dma("Y", cbuf[:, :], spm_all[:, q * 1024:(q + 1) * 1024],
                        waits=[tAG2] + ([tcn] if tcn else []), sem=f"scb{tag}")
            tr = pr.op("V", lambda v: v.reduce_max(ctmp[:, :], cbuf[:, :], AX),
                       [tq], inc=sC)
            tcn = pr.op("V", lambda v: v.tensor_max(
                cnt64[:, :], cnt64[:, :], ctmp[:, :]), [tr], inc=sC)
        tof = pr.op("T", lambda t: t.matmul(
            psL[0:EXP, 0:1], selm[:, :], cnt64[:, :]), [tcn, tcp], inc=sT)
        tpf = pr.op("T", lambda t: t.matmul(
            psL[0:EXP, 1:2], lselm[:, :], cnt64[:, :]), [tcn], inc=sT)
        tcpo = pr.op("V", lambda v: v.tensor_copy(
            o8pf[:, :], psL[0:EXP, 0:2]), [tof, tpf], inc=sC)
        to8 = pr.op("V", lambda v: v.tensor_scalar(
            o8m1[:, :], o8pf[:, 0:1], -1.0, None, ALU.add), [tcpo], inc=sC)
        tpc = pr.op("T", lambda t: t.matmul(
            psL[:, 2:3], selc[:, :], o8pf[:, 1:2]), [tcpo], inc=sT)
        tpm = pr.op("V", lambda v: v.tensor_scalar(
            prefm1[:, :], psL[:, 2:3], -1.0, None, ALU.add), [tpc], inc=sC)

        if stop <= 3:
            res.close()
            return dummy_out()
        res.close()
        fx = pr.fence()

        ex = ExitStack()
        xe = ex.enter_context
        gpA = xe(nc.sbuf_tensor(f"gpA{tag}", [EXP, NTOK], F32))
        keptmA = xe(nc.sbuf_tensor(f"keptmA{tag}", [EXP, NTOK], BF16))
        wtA = xe(nc.sbuf_tensor(f"wtA{tag}", [EXP, NTOK], F32))
        myexp = xe(nc.sbuf_tensor(f"myexp{tag}", [P, 256], F32))
        spf = xe(nc.sbuf_tensor(f"spf{tag}", [P, 256], F32))
        c1t = xe(nc.sbuf_tensor(f"c1t{tag}", [P, 256], F32))
        c2t = xe(nc.sbuf_tensor(f"c2t{tag}", [P, 256], F32))
        cm8 = xe(nc.sbuf_tensor(f"cm8{tag}", [P, 256], U8))
        negt = xe(nc.sbuf_tensor(f"negt{tag}", [P, 256], F32))
        sel_o = xe(nc.sbuf_tensor(f"selo{tag}", [P, 256], F32))
        sel2_o = xe(nc.sbuf_tensor(f"sel2o{tag}", [P, 256], F32))
        sp16 = xe(nc.sbuf_tensor(f"sp16{tag}", [P, 256], I16))
        tbl1 = xe(nc.sbuf_tensor(f"tbl1{tag}", [P, 5376], I16))
        tb1f = xe(nc.sbuf_tensor(f"tb1f{tag}", [P, 5376], F32))
        evrow = xe(nc.sbuf_tensor(f"evrow{tag}", [1, 5376], I16))
        ones1f = xe(nc.sbuf_tensor(f"ones1f{tag}", [P, 1], F32))
        tbl16 = xe(nc.sbuf_tensor(f"tbl16{tag}", [P, CAP // P], I16))
        spr = xe(nc.sbuf_tensor(f"spr{tag}", [P, 256], F32))

        tgp = pr.op("V", lambda v: v.tensor_scalar(
            gpA[:, :], posA[:, :], o8m1[:, 0:1], None, ALU.add),
            [to8] + fx, inc=sC)
        tk1 = pr.op("V", lambda v: v.tensor_single_scalar(
            keptmA[:, :], gpA[:, :], float(CAP) - 0.5, ALU.is_lt), [tgp], inc=sC)
        tk2 = pr.op("V", lambda v: v.tensor_mul(
            keptmA[:, :], keptmA[:, :], maskA[:, :]), [tk1], inc=sC)
        tw1_ = pr.op("V", lambda v: v.tensor_scalar(
            wtA[:, :], gpA[:, :], e8[:, 0:1], None, ALU.add), [tk1], inc=sC)
        tw1b = tw1_
        for qi in range(1, 4):
            tqm = pr.op("V", lambda v, qi=qi: v.tensor_single_scalar(
                posA[:, :], gpA[:, :], qi * float(CAP // 4) - 0.5, ALU.is_gt),
                [tw1b, tk1], inc=sC)
            tw1b = pr.op("V", lambda v: v.scalar_tensor_tensor(
                wtA[:, :], posA[:, :], float((NCORE - 1) * (CAP // 4)),
                wtA[:, :], ALU.mult, ALU.add), [tqm], inc=sC)
        tw2_ = pr.op("V", lambda v: v.tensor_mul(
            wtA[:, :], wtA[:, :], keptmA[:, :]), [tw1b, tk2], inc=sC)

        tkx = tfx = None
        for j in range(NBLK):
            tkx = pr.op("T", lambda t, j=j: t.matmul(
                psL[:, 32 + j:33 + j], keptmA[:, j * P:(j + 1) * P], on8b[:, :]),
                [tk2, tpm], inc=sT)
            tfx = pr.op("T", lambda t, j=j: t.matmul(
                psL[:, 64 + j:65 + j], wtA[:, j * P:(j + 1) * P], on8f[:, :]),
                [tw2_], inc=sT)
        tke = pr.op("V", lambda v: v.tensor_copy(
            kept_f[:, :], psL[:, 32:32 + NBLK]), [tkx], inc=sC)
        tfe = pr.op("V", lambda v: v.tensor_copy(
            flat32[:, :], psL[:, 64:64 + NBLK]), [tfx], inc=sC)
        tptk = pr.dma("Y", ptok[:, :],
                      bass.AP(pinvrow, 0, [[1, P], [P, NBLK]]), waits=t_pinv,
                      allow_slow_non_contiguous=True, sem=f"spt{tag}")
        taf = pr.op("V", lambda v: v.scalar_tensor_tensor(
            af_t[:, :], ptok[:, :], 0.5, kept_f[:, :], ALU.mult, ALU.mult),
            [tke, tptk], inc=sC)
        tbf = pr.op("V", lambda v: v.tensor_scalar(
            bf_t[:, :], kept_f[:, :], -0.5, 0.5, ALU.mult, ALU.add),
            [tke], inc=sC)

        if stop <= 4:
            ex.close()
            return dummy_out()
        trl = pr.op("G", lambda g: (g.reg_load(greg, co32[0:1, 0:1]),
                                    g.nop())[1],
                    [co_tok, tAG2], inc=sG)
        tme = pr.dma("G", myexp[:, :],
                     bass.AP(spm_all, greg,
                             [[EXP * NTOK, EXP], [256, 16], [1, 256]]),
                     waits=[trl] + fx, sem=f"sme{tag}")
        if stop == 51:
            ex.close()
            return dummy_out()
        tng = pr.op("V", lambda v: v.memset(negt[:, :], -1.0), fx, inc=sC)
        tsf = pr.op("V", lambda v: v.tensor_scalar(
            spf[:, :], myexp[:, :], prefm1[:, 0:1], None, ALU.add),
            [tme, tpm], inc=sC)
        tc1 = pr.op("V", lambda v: v.tensor_single_scalar(
            c1t[:, :], myexp[:, :], 0.5, ALU.is_gt), [tme], inc=sC)
        tc2 = pr.op("V", lambda v: v.tensor_single_scalar(
            c2t[:, :], spf[:, :], float(CAP) - 0.5, ALU.is_lt), [tsf], inc=sC)
        tcm = pr.op("V", lambda v: v.tensor_mul(
            cm8[:, :], c1t[:, :], c2t[:, :]), [tc1, tc2], inc=sC)
        tsl0 = pr.op("V", lambda v: v.tensor_copy(
            sel_o[:, :], negt[:, :]), [tng], inc=sC)
        tsl = pr.op("V", lambda v: v.copy_predicated(
            sel_o[:, :], cm8[:, :], spf[:, :]), [tcm, tsl0], inc=sC)
        tll = pr.op("G", lambda g: (g.load_library(libcfg.local_scatter),
                                    g.nop())[1],
                    pr.fence(), inc=sG)
        tsc_ = None
        for rpass in range(3):
            base = rpass * 1792
            td1 = pr.op("V", lambda v, base=base: v.tensor_scalar(
                spr[:, :], sel_o[:, :], -float(base), None, ALU.add),
                [tsl] + ([tsc_] if tsc_ else []), inc=sC)
            tc1b = pr.op("V", lambda v: v.tensor_single_scalar(
                c1t[:, :], spr[:, :], -0.5, ALU.is_gt), [td1], inc=sC)
            tc2b = pr.op("V", lambda v: v.tensor_single_scalar(
                c2t[:, :], spr[:, :], 1791.5, ALU.is_lt), [td1], inc=sC)
            tcmb = pr.op("V", lambda v: v.tensor_mul(
                cm8[:, :], c1t[:, :], c2t[:, :]), [tc1b, tc2b], inc=sC)
            tslb0 = pr.op("V", lambda v: v.tensor_copy(
                sel2_o[:, :], negt[:, :]), [tng], inc=sC)
            tslb = pr.op("V", lambda v: v.copy_predicated(
                sel2_o[:, :], cm8[:, :], spr[:, :]), [tcmb, tslb0], inc=sC)
            t16b = pr.op("V", lambda v: v.tensor_copy(
                sp16[:, :], sel2_o[:, :]), [tslb], inc=sC)
            tsc_ = pr.op("G", lambda g, base=base: g.local_scatter(
                tbl1[:, base:base + 1792], tok16[:, :], sp16[:, :], P, 1792,
                256), [t16b, tll, tok16_tok] + ([tsc_] if tsc_ else []),
                inc=sG)
        if stop == 52:
            ex.close()
            return dummy_out()
        tcst = pr.op("V", lambda v: v.tensor_copy(tb1f[:, :], tbl1[:, :]),
                     [tsc_], inc=sC)
        tone = pr.op("V", lambda v: v.memset(ones1f[:, :], 1.0), fx, inc=sC)
        tev_ = None
        tevs = []
        for ci in range(11):
            c0 = ci * 512
            cw = 512 if ci < 10 else 256
            wv = [tcst, tone]
            if ci == 0:
                wv += [tke, tfe]
            if tev_ is not None:
                wv.append(tev_)
            tmmc = pr.op("T", lambda t, c0=c0, cw=cw: t.matmul(
                psL[0:1, 0:cw], ones1f[:, :], tb1f[:, c0:c0 + cw],
                start=True, stop=True), wv, inc=sT)
            tev_ = pr.op("V", lambda v, c0=c0, cw=cw:
                         v.tensor_copy(evrow[0:1, c0:c0 + cw],
                                       psL[0:1, 0:cw]), [tmmc], inc=sC)
            tevs.append(tev_)
        if stop == 53:
            ex.close()
            return dummy_out()
        ttr1 = pr.dma("Y", tblrow[0:1, :], evrow[0:1, :], waits=tevs,
                      sem=f"st1{tag}")
        ttr2 = pr.dma("Y", tbl16[:, :],
                      bass.AP(tblrow, 0, [[1, P], [P, CAP // P]]), waits=[ttr1],
                      allow_slow_non_contiguous=True, sem=f"st2{tag}")
        tt32 = pr.op("V", lambda v: v.tensor_copy(
            tblT32[:, :], tbl16[:, :]), [ttr2], inc=sC)

        if stop <= 5:
            ex.close()
            return dummy_out()
        ex.close()
        fx2 = pr.fence()

        me = ExitStack()
        mm = me.enter_context
        w2sb = mm(nc.sbuf_tensor(f"w2sb{tag}", [P, NBLK * DM], BF16))
        dispTM = [mm(nc.sbuf_tensor(f"dtm{tag}{i}", [P, DM], BF16)) for i in range(2)]
        dispF = mm(nc.sbuf_tensor(f"dispF{tag}", [P, EXP * SSPAN], BF16))
        hT = mm(nc.sbuf_tensor(f"hT{tag}", [P, NBLK * SSPAN], BF16))
        zt = [mm(nc.sbuf_tensor(f"zt{tag}{i}", [P, DM], BF16)) for i in range(2)]

        tW2 = pr.dma("G", w2sb[:, :].rearrange("p (m c) -> p m c", c=DM),
                     bass.AP(W2d, 0, [[DM, P], [P * DM, NBLK], [1, DM]]),
                     waits=fx2, sem=f"sw2{tag}")

        t_z = []
        t_gprev = [None, None]
        t_dfprev = None
        t_zdma = [None, None]
        t_relu = [None] * NBLK
        tAGq = []
        for w in range(NSSPAN):
            tdf = []
            for b2 in range(2):
                blk = 2 * w + b2
                wg = tAG1s + [tt32] + fx2
                if t_gprev[b2] is not None:
                    wg.append(t_gprev[b2])
                tg = pr.op("G", lambda g, b2=b2, blk=blk: g.indirect_dma_start(
                    dispTM[b2][:, :], None, xln_all[:, :],
                    IndirectOffsetOnAxis(tblT32[:, blk:blk + 1], 0)),
                    wg, inc=f"sdg{tag}{b2}", amt=16)
                ttr = None
                for c in range(EXP):
                    wtr = [tg]
                    if c == 0 and t_dfprev is not None:
                        wtr.append(t_dfprev)
                    if c == 0 and tdf:
                        wtr.append(tdf[-1])
                    ttr = pr.op("T", lambda t, b2=b2, c=c: t.transpose(
                        psTR[:, c * P:(c + 1) * P],
                        dispTM[b2][:, c * P:(c + 1) * P],
                        identb[:, :]), wtr, inc=sT)
                tdf.append(pr.op("V", lambda v, b2=b2: v.tensor_copy(
                    dispF[:, :].rearrange("p (c f) -> p c f", f=SSPAN)
                    [:, :, b2 * P:(b2 + 1) * P],
                    psTR[:, :].rearrange("p (c f) -> p c f", f=P)),
                    [ttr], inc=sC))
                t_gprev[b2] = tdf[-1]
            for m in range(NBLK):
                pw = psW1[m % 2]
                wv = tdf + [tW1]
                if m >= 2:
                    wv.append(t_relu[m - 2])
                elif t_relu[m - 2 + NBLK] is not None:
                    wv.append(t_relu[m - 2 + NBLK])
                tm = None
                for k in range(EXP):
                    tm = pr.op("T", lambda t, k=k, m=m, pw=pw: t.matmul(
                        pw[:, 0:SSPAN],
                        w1sb[:, k * DFF + m * P:k * DFF + (m + 1) * P],
                        dispF[:, k * SSPAN:(k + 1) * SSPAN],
                        start=(k == 0), stop=(k == EXP - 1)),
                        wv if k == 0 else [], inc=sT)
                if m % 2 == 0:
                    t_relu[m] = pr.op("V", lambda v, m=m, pw=pw: v.tensor_scalar(
                        hT[:, m * SSPAN:(m + 1) * SSPAN], pw[:, 0:SSPAN], 0.0,
                        None, ALU.max), [tm], inc=sC)
                else:
                    t_relu[m] = pr.op("S", lambda s, m=m, pw=pw: s.activation(
                        hT[:, m * SSPAN:(m + 1) * SSPAN], pw[:, 0:SSPAN],
                        AF.Relu), [tm], inc=sS)
            t_dfprev = tm
            for b2 in range(2):
                pz = psW2[b2]
                tz_half = None
                for half in range(2):
                    tmm = None
                    for m in range(NBLK):
                        wv = [t_relu[m], tW2]
                        if half == 1 and tz_half is not None:
                            wv.append(tz_half)
                        if half == 0 and t_zdma[b2] is not None:
                            wv.append(t_zdma[b2])
                        tmm = pr.op("T", lambda t, m=m, b2=b2, pz=pz, half=half:
                                    t.matmul(
                            pz[:, :],
                            hT[:, m * SSPAN + b2 * P:m * SSPAN + (b2 + 1) * P],
                            w2sb[:, m * DM + half * 512:m * DM + (half + 1) * 512],
                            start=(m == 0), stop=(m == NBLK - 1)),
                            wv, inc=sT)
                    if half == 0:
                        tz_half = pr.op("S", lambda s, b2=b2, half=half, pz=pz:
                                        s.copy(
                            zt[b2][:, half * 512:(half + 1) * 512], pz[:, :]),
                            [tmm], inc=sS)
                    else:
                        tz_half = pr.op("V", lambda v, b2=b2, half=half, pz=pz:
                                        v.tensor_copy(
                            zt[b2][:, half * 512:(half + 1) * 512], pz[:, :]),
                            [tmm], inc=sC)
                t_zdma[b2] = pr.dma(
                    "Y", eout_in[(2 * w + b2) * P:(2 * w + b2 + 1) * P, :],
                    zt[b2][:, :], waits=[tz_half], sem=f"szw{tag}{b2}")
                t_z.append(t_zdma[b2])
            if (w + 1) % (NSSPAN // 4) == 0 and w + 1 < NSSPAN:
                qq = (w + 1) // (NSSPAN // 4) - 1
                tAGq.append(pr.op("G", lambda g, qq=qq: g.collective_compute(
                    "AllGather", ALU.bypass, replica_groups=RG,
                    ins=[eout_in[qq * QCAP:(qq + 1) * QCAP, :].opt()],
                    outs=[eout_all[qq * NCORE * QCAP:
                                   (qq + 1) * NCORE * QCAP, :].opt()]),
                    waits=t_z[qq * (NSSPAN // 2):] +
                          ([tAG2] if qq == 0 else []), inc=f"cC{tag}"))

        tAG3 = pr.op("G", lambda g: g.collective_compute(
            "AllGather", ALU.bypass, replica_groups=RG,
            ins=[eout_in[3 * QCAP:CAP, :].opt()],
            outs=[eout_all[3 * NCORE * QCAP:4 * NCORE * QCAP, :].opt()]),
            waits=t_z[3 * (NSSPAN // 2):], inc=f"cC{tag}")

        if stop <= 6:
            me.close()
            return dummy_out()
        me.close()
        fx3 = pr.fence()

        ce = ExitStack()
        cm_ = ce.enter_context
        NCR = 6
        eoutg = [cm_(nc.sbuf_tensor(f"eog{tag}{i}", [P, DM], BF16)) for i in range(NCR)]
        xlt = [cm_(nc.sbuf_tensor(f"xlt{tag}{i}", [P, DM], BF16)) for i in range(NCR)]
        rsb = [cm_(nc.sbuf_tensor(f"rsb{tag}{i}", [P, DM], F32)) for i in range(NCR)]
        ctm = [cm_(nc.sbuf_tensor(f"ctm{tag}{i}", [P, DM], F32)) for i in range(NCR)]
        touts = []
        tgs, txs, trss = [], [], []
        tprevc = [None] * NCR
        for j in range(NBLK):
            i = j % NCR
            si = j % 4
            wg = tAGq + [tAG3, tfe, taf, tbf] + fx3
            if tprevc[i] is not None:
                wg.append(tprevc[i])
                wg.append(touts[j - NCR])
            tg = pr.op("G", lambda g, j=j, i=i: g.indirect_dma_start(
                eoutg[i][:, :], None, eout_all[:, :],
                IndirectOffsetOnAxis(flat32[:, j:j + 1], 0)),
                wg + ([tgs[j - 4]] if j >= 4 else []),
                inc=f"seg{tag}{si}", amt=16)
            tx = pr.dma("Y", xlt[i][:, :], xln_in[j * P:(j + 1) * P, :],
                        waits=wg + ([txs[j - 4]] if j >= 4 else []),
                        sem=f"sxl{tag}{si}")
            trs = pr.dma("Y", rsb[i][:, :], src[j * P:(j + 1) * P, :],
                         waits=wg + ([trss[j - 4]] if j >= 4 else []),
                         sem=f"srs{tag}{si}")
            t1 = pr.op("V", lambda v, j=j, i=i: v.scalar_tensor_tensor(
                ctm[i][:, :], eoutg[i][:, :], af_t[:, j:j + 1], rsb[i][:, :],
                ALU.mult, ALU.add), [tg, trs], inc=sC)
            t2 = pr.op("V", lambda v, j=j, i=i: v.scalar_tensor_tensor(
                ctm[i][:, :], xlt[i][:, :], bf_t[:, j:j + 1], ctm[i][:, :],
                ALU.mult, ALU.add), [t1, tx], inc=sC)
            tprevc[i] = t2
            touts.append(pr.dma("Y", dst[j * P:(j + 1) * P, :], ctm[i][:, :],
                                waits=[t2] + ([touts[j - 4]] if j >= 4 else []),
                                sem=f"sow{tag}{si}"))
            tgs.append(tg)
            txs.append(tx)
            trss.append(trs)
        ce.close()
        fes.close()
        return touts, t_gprev[1] or t_gprev[0]


    def attn_phase(src, dst, deps, src_tok=None, ptr_tok=None):
        sC, sS, sT, sG = "vX", "sX", "tX", "gX"
        ASP = 512
        NASP = NTOK // ASP
        CPA = ASP // P
        aes = ExitStack()
        ae = aes.enter_context
        wqs = ae(nc.sbuf_tensor("wqs", [P, EXP * 512], BF16))
        wks = ae(nc.sbuf_tensor("wks", [P, EXP * 512], BF16))
        wvs = ae(nc.sbuf_tensor("wvs", [P, EXP * 1024], BF16))
        wgs = ae(nc.sbuf_tensor("wgs", [P, EXP * 1024], BF16))
        wos = ae(nc.sbuf_tensor("wos", [P, EXP * 1024], BF16))
        wbs = ae(nc.sbuf_tensor("wbs", [P, EXP * H], BF16))
        xr2, xl2 = xringT, xlnbT
        xaf = xafT
        qf = ae(nc.sbuf_tensor("qf", [P, H * ASP], BF16))
        kfr = ae(nc.sbuf_tensor("kfr", [P, H * ASP], BF16))
        ktok = ae(nc.sbuf_tensor("ktok", [P, CPA * 512], BF16))
        vbt = ae(nc.sbuf_tensor("vbt", [P, CPA * 1024], BF16))
        gf = ae(nc.sbuf_tensor("gf", [P, 2 * H * ASP], BF16))
        betaT = ae(nc.sbuf_tensor("betaT", [P, CPA * H], F32))
        negbT = ae(nc.sbuf_tensor("negbT", [P, CPA * H], F32))
        kss = ae(nc.sbuf_tensor("kss", [P, CPA * H], F32))
        krs = ae(nc.sbuf_tensor("krs", [P, CPA * H], F32))
        kfn2 = [ae(nc.sbuf_tensor(f"kfn{i}", [P, 512], BF16)) for i in range(2)]
        sNB2 = [ae(nc.sbuf_tensor(f"sNB{i}", [P, 512], BF16)) for i in range(2)]
        sNBT2 = [ae(nc.sbuf_tensor(f"sNBT{i}", [P, 512], BF16)) for i in range(2)]
        powA2 = [[ae(nc.sbuf_tensor(f"powA{i}{jj}", [P, 512], BF16))
                  for jj in range(2)] for i in range(2)]
        powT2 = [[ae(nc.sbuf_tensor(f"powT{i}{jj}", [P, 512], BF16))
                  for jj in range(2)] for i in range(2)]
        sX2 = [ae(nc.sbuf_tensor(f"sX{i}", [P, 512], BF16)) for i in range(2)]
        nbk2 = [ae(nc.sbuf_tensor(f"nbk{i}", [P, 512], BF16)) for i in range(2)]
        sW1c2 = [ae(nc.sbuf_tensor(f"sW1c{i}", [P, 1024], BF16)) for i in range(2)]
        sW2cT2 = [ae(nc.sbuf_tensor(f"sW2cT{i}", [P, 512], BF16)) for i in range(2)]
        smk2 = [ae(nc.sbuf_tensor(f"smk{i}", [P, 512], BF16)) for i in range(2)]
        sD = ae(nc.sbuf_tensor("sD", [P, 1024], BF16))
        sS0 = ae(nc.sbuf_tensor("sS0", [P, 1024], BF16))
        sSf = ae(nc.sbuf_tensor("sSf", [P, 1024], F32))
        oTs = [ae(nc.sbuf_tensor(f"oTs{i}", [P, CPA * 512], BF16)) for i in range(2)]
        sqb = ae(nc.sbuf_tensor("sqb", [P, CPA * 512], BF16))
        rr1 = ae(nc.sbuf_tensor("rr1", [1, CPA * 512], F32))
        epsT = ae(nc.sbuf_tensor("epsT", [1, 1], F32))
        scT = ae(nc.sbuf_tensor("scT", [1, 1], F32))
        rrb = ae(nc.sbuf_tensor("rrb", [1, CPA * 512], BF16))
        rbc = ae(nc.sbuf_tensor("rbc", [P, CPA * 512], BF16))
        sgt = [ae(nc.sbuf_tensor(f"sgt{i}", [P, 512], BF16)) for i in range(2)]
        rs2 = [ae(nc.sbuf_tensor(f"rs2{i}", [P, DM], F32)) for i in range(2)]
        yout = [ae(nc.sbuf_tensor(f"yout{i}", [P, DM], F32)) for i in range(2)]
        psS = ae(nc.psum_tensor("psS", [P, 512], F32))
        psX = ae(nc.psum_tensor("psX", [P, 512], F32))
        psDW = ae(nc.psum_tensor("psDW", [P, 512], F32))
        psQT = ae(nc.psum_tensor("psQT", [P, 512], F32))
        psOp = ae(nc.psum_tensor("psOp", [P, 1024], F32))
        psEp = ae(nc.psum_tensor("psEp", [P, 512], F32))
        psT2 = psTRg

        tw = {}
        for nm, sb, dr, cw in [("q", wqs, Wqd, 512), ("k", wks, Wkd, 512),
                               ("v", wvs, Wvd, 1024), ("g", wgs, Wgd, 1024),
                               ("o", wos, Wod, 1024), ("b", wbs, Wbd, H)]:
            tw[nm] = pr.dma(
                "G", sb[:, :].rearrange("p (k c) -> p k c", c=cw),
                bass.AP(dr, 0, [[cw, P], [P * cw, EXP], [1, cw]]),
                waits=deps, sem=f"aw{nm}")

        tS0m = pr.op("V", lambda v: v.memset(sS0[:, :], 0.0), deps, inc=sC)
        teps = pr.op("V", lambda v: v.memset(epsT[:, :], EPS_ON), deps, inc=sC)
        tsc_c = pr.op("V", lambda v: v.memset(scT[:, :], 1.0 / DV), deps, inc=sC)
        tSadd = pr.op("V", lambda v: v.memset(sSf[:, :], 0.0), deps, inc=sC)
        touts = []
        tSupd = [None] * H
        tS0c = tS0m
        tDc = None
        tXc = None
        tOev = None
        tWo = None
        tYev = [None, None]
        tYw = [None, None]
        tRs = [None, None]
        tNBT_prev = None
        tmkm_last = None
        tneglast = None
        tgsil = None
        tvev = None
        tr1_p = tr1b_p = None
        tbc_p = None
        tEfree = tQfree = tXfree = tDWfree = None
        tknG = [None, None]
        tkng = [None] * CPA
        lnring = [None, None]
        prev_evac = [None]

        for a in range(NASP):
            def asink(j, xb, tx, a=a):
                bb = j - 4 * a
                tev = None
                for wv in range(2):
                    trs = None
                    for cc4 in range(4):
                        cch = wv * 4 + cc4
                        wtr = [tx]
                        if cc4 == 0:
                            wtr += [prev_evac[0], ptr_tok]
                        trs = pr.op("T", lambda t, cch=cch, cc4=cc4, xb=xb:
                                    t.transpose(
                            psT2[:, 512 + cc4 * P:512 + (cc4 + 1) * P],
                            xb[:, cch * P:(cch + 1) * P],
                            identb[:, :]), wtr, inc=sT)
                    tev = pr.op("V", lambda v, bb=bb, wv=wv: v.tensor_copy(
                        xaf[:, :].rearrange("p (c f) -> p c f", f=ASP)
                        [:, 4 * wv:4 * wv + 4, bb * P:(bb + 1) * P],
                        psT2[:, 512:1024].rearrange("p (c f) -> p c f", f=P)),
                        [trs], inc=sC)
                    prev_evac[0] = tev
                return tev

            abdeps = None
            if src_tok is not None:
                abdeps = [None] * NBLK
                for jj in range(NBLK):
                    abdeps[jj] = [src_tok[jj]]
            t_ln, t_lns = emit_ln(src, deps, asink, xr2, xl2, sC, sS,
                                  blk0=4 * a, nblk=4, ring_tok=lnring,
                                  blk_deps=abdeps)
            lnring = [t_lns[4 * a + 2], t_lns[4 * a + 3]]
            txaf = t_ln[4 * a + 3]

            projps = [(psDW, 0), (psX, 0), (psOp, 0), (psOp, 512)]
            pv = [[t for t in (tDWfree,) if t is not None],
                  [t for t in (tXfree,) if t is not None],
                  [t for t in (tYev[0], tYev[1], tOev, tr1b_p) if t is not None],
                  [t for t in (tYev[0], tYev[1], tOev, tr1b_p) if t is not None]]
            nps = [0]

            def proj(outsb, outcol, wsb, wcol, scale, toks, slots=(0, 1)):
                slot = slots[nps[0] % len(slots)]
                ps, off = projps[slot]
                nps[0] += 1
                cwid = wsb.shape[1] // EXP
                tm = None
                for k in range(EXP):
                    w = list(toks) + [txaf] + (pv[slot] if k == 0 else [])
                    tm = pr.op("T", lambda t, k=k, ps=ps, off=off: t.matmul(
                        ps[:, off:off + 512],
                        wsb[:, k * cwid + wcol:k * cwid + wcol + P],
                        xaf[:, k * ASP:k * ASP + 512],
                        start=(k == 0), stop=(k == EXP - 1)),
                        w if k == 0 else [], inc=sT)
                if scale is None:
                    if nps[0] % 2 == 0:
                        te = pr.op("S", lambda s, ps=ps, off=off: s.copy(
                            outsb[:, outcol:outcol + 512], ps[:, off:off + 512]),
                            [tm], inc=sS)
                    else:
                        te = pr.op("V", lambda v, ps=ps, off=off: v.tensor_copy(
                            outsb[:, outcol:outcol + 512], ps[:, off:off + 512]),
                            [tm], inc=sC)
                else:
                    te = pr.op("S", lambda s, ps=ps, off=off: s.mul(
                        outsb[:, outcol:outcol + 512], ps[:, off:off + 512],
                        scale), [tm], inc=sS)
                pv[slot] = [te]
                return te, slot

            tqev = tkev = tgev = None
            for m in range(H):
                tqev, _sl = proj(qf, m * ASP, wqs, m * P, SCALE_Q, [tw["q"]])
            for m in range(H):
                tkev, _sl = proj(kfr, m * ASP, wks, m * P, None, [tw["k"]])
            for m in range(2 * H):
                tgev, _sl = proj(gf, m * ASP, wgs, m * P, None,
                                 [tw["g"], tgsil], slots=(2, 3))
            for bb in range(CPA):
                tmb = None
                for k in range(EXP):
                    w = [tw["b"], txaf]
                    if k == 0:
                        w += [tmkm_last, tneglast, tbc_p, tEfree]
                    tmb = pr.op("T", lambda t, k=k, bb=bb: t.matmul(
                        psEp[:, bb * H:(bb + 1) * H],
                        xaf[:, k * ASP + bb * P:k * ASP + (bb + 1) * P],
                        wbs[:, k * H:(k + 1) * H],
                        start=(k == 0), stop=(k == EXP - 1)),
                        w if k == 0 else [], inc=sT)
                tsig = pr.op("S", lambda s, bb=bb: s.activation(
                    betaT[:, bb * H:(bb + 1) * H], psEp[:, bb * H:(bb + 1) * H],
                    AF.Sigmoid), [tmb], inc=sS)
                tneglast = pr.op("V", lambda v, bb=bb: v.tensor_scalar(
                    negbT[:, bb * H:(bb + 1) * H],
                    betaT[:, bb * H:(bb + 1) * H], -1.0, None, ALU.mult),
                    [tsig], inc=sC)
                for half in range(2):
                    slot = nps[0] % 2
                    ps, off = projps[slot]
                    nps[0] += 1
                    tm = None
                    for k in range(EXP):
                        w = [tw["v"], txaf] + (pv[slot] if k == 0 else [])
                        tm = pr.op("T", lambda t, k=k, bb=bb, half=half, ps=ps,
                                   off=off: t.matmul(
                            ps[:, off:off + 512],
                            xaf[:, k * ASP + bb * P:k * ASP + (bb + 1) * P],
                            wvs[:, k * 1024 + half * 512:
                                k * 1024 + (half + 1) * 512],
                            start=(k == 0), stop=(k == EXP - 1)),
                            w if k == 0 else [], inc=sT)
                    for hh in range(2):
                        h = half * 2 + hh
                        tvev = pr.op("V", lambda v, bb=bb, h=h, hh=hh, ps=ps,
                                     off=off: v.tensor_scalar(
                            vbt[:, bb * 1024 + h * 256:bb * 1024 + (h + 1) * 256],
                            ps[:, off + hh * 256:off + (hh + 1) * 256],
                            betaT[:, bb * H + h:bb * H + h + 1], None,
                            ALU.mult), [tm, tsig], inc=sC)
                    pv[slot] = [tvev]

            tkng_new = [None] * CPA
            tDWfree = pv[0][0] if pv[0] else tDWfree
            tXfree = pv[1][0] if pv[1] else tXfree
            for cl in range(CPA):
                c = a * CPA + cl
                par = c % 2
                kfn = kfn2[par]
                sNB, sNBT, sX = sNB2[par], sNBT2[par], sX2[par]
                powA, powT = powA2[par], powT2[par]
                nbk, sW1c, sW2cT, smk = (nbk2[par], sW1c2[par], sW2cT2[par],
                                         smk2[par])
                tts = []
                for h in range(H):
                    w = [tkev]
                    if h == 0:
                        w += [tkng[cl], tNBT_prev, tknG[par]]
                    tts.append(pr.op("T", lambda t, h=h, cl=cl: t.transpose(
                        psT2[:, h * P:(h + 1) * P],
                        kfr[:, h * ASP + cl * P:h * ASP + (cl + 1) * P],
                        identb[:, :]), w if h == 0 else [], inc=sT))
                tka = None
                for h in range(H):
                    tka = pr.op("S", lambda s, h=h, cl=cl: s.activation(
                        lntrash[:, 0:P], psT2[:, h * P:(h + 1) * P], AF.Square,
                        accum_out=kss[:, cl * H + h:cl * H + h + 1]),
                        [tts[h]], inc=sS)
                sl = slice(cl * H, (cl + 1) * H)
                pr.op("V", lambda v, sl=sl: v.tensor_scalar(
                    krs[:, sl], kss[:, sl], 1.0, EPS_K, ALU.mult, ALU.add),
                    [tka], inc=sC)
                tk2 = pr.op("V", lambda v, sl=sl: v.reciprocal(
                    krs[:, sl], krs[:, sl]), inc=sC)
                tk3 = pr.op("S", lambda s, sl=sl: s.activation(
                    krs[:, sl], krs[:, sl], AF.Sqrt), [tk2], inc=sS)
                tktk = None
                for h in range(H):
                    tktk = pr.op("V", lambda v, h=h, cl=cl: v.tensor_scalar(
                        ktok[:, cl * 512 + h * P:cl * 512 + (h + 1) * P],
                        psT2[:, h * P:(h + 1) * P],
                        krs[:, cl * H + h:cl * H + h + 1], None, ALU.mult),
                        [tk3], inc=sC)
                tts2 = []
                for h in range(H):
                    tts2.append(pr.op("T", lambda t, h=h, cl=cl: t.transpose(
                        psT2[:, h * P:(h + 1) * P],
                        ktok[:, cl * 512 + h * P:cl * 512 + (h + 1) * P],
                        identb[:, :]), [tktk] if h == 0 else [], inc=sT))
                tkfn = pr.op("S", lambda s, kfn=kfn: s.copy(
                    kfn[:, :], psT2[:, 0:512]), [tts2[-1]], inc=sS)
                tknG[par] = tkfn
                tG = None
                for h in range(H):
                    w = [tkfn]
                    if h == 0:
                        w += [tneglast, tEfree]
                    tG = pr.op("T", lambda t, h=h, kfn=kfn: t.matmul(
                        psEp[:, h * P:(h + 1) * P], kfn[:, h * P:(h + 1) * P],
                        kfn[:, h * P:(h + 1) * P]), w if h == 0 else [], inc=sT)
                tNB = None
                for h in range(H):
                    tNB = pr.op("V", lambda v, h=h, cl=cl, sNB=sNB:
                                v.scalar_tensor_tensor(
                        sNB[:, h * P:(h + 1) * P], psEp[:, h * P:(h + 1) * P],
                        negbT[:, cl * H + h:cl * H + h + 1], slm[:, :],
                        ALU.mult, ALU.mult), [tG], inc=sC)
                tts3 = []
                for h in range(H):
                    tts3.append(pr.op("T", lambda t, h=h, sNB=sNB: t.transpose(
                        psT2[:, h * P:(h + 1) * P], sNB[:, h * P:(h + 1) * P],
                        identb[:, :]), [tNB, tkfn] if h == 0 else [], inc=sT))
                tNBT = pr.op("S", lambda s, sNBT=sNBT: s.copy(
                    sNBT[:, :], psT2[:, 0:512]), [tts3[-1]], inc=sS)
                tNBT_prev = tNBT
                tX = pr.op("V", lambda v, sX=sX, sNBT=sNBT: v.tensor_add(
                    sX[:, :], sNBT[:, :], id4[:, :]), [tNBT], inc=sC)
                pA, pAT = sNB, sNBT
                tpa = tNBT
                tpat = tNBT
                for j in range(1, 7):
                    tsq = None
                    for h in range(H):
                        w = [tpa, tpat]
                        if h == 0:
                            w.append(tEfree)
                        tsq = pr.op("T", lambda t, h=h, pA=pA, pAT=pAT: t.matmul(
                            psEp[:, h * P:(h + 1) * P],
                            pAT[:, h * P:(h + 1) * P],
                            pA[:, h * P:(h + 1) * P]), w if h == 0 else [],
                            inc=sT)
                    npA = powA[j % 2]
                    tpa_n = pr.op("S", lambda s, npA=npA: s.copy(
                        npA[:, :], psEp[:, 0:512]), [tsq], inc=sS)
                    tEfree = tpa_n
                    if j < 6:
                        tsqt = None
                        for h in range(H):
                            w = [tpa, tpat]
                            if h == 0:
                                w.append(tQfree)
                            tsqt = pr.op("T", lambda t, h=h, pA=pA, pAT=pAT:
                                         t.matmul(
                                psQT[:, h * P:(h + 1) * P],
                                pA[:, h * P:(h + 1) * P],
                                pAT[:, h * P:(h + 1) * P]),
                                w if h == 0 else [], inc=sT)
                        npAT = powT[j % 2]
                        tpat_n = pr.op("S", lambda s, npAT=npAT: s.copy(
                            npAT[:, :], psQT[:, 0:512]), [tsqt], inc=sS)
                        tQfree = tpat_n
                    txm = None
                    for h in range(H):
                        w = [tpa_n, tX]
                        if h == 0:
                            w.append(tXfree)
                        txm = pr.op("T", lambda t, h=h, sX=sX: t.matmul(
                            psX[:, h * P:(h + 1) * P], identb[:, :],
                            sX[:, h * P:(h + 1) * P], start=True, stop=False),
                            w if h == 0 else [], inc=sT)
                        txm = pr.op("T", lambda t, h=h, npA=npA, sX=sX: t.matmul(
                            psX[:, h * P:(h + 1) * P],
                            npA[:, h * P:(h + 1) * P],
                            sX[:, h * P:(h + 1) * P], start=False, stop=True),
                            [], inc=sT)
                    tX = pr.op("S", lambda s, sX=sX: s.copy(
                        sX[:, :], psX[:, 0:512]), [txm], inc=sS)
                    tXfree = tX
                    if j < 6:
                        pA, pAT = npA, npAT
                        tpa, tpat = tpa_n, tpat_n
                    else:
                        tpa = tpa_n
                tXc = tX
                tw1 = None
                for hh in range(2):
                    tw1m = None
                    for h in (2 * hh, 2 * hh + 1):
                        w = [tX, tvev]
                        if h == 2 * hh:
                            w.append(tXfree)
                        tw1m = pr.op("T", lambda t, h=h, hh=hh, cl=cl, sX=sX:
                                     t.matmul(
                            psX[:, (h % 2) * 256:(h % 2) * 256 + 256],
                            sX[:, h * P:(h + 1) * P],
                            vbt[:, cl * 1024 + h * 256:cl * 1024 + (h + 1) * 256]),
                            w if h == 2 * hh else [], inc=sT)
                    tw1 = pr.op("S", lambda s, hh=hh, sW1c=sW1c: s.copy(
                        sW1c[:, hh * 512:(hh + 1) * 512], psX[:, 0:512]),
                        [tw1m] + ([tw1] if tw1 else []), inc=sS)
                    tXfree = tw1
                tnbk = None
                for h in range(H):
                    tnbk = pr.op("V", lambda v, h=h, cl=cl, nbk=nbk:
                                 v.tensor_scalar(
                        nbk[:, h * P:(h + 1) * P],
                        ktok[:, cl * 512 + h * P:cl * 512 + (h + 1) * P],
                        negbT[:, cl * H + h:cl * H + h + 1], None, ALU.mult),
                        [tktk], inc=sC)
                tw2m = None
                for h in range(H):
                    w = [tnbk, tX, tpa, tEfree]
                    tw2m = pr.op("T", lambda t, h=h, nbk=nbk, sX=sX: t.matmul(
                        psEp[:, h * P:(h + 1) * P], nbk[:, h * P:(h + 1) * P],
                        sX[:, h * P:(h + 1) * P]), w if h == 0 else [], inc=sT)
                tw2 = pr.op("S", lambda s, sW2cT=sW2cT: s.copy(
                    sW2cT[:, :], psEp[:, 0:512]), [tw2m], inc=sS)
                tEfree = tw2
                tmkm = None
                for h in range(H):
                    w = [tw2, tkfn, tqev]
                    tmkm = pr.op("T", lambda t, h=h, cl=cl, kfn=kfn: t.matmul(
                        psEp[:, h * P:(h + 1) * P], kfn[:, h * P:(h + 1) * P],
                        qf[:, h * ASP + cl * P:h * ASP + (cl + 1) * P]),
                        w if h == 0 else [], inc=sT)
                tmkm_last = tmkm
                tmk = pr.op("V", lambda v, smk=smk: v.tensor_mul(
                    smk[:, :], psEp[:, 0:512], ut4[:, :]), [tmkm], inc=sC)
                tEfree = tmk
                if c > 0:
                    tS0c = pr.op("S", lambda s: s.copy(
                        sS0[:, :], sSf[:, :]), [tSadd, tS0c, tOev], inc=sS)
                tDc = None
                for hh in range(2):
                    tDm = None
                    for h in (2 * hh, 2 * hh + 1):
                        w = [tS0c, tw1, tw2]
                        if h == 2 * hh:
                            w.append(tDWfree)
                        tDm = pr.op("T", lambda t, h=h, sW1c=sW1c: t.matmul(
                            psDW[:, (h % 2) * 256:(h % 2) * 256 + 256],
                            identb[:, :],
                            sW1c[:, h * 256:(h + 1) * 256], start=True,
                            stop=False), w if h == 2 * hh else [], inc=sT)
                        tDm = pr.op("T", lambda t, h=h, sW2cT=sW2cT: t.matmul(
                            psDW[:, (h % 2) * 256:(h % 2) * 256 + 256],
                            sW2cT[:, h * P:(h + 1) * P],
                            sS0[:, h * 256:(h + 1) * 256], start=False,
                            stop=True), [], inc=sT)
                    tDh = pr.op("V", lambda v, hh=hh: v.tensor_copy(
                        sD[:, hh * 512:(hh + 1) * 512], psDW[:, 0:512]),
                        [tDm] + ([tDc] if tDc else []), inc=sC)
                    tDWfree = tDh
                    tDc = tDh
                for hh in range(2):
                    tSm = None
                    for h in (2 * hh, 2 * hh + 1):
                        w = [tDc, tktk]
                        if h == 2 * hh:
                            w.append(tSadd)
                        tSm = pr.op("T", lambda t, h=h, cl=cl: t.matmul(
                            psS[:, (h % 2) * 256:(h % 2) * 256 + 256],
                            ktok[:, cl * 512 + h * P:cl * 512 + (h + 1) * P],
                            sD[:, h * 256:(h + 1) * 256]),
                            w if h == 2 * hh else [], inc=sT)
                    tSadd = pr.op("V", lambda v, hh=hh: v.tensor_add(
                        sSf[:, hh * 512:(hh + 1) * 512],
                        sSf[:, hh * 512:(hh + 1) * 512], psS[:, 0:512]),
                        [tSm], inc=sC)
                tkng_new[cl] = tSadd
                tOm = None
                for h in range(H):
                    for f in range(2):
                        w = [tS0c, tDc, tmk, tqev]
                        if h == 0 and f == 0:
                            w += [t for t in (tOev, tYev[0], tYev[1], tWo)
                                  if t is not None]
                        tOm = pr.op("T", lambda t, h=h, f=f, cl=cl, smk=smk:
                                    t.matmul(
                            psOp[:, (2 * h + f) * P:(2 * h + f + 1) * P],
                            sS0[:, h * 256 + f * P:h * 256 + (f + 1) * P],
                            qf[:, h * ASP + cl * P:h * ASP + (cl + 1) * P],
                            start=True, stop=False), w if (h == 0 and f == 0)
                            else [], inc=sT)
                        tOm = pr.op("T", lambda t, h=h, f=f, smk=smk: t.matmul(
                            psOp[:, (2 * h + f) * P:(2 * h + f + 1) * P],
                            sD[:, h * 256 + f * P:h * 256 + (f + 1) * P],
                            smk[:, h * P:(h + 1) * P],
                            start=False, stop=True), [], inc=sT)
                for f in range(2):
                    tOev = pr.op("V", lambda v, f=f, cl=cl: v.tensor_copy(
                        oTs[f][:, cl * 512:(cl + 1) * 512]
                        .rearrange("p (c2 t) -> p c2 t", t=P),
                        psOp[:, :].rearrange("p (h2 s f2) -> p h2 s f2",
                                             s=2, f2=P)[:, :, f, :]),
                        [tOm], inc=sC)
            tkng = tkng_new
            tcs = None
            for f in range(2):
                tsq = pr.op("S", lambda s, f=f: s.activation(
                    sqb[:, :], oTs[f][:, :], AF.Square),
                    [tOev] + ([tcs] if tcs else []), inc=sS)
                for q4 in range(CPA):
                    r0 = 32 * (q4 // 2)
                    off = 512 * (q4 % 2)
                    w = [tsq]
                    if f == 0 and q4 == 0:
                        w += [tOev]
                    tcs = pr.op("T", lambda t, q4=q4, f=f, r0=r0, off=off:
                                t.matmul(
                        psOp[r0:r0 + 1, off:off + 512], oncb[:, 0:1],
                        sqb[:, q4 * 512:(q4 + 1) * 512],
                        start=(f == 0), stop=(f == 1)), w, inc=sT)
            tr1 = pr.op("S", lambda s: s.activation(
                rr1[0:1, 0:1024], psOp[0:1, :], AF.Sqrt,
                bias=epsT[0:1, 0:1], scale=scT[0:1, 0:1]),
                [tcs, teps, tsc_c], inc=sS)
            tr1b = pr.op("S", lambda s: s.activation(
                rr1[0:1, 1024:2048], psOp[32:33, :], AF.Sqrt,
                bias=epsT[0:1, 0:1], scale=scT[0:1, 0:1]), [tcs], inc=sS)
            tr2 = pr.op("V", lambda v: v.reciprocal(rr1[0:1, :], rr1[0:1, :]),
                        [tr1, tr1b], inc=sC)
            tr4 = pr.op("S", lambda s: s.copy(rrb[0:1, :], rr1[0:1, :]),
                        [tr2], inc=sS)
            tr1_p, tr1b_p = tr1, tr1b
            tbc = None
            for cl in range(CPA):
                tbm = None
                for h in range(H):
                    w = [tr4]
                    if h == 0:
                        w.append(tbc if tbc is not None else tcs)
                    tbm = pr.op("T", lambda t, cl=cl, h=h: t.matmul(
                        psOp[:, h * P:(h + 1) * P], on1b[:, :],
                        rrb[0:1, cl * 512 + h * P:cl * 512 + (h + 1) * P]),
                        w if h == 0 else [], inc=sT)
                tbc = pr.op("V", lambda v, cl=cl: v.tensor_copy(
                    rbc[:, cl * 512:(cl + 1) * 512], psOp[:, 0:512]),
                    [tbm], inc=sC)
                tbc_p = tbc
            tognew = None
            togs = [None, None]
            for ch2 in range(2 * H):
                h, f = ch2 // 2, ch2 % 2
                i = ch2 % 2
                w = [tgev, tbc]
                if togs[i] is not None:
                    w.append(togs[i])
                tsg = pr.op("S", lambda s, ch2=ch2, i=i: s.activation(
                    sgt[i][:, :], gf[:, ch2 * ASP:(ch2 + 1) * ASP], AF.Silu),
                    w, inc=sS)
                tsg2 = pr.op("V", lambda v, h=h, i=i: v.tensor_mul(
                    sgt[i][:, :].rearrange("p (c t) -> p c t", t=P),
                    sgt[i][:, :].rearrange("p (c t) -> p c t", t=P),
                    rbc[:, :].rearrange("p (c hh t) -> p c hh t", hh=H, t=P)
                    [:, :, h, :]), [tsg, tbc], inc=sC)
                tognew = pr.op("V", lambda v, h=h, f=f, i=i:
                               v.scalar_tensor_tensor(
                    oTs[f][:, :].rearrange("p (c hh t) -> p c hh t", hh=H, t=P)
                    [:, :, h, :],
                    oTs[f][:, :].rearrange("p (c hh t) -> p c hh t", hh=H, t=P)
                    [:, :, h, :],
                    nwsb[:, f:f + 1],
                    sgt[i][:, :].rearrange("p (c t) -> p c t", t=P),
                    ALU.mult, ALU.mult), [tsg2, tOev], inc=sC)
                togs[i] = tognew
            tgsil = tognew
            for cl in range(CPA):
                j = 4 * a + cl
                i = cl % 2
                wv = [tYev[i]] if tYev[i] else []
                if src_tok is not None:
                    wv.append(src_tok[j])
                tRs[i] = pr.dma("Y", rs2[i][:, :], src[j * P:(j + 1) * P, :],
                                waits=wv + [tognew], sem=f"ars{i}")
                tWo = None
                for half in range(2):
                    for ch2 in range(2 * H):
                        h, f = ch2 // 2, ch2 % 2
                        w2 = [tognew, tw["o"]]
                        if ch2 == 0 and half == 0:
                            w2 += [t for t in (tYev[0], tYev[1], tbc, tr1b)
                                   if t is not None]
                        tWo = pr.op("T", lambda t, cl=cl, half=half, h=h, f=f,
                                    ch2=ch2: t.matmul(
                            psOp[:, half * 512:(half + 1) * 512],
                            oTs[f][:, cl * 512 + h * P:cl * 512 + (h + 1) * P],
                            wos[:, ch2 * 1024 + half * 512:
                                ch2 * 1024 + (half + 1) * 512],
                            start=(ch2 == 0), stop=(ch2 == 2 * H - 1)),
                            w2 if (ch2 == 0 and half == 0) else [], inc=sT)
                tye = None
                for half in range(2):
                    w2 = [tWo, tRs[i]]
                    if tYw[i] is not None:
                        w2.append(tYw[i])
                    tye = pr.op("V", lambda v, i=i, half=half: v.tensor_add(
                        yout[i][:, half * 512:(half + 1) * 512],
                        rs2[i][:, half * 512:(half + 1) * 512],
                        psOp[:, half * 512:(half + 1) * 512]), w2, inc=sC)
                tYev[i] = tye
                tYw[i] = pr.dma("Y", dst[j * P:(j + 1) * P, :], yout[i][:, :],
                                waits=[tye], sem=f"ayw{i}")
                touts.append(tYw[i])
        aes.close()
        return touts
    if mode == "ffn1":
        touts, _ = ffn_phase(x_in, out_t, "A", tcon)
    elif mode == "attn":
        t1, tp1 = ffn_phase(x_in, resid1, "A", tcon)
        touts = attn_phase(resid1, out_t, pr.fence(), src_tok=t1, ptr_tok=tp1)
    elif mode == "full":
        t1, tp1 = ffn_phase(x_in, resid1, "A", tcon)
        t2 = attn_phase(resid1, resid2, pr.fence(), src_tok=t1, ptr_tok=tp1)
        touts, _ = ffn_phase(resid2, out_t, "B", pr.fence(), src_tok=t2)
    elif mode.startswith("m"):
        touts, _ = ffn_phase(x_in, out_t, "A", tcon, stop=int(mode[1:]))
    else:
        raise NotImplementedError(mode)

    pr.op("Y", lambda e: None, waits=pr.fence())
    pr.emit()
    es_top.close()
    return nc


def _consts(core):
    ident = np.eye(P).astype(ml_dtypes.bfloat16)
    slmask = np.tril(np.ones((P, P)), -1).astype(ml_dtypes.bfloat16)
    utmask = np.triu(np.ones((P, P)), 0).astype(ml_dtypes.bfloat16)
    ones8f = np.ones((EXP, 1), np.float32)
    ones8b = np.ones((EXP, 1)).astype(ml_dtypes.bfloat16)
    ones1b = np.ones((1, P)).astype(ml_dtypes.bfloat16)
    onescb = np.ones((P, 1)).astype(ml_dtypes.bfloat16)
    ident4 = np.tile(np.eye(P), (1, 4)).astype(ml_dtypes.bfloat16)
    ut4 = np.tile(np.triu(np.ones((P, P)), 0), (1, 4)).astype(ml_dtypes.bfloat16)
    pert8 = (np.arange(EXP).reshape(EXP, 1) * PERT).astype(np.float32)
    eoff8 = (np.arange(EXP).reshape(EXP, 1) * float(CAP // 4)).astype(np.float32)
    selmat = np.zeros((NCORE * EXP, EXP), np.float32)
    for c in range(core):
        for e in range(EXP):
            selmat[8 * c + e, e] = 1.0
    lselmat = np.zeros((NCORE * EXP, NCORE), np.float32)
    for cdst in range(NCORE):
        for csrc in range(cdst):
            lselmat[8 * csrc + core, cdst] = 1.0
    selcol = np.zeros((EXP, P), np.float32)
    for pp in range(P):
        selcol[pp // 16, pp] = 1.0
    pp = np.arange(P).reshape(P, 1)
    tokid = ((pp % 16 // 4) * (NCORE * 1024) + pp // 16 * 1024
             + (pp % 4) * 256
             + np.arange(256).reshape(1, 256)).astype(np.int16)
    coreoff = np.array([[core * NTOK]], np.int32)
    return dict(ident=ident, slmask=slmask, utmask=utmask, ones8f=ones8f,
                ones8b=ones8b, ones1b=ones1b, onescb=onescb, ident4=ident4,
                ut4c=ut4, pert8=pert8, eoff8=eoff8,
                selmat=selmat, lselmat=lselmat, selcol=selcol, tokid=tokid,
                coreoff=coreoff)


_BUILD_CACHE = {}


def kernel(L0_attn_x, L0_attn_Wq, L0_attn_Wk, L0_attn_Wv, L0_attn_Wb,
           L0_attn_Wg, L0_attn_Wo, L0_attn_norm_w, L0_ffn_Wr, L0_ffn_W1,
           L0_ffn_W2, _mode="full"):
    if _mode not in _BUILD_CACHE:
        ncb = build(_mode)
        mybir.codegen_inst_isa_subclasses(ncb)
        _BUILD_CACHE[_mode] = ncb
    ncb = _BUILD_CACHE[_mode]
    x = np.ascontiguousarray(np.asarray(L0_attn_x, np.float32))
    w = {k: np.ascontiguousarray(np.asarray(v, np.float32)) for k, v in dict(
        Wq=L0_attn_Wq, Wk=L0_attn_Wk, Wv=L0_attn_Wv, Wb=L0_attn_Wb,
        Wg=L0_attn_Wg, Wo=L0_attn_Wo, NW=L0_attn_norm_w, Wr=L0_ffn_Wr).items()}
    W1 = np.ascontiguousarray(np.asarray(L0_ffn_W1, np.float32))
    W2 = np.ascontiguousarray(np.asarray(L0_ffn_W2, np.float32))
    in_maps = []
    for c in range(NCORE):
        m = dict(x=x[c], W1e=W1[c], W2e=W2[c], **w, **_consts(c))
        in_maps.append(m)
    res = run_bass_kernel_spmd(ncb, in_maps, core_ids=list(range(NCORE)))
    return np.stack([np.asarray(res.results[c]["out"]) for c in range(NCORE)], 0)


# version 13
# speedup vs baseline: 1.0091x; 1.0091x over previous
import numpy as np
import ml_dtypes
from contextlib import ExitStack

import concourse.bass as bass
import concourse.mybir as mybir
import bass_rust
import concourse.library_config as libcfg
from concourse.bass import IndirectOffsetOnAxis
from concourse.bass_utils import run_bass_kernel_spmd

F32 = mybir.dt.float32
BF16 = mybir.dt.bfloat16
I16 = mybir.dt.int16
I32 = mybir.dt.int32
U8 = mybir.dt.uint8
ALU = mybir.AluOpType
AF = mybir.ActivationFunctionType
AX = mybir.AxisListType.X

P = 128
DM = 1024
NTOK = 4096
NBLK = 32
EXP = 8
CAP = 5120
DFF = 4096
H = 4
DK = 128
DV = 256
NCHUNK = 32
NSPAN = 4
SSPAN = 256
NSSPAN = CAP // SSPAN
NCORE = 8
EPS_LN = 1e-5
EPS_ON = 1e-5
EPS_K = 1e-12
SCALE_Q = float(DK) ** -0.5
PAD_NEG = -3.0e38
PERT = 1e-6


class Prog:

    def __init__(self, nc, es):
        self.nc = nc
        self.es = es
        self.ops = {e: [] for e in "TVSGY"}
        self.sems = {}
        self.cnt = {}
        self.last = {}

    def sem(self, name):
        if name not in self.sems:
            self.sems[name] = self.es.enter_context(self.nc.semaphore(name))
            self.cnt[name] = 0
        return self.sems[name]

    BULK = ("q_y", "q_g", "q_s")

    def op(self, eng, fn, waits=(), inc=None, amt=1):
        ws = []
        for w in waits:
            if w is None:
                continue
            s, v = w
            if s in self.BULK:
                v = self.cnt.get(s, 0)
            ws.append((s, v))
        if eng in "VSG" and amt == 1 and self.last.get(eng) is not None:
            ws.append(self.last[eng])
        tok = None
        if inc is not None:
            self.sem(inc)
            self.cnt[inc] += amt
            tok = (inc, self.cnt[inc])
            if eng in "VSG" and amt == 1:
                self.last[eng] = tok
        self.ops[eng].append((fn, ws, inc, amt))
        return tok

    def dma(self, eng, out, in_, waits=(), sem=None, **kw):
        q = sem or {"Y": "q_y", "G": "q_g", "S": "q_s"}[eng]
        return self.op(eng, lambda e: e.dma_start(out=out, in_=in_, **kw),
                       waits, inc=q, amt=16)

    def fence(self):
        return [(s, self.cnt[s]) for s in self.cnt if self.cnt[s] > 0]

    def emit(self):
        nc = self.nc
        with nc.Block() as blk:
            decos = {"T": blk.tensor, "V": blk.vector, "S": blk.scalar,
                     "G": blk.gpsimd, "Y": blk.sync}
            for eng in "TVSGY":
                if not self.ops[eng]:
                    continue
                ops = self.ops[eng]

                def run(e, ops=ops):
                    seen = {}
                    for fn, ws, inc, amt in ops:
                        for (s, v) in ws:
                            if seen.get(s, -1) < v:
                                e.wait_ge(self.sems[s], v)
                                seen[s] = v
                        inst = fn(e)
                        if inc is not None:
                            inst.then_inc(self.sems[inc], amt)
                decos[eng](run)


def build(mode="full"):
    nc = bass.Bass("TRN2", target_bir_lowering=False)

    x_in = nc.dram_tensor("x", [NTOK, DM], F32, kind="ExternalInput")
    Wqd = nc.dram_tensor("Wq", [DM, H * DK], F32, kind="ExternalInput")
    Wkd = nc.dram_tensor("Wk", [DM, H * DK], F32, kind="ExternalInput")
    Wvd = nc.dram_tensor("Wv", [DM, H * DV], F32, kind="ExternalInput")
    Wbd = nc.dram_tensor("Wb", [DM, H], F32, kind="ExternalInput")
    Wgd = nc.dram_tensor("Wg", [DM, H * DV], F32, kind="ExternalInput")
    Wod = nc.dram_tensor("Wo", [H * DV, DM], F32, kind="ExternalInput")
    NWd = nc.dram_tensor("NW", [DV], F32, kind="ExternalInput")
    Wrd = nc.dram_tensor("Wr", [DM, EXP], F32, kind="ExternalInput")
    W1d = nc.dram_tensor("W1e", [DM, DFF], F32, kind="ExternalInput")
    W2d = nc.dram_tensor("W2e", [DFF, DM], F32, kind="ExternalInput")
    identd = nc.dram_tensor("ident", [P, P], BF16, kind="ExternalInput")
    slmaskd = nc.dram_tensor("slmask", [P, P], BF16, kind="ExternalInput")
    utmaskd = nc.dram_tensor("utmask", [P, P], BF16, kind="ExternalInput")
    ones8fd = nc.dram_tensor("ones8f", [EXP, 1], F32, kind="ExternalInput")
    ones8bd = nc.dram_tensor("ones8b", [EXP, 1], BF16, kind="ExternalInput")
    ones1bd = nc.dram_tensor("ones1b", [1, P], BF16, kind="ExternalInput")
    onescbd = nc.dram_tensor("onescb", [P, 1], BF16, kind="ExternalInput")
    pert8d = nc.dram_tensor("pert8", [EXP, 1], F32, kind="ExternalInput")
    eoff8d = nc.dram_tensor("eoff8", [EXP, 1], F32, kind="ExternalInput")
    selmatd = nc.dram_tensor("selmat", [NCORE * EXP, EXP], F32, kind="ExternalInput")
    lselmatd = nc.dram_tensor("lselmat", [NCORE * EXP, NCORE], F32, kind="ExternalInput")
    selcold = nc.dram_tensor("selcol", [EXP, P], F32, kind="ExternalInput")
    tokidd = nc.dram_tensor("tokid", [P, 256], I16, kind="ExternalInput")
    ident4d = nc.dram_tensor("ident4", [P, 512], BF16, kind="ExternalInput")
    ut4d = nc.dram_tensor("ut4c", [P, 512], BF16, kind="ExternalInput")
    coreoffd = nc.dram_tensor("coreoff", [1, 1], I32, kind="ExternalInput")
    out_t = nc.dram_tensor("out", [NTOK, DM], F32, kind="ExternalOutput")

    resid1 = nc.dram_tensor("resid1", [NTOK, DM], F32)
    resid2 = nc.dram_tensor("resid2", [NTOK, DM], F32)
    xln_in = nc.dram_tensor("xln_in", [NTOK, DM], BF16)
    xln_all = nc.dram_tensor("xln_all", [NCORE * NTOK, DM], BF16, addr_space="Shared")
    spm_in = nc.dram_tensor("spm_in", [EXP, NTOK], F32)
    spm_all = nc.dram_tensor("spm_all", [NCORE * EXP, NTOK], F32, addr_space="Shared")
    eout_in = nc.dram_tensor("eout_in", [CAP, DM], BF16)
    QCAP = CAP // 4
    eout_all = nc.dram_tensor("eout_all", [4 * NCORE * QCAP, DM], BF16, addr_space="Shared")
    tblrow = nc.dram_tensor("tblrow", [1, 5376], I16)
    tblD = nc.dram_tensor("tblD", [P, 5376], I16)
    pinvrow = nc.dram_tensor("pinvrow", [1, NTOK], F32)
    RG = [list(range(NCORE))]

    es_top = ExitStack()
    es = es_top.enter_context
    pr = Prog(nc, es_top)
    greg = es(nc.gpsimd.register("greg"))

    identb = es(nc.sbuf_tensor("identb", [P, P], BF16))
    slm = es(nc.sbuf_tensor("slm", [P, P], BF16))
    utm = es(nc.sbuf_tensor("utm", [P, P], BF16))
    on8f = es(nc.sbuf_tensor("on8f", [EXP, 1], F32))
    on8b = es(nc.sbuf_tensor("on8b", [EXP, 1], BF16))
    on1b = es(nc.sbuf_tensor("on1b", [1, P], BF16))
    oncb = es(nc.sbuf_tensor("oncb", [P, 1], BF16))
    p8 = es(nc.sbuf_tensor("p8", [EXP, 1], F32))
    e8 = es(nc.sbuf_tensor("e8", [EXP, 1], F32))
    selm = es(nc.sbuf_tensor("selm", [NCORE * EXP, EXP], F32))
    lselm = es(nc.sbuf_tensor("lselm", [NCORE * EXP, NCORE], F32))
    selc = es(nc.sbuf_tensor("selc", [EXP, P], F32))
    tok16 = es(nc.sbuf_tensor("tok16", [P, 256], I16))
    id4 = es(nc.sbuf_tensor("id4", [P, 512], BF16))
    ut4 = es(nc.sbuf_tensor("ut4", [P, 512], BF16))
    wrsb = es(nc.sbuf_tensor("wrsb", [P, EXP * EXP], BF16))
    nwsb = es(nc.sbuf_tensor("nwsb", [P, 2], F32))
    co32 = es(nc.sbuf_tensor("co32", [1, 1], I32))
    flat32 = es(nc.sbuf_tensor("flat32", [P, NBLK], I32))
    af_t = es(nc.sbuf_tensor("af_t", [P, NBLK], F32))
    bf_t = es(nc.sbuf_tensor("bf_t", [P, NBLK], F32))
    msum = es(nc.sbuf_tensor("msum", [P, NBLK], F32))
    sqsum = es(nc.sbuf_tensor("sqsum", [P, NBLK], F32))
    mval_t = es(nc.sbuf_tensor("mval_t", [P, NBLK], F32))
    var_t = es(nc.sbuf_tensor("var_t", [P, NBLK], F32))
    rcp_t = es(nc.sbuf_tensor("rcp_t", [P, NBLK], F32))
    rstd_t = es(nc.sbuf_tensor("rstd_t", [P, NBLK], F32))
    nm_t = es(nc.sbuf_tensor("nm_t", [P, NBLK], F32))
    lntrash = es(nc.sbuf_tensor("lntrash", [P, DM], BF16))
    xringT = [es(nc.sbuf_tensor(f"xringT{i}", [P, DM], F32)) for i in range(2)]
    xlnbT = [es(nc.sbuf_tensor(f"xlnbT{i}", [P, DM], BF16)) for i in range(2)]
    xafT = es(nc.sbuf_tensor("xafT", [P, EXP * 512], BF16))
    psTRg = es(nc.psum_tensor("psTRg", [P, 1024], BF16))
    tblT32 = es(nc.sbuf_tensor("tblT32", [P, CAP // P], I32))

    tcon = []
    for dst, src in [(identb, identd), (slm, slmaskd), (utm, utmaskd),
                     (on8f, ones8fd), (on8b, ones8bd), (on1b, ones1bd),
                     (oncb, onescbd),
                     (p8, pert8d), (e8, eoff8d), (selm, selmatd),
                     (lselm, lselmatd), (selc, selcold), (tok16, tokidd),
                     (co32, coreoffd), (id4, ident4d), (ut4, ut4d)]:
        tcon.append(pr.dma("Y", dst[:, :], src[:, :]))
    t_wr = pr.dma("G", wrsb[:, :].rearrange("p (k c) -> p k c", c=EXP),
                  bass.AP(Wrd, 0, [[EXP, P], [P * EXP, EXP], [1, EXP]]),
                  sem="s_wr")
    tcon.append(t_wr)
    t_nw = pr.dma("Y", nwsb[:, :], bass.AP(NWd, 0, [[1, P], [P, 2]]),
                  allow_slow_non_contiguous=True)
    tcon.append(t_nw)
    import os as _os
    if _os.environ.get("SKIPLIB0"):
        t_lib0 = None
    else:
        t_lib0 = pr.op("G", lambda g: (g.load_library(libcfg.attn),
                                       g.nop())[1],
                       pr.fence(), inc="slib")
        tcon.append(t_lib0)
    tok16_tok = tcon[11]
    co_tok = tcon[12]

    def emit_ln(src, deps, sink, xring, xlnb, sC, sS, blk0=0, nblk=NBLK,
                ring_tok=None, blk_deps=None):
        GRP = 2
        toks = [None] * (blk0 + nblk)
        t_s = [None] * (blk0 + nblk)
        t_x = [None] * (blk0 + nblk)
        t_v = [None] * (blk0 + nblk)
        for g in range(blk0 // GRP, (blk0 + nblk) // GRP):
            for j in range(GRP * g, GRP * g + GRP):
                rb = xring[j % GRP]
                w = list(deps) if blk_deps is None else list(blk_deps[j])
                if j - GRP >= blk0:
                    w += [toks[j - GRP], t_s[j - GRP]]
                elif ring_tok is not None and ring_tok[j % GRP] is not None:
                    w += [ring_tok[j % GRP]]
                t_x[j] = pr.dma("Y", rb[:, :], src[j * P:(j + 1) * P, :],
                                waits=w, sem=f"lnx{j % GRP}")
                t_v[j] = pr.op("V", lambda v, j=j, rb=rb: v.reduce_sum(
                    msum[:, j:j + 1], rb[:, :], AX), [t_x[j]], inc=sC)
                t_s[j] = pr.op("S", lambda s, j=j, rb=rb: s.activation(
                    lntrash[:, :], rb[:, :], AF.Square,
                    accum_out=sqsum[:, j:j + 1]),
                    [t_x[j], t_s[j - 1] if j else None], inc=sS)
            j = GRP * g + GRP - 1
            sl = slice(j - GRP + 1, j + 1)
            pr.op("V", lambda v, sl=sl: v.tensor_scalar(
                mval_t[:, sl], msum[:, sl], 1.0 / DM, None, ALU.mult),
                [t_v[j]], inc=sC)
            pr.op("V", lambda v, sl=sl: v.tensor_scalar(
                var_t[:, sl], sqsum[:, sl], 1.0 / DM, EPS_LN, ALU.mult,
                ALU.add), [t_s[j]], inc=sC)
            pr.op("V", lambda v, sl=sl: v.tensor_mul(
                rcp_t[:, sl], mval_t[:, sl], mval_t[:, sl]), inc=sC)
            pr.op("V", lambda v, sl=sl: v.tensor_sub(
                var_t[:, sl], var_t[:, sl], rcp_t[:, sl]), inc=sC)
            tv2 = pr.op("V", lambda v, sl=sl: v.reciprocal(
                rcp_t[:, sl], var_t[:, sl]), inc=sC)
            tq = pr.op("S", lambda s, sl=sl: s.activation(
                rstd_t[:, sl], rcp_t[:, sl], AF.Sqrt), [tv2], inc=sS)
            pr.op("V", lambda v, sl=sl: v.tensor_mul(
                nm_t[:, sl], mval_t[:, sl], rstd_t[:, sl]), [tq], inc=sC)
            tgp = pr.op("V", lambda v, sl=sl: v.tensor_scalar(
                nm_t[:, sl], nm_t[:, sl], -1.0, None, ALU.mult), inc=sC)
            for jj in range(j - GRP + 1, j + 1):
                xb = xlnb[jj % 2]
                w2 = [tgp, t_x[jj]]
                if jj - 2 >= blk0 and toks[jj - 2] is not None:
                    w2.append(toks[jj - 2])
                if jj % 2 == 1:
                    tx = pr.op("S", lambda s, jj=jj, xb=xb: s.activation(
                        xb[:, :], xring[jj % GRP][:, :], AF.Identity,
                        scale=rstd_t[:, jj:jj + 1],
                        bias=nm_t[:, jj:jj + 1]), w2, inc=sS)
                else:
                    tx = pr.op("V", lambda v, jj=jj, xb=xb: v.tensor_scalar(
                        xb[:, :], xring[jj % GRP][:, :], rstd_t[:, jj:jj + 1],
                        nm_t[:, jj:jj + 1], ALU.mult, ALU.add), w2, inc=sC)
                toks[jj] = sink(jj, xb, tx)
        return toks, t_s

    def ffn_phase(src, dst, tag, deps, stop=99, src_tok=None):
        sC, sS, sT, sG, cc = f"v{tag}", f"s{tag}", f"t{tag}", f"g{tag}", "cc"

        def dummy_out():
            tz = pr.op("V", lambda v: v.memset(mval_t[:, :], 0.0), pr.fence(),
                       inc=sC)
            touts = []
            for j in range(NBLK):
                touts.append(pr.dma("Y", dst[j * P:(j + 1) * P, 0:NBLK],
                                    mval_t[:, :], waits=[tz]))
            fes.close()
            return touts, None
        fes = ExitStack()
        fe = fes.enter_context
        w1sb = fe(nc.sbuf_tensor(f"w1sb{tag}", [P, EXP * DFF], BF16))
        xring, xlnb = xringT, xlnbT
        maskA = fe(nc.sbuf_tensor(f"maskA{tag}", [EXP, NTOK], BF16))
        posA = fe(nc.sbuf_tensor(f"posA{tag}", [EXP, NTOK], I16))
        carry8 = fe(nc.sbuf_tensor(f"carry8{tag}", [EXP, 1], F32))
        pinv1 = fe(nc.sbuf_tensor(f"pinv1{tag}", [1, 1024], F32))
        cnt64 = fe(nc.sbuf_tensor(f"cnt64{tag}", [NCORE * EXP, 1], F32))
        ctmp = fe(nc.sbuf_tensor(f"ctmp{tag}", [NCORE * EXP, 1], F32))
        o8pf = fe(nc.sbuf_tensor(f"o8pf{tag}", [EXP, 2], F32))
        o8m1 = fe(nc.sbuf_tensor(f"o8m1{tag}", [EXP, 1], F32))
        kept_f = fe(nc.sbuf_tensor(f"keptf{tag}", [P, NBLK], F32))
        ptok = fe(nc.sbuf_tensor(f"ptok{tag}", [P, NBLK], F32))
        prefm1 = fe(nc.sbuf_tensor(f"prefm1{tag}", [P, 1], F32))
        psL = fe(nc.psum_tensor(f"psL{tag}", [P, 512], F32))
        psTR = psTRg
        psW1 = [fe(nc.psum_tensor(f"psW1{tag}{i}", [P, 512], F32)) for i in range(2)]
        psW2 = [fe(nc.psum_tensor(f"psW2{tag}{i}", [P, 512], F32)) for i in range(2)]

        libw = []
        if tag != "A":
            libw = [pr.op("G", lambda g: (g.load_library(libcfg.attn),
                                          g.nop())[1],
                          pr.fence(), inc=sG)]
        tW1 = pr.dma("G", w1sb[:, :].rearrange("p (k c) -> p k c", c=DFF),
                     bass.AP(W1d, 0, [[DFF, P], [P * DFF, EXP], [1, DFF]]),
                     waits=deps + libw, sem=f"sw1{tag}")
        pr.op("V", lambda v: v.memset(carry8[:, :], 0.0), deps, inc=sC)
        pr.op("V", lambda v: v.memset(cnt64[:, :], 0.0), inc=sC)

        def ln_sink(j, xb, tx):
            return pr.dma("Y", xln_in[j * P:(j + 1) * P, :], xb[:, :],
                          waits=[tx], sem=f"xw{tag}{j % 2}")
        bdeps = None if src_tok is None else [[src_tok[j]] for j in range(NBLK)]
        t_xlnw, _lns = emit_ln(src, deps, ln_sink, xring, xlnb, sC, sS,
                               blk_deps=bdeps)

        if stop <= 1:
            return dummy_out()
        if stop == 15:
            pass
        tAG1 = None
        if stop == 15:
            return dummy_out()

        res = ExitStack()
        re_ = res.enter_context
        xlf = re_(nc.sbuf_tensor(f"xlf{tag}", [P, EXP * 1024], BF16))
        l16 = re_(nc.sbuf_tensor(f"l16{tag}", [2 * EXP, 1024], F32))
        e16 = re_(nc.sbuf_tensor(f"e16{tag}", [2 * EXP, 1024], F32))
        ar16 = re_(nc.sbuf_tensor(f"ar16{tag}", [2 * EXP, 1024], F32))
        cbuf = re_(nc.sbuf_tensor(f"cbuf{tag}", [NCORE * EXP, 1024], F32))
        pr.op("V", lambda v: v.memset(l16[:, :], PAD_NEG), deps, inc=sC)
        pr.op("V", lambda v: v.memset(e16[:, :], 0.0), inc=sC)
        tglib = t_lib0 if tag == "A" else libw[0]
        if stop == 25:
            res.close()
            return dummy_out()
        t_spm = []
        t_pinv = []
        tcp = None
        tex_prev = None
        tpv_prev = None
        tspm_prev = None
        xblk = [re_(nc.sbuf_tensor(f"xblk{tag}{i}", [P, DM], BF16))
                for i in range(2)]
        t_xbe = [None, None]
        tAG1s = []
        for sp in range(NSPAN):
            cols = slice(sp * 1024, (sp + 1) * 1024)
            tAG1s.append(pr.op("G", lambda g, sp=sp: g.collective_compute(
                "AllGather", ALU.bypass, replica_groups=RG,
                ins=[xln_in[sp * 1024:(sp + 1) * 1024, :].opt()],
                outs=[xln_all[sp * NCORE * 1024:
                              (sp + 1) * NCORE * 1024, :].opt()]),
                waits=t_xlnw[sp * 8:(sp + 1) * 8], inc=f"cA{tag}"))
            ttp = []
            for b in range(8):
                i = b % 2
                w = [t_xlnw[sp * 8 + b]] + list(deps)
                if t_xbe[i] is not None:
                    w.append(t_xbe[i])
                if sp > 0 and b < 2:
                    w.append(tcp)
                txb = pr.dma("Y", xblk[i][:, :],
                             xln_in[(sp * 8 + b) * P:(sp * 8 + b + 1) * P, :],
                             waits=w, sem=f"sxb{tag}{i}")
                if stop in (21, 26):
                    t_xbe[i] = txb
                    ttp.append(txb)
                    continue
                ttr = None
                for c in range(EXP):
                    wtr = [txb]
                    if c == 0 and ttp:
                        wtr.append(ttp[-1])
                    ttr = pr.op("T", lambda t, i=i, c=c: t.transpose(
                        psTR[:, c * P:(c + 1) * P],
                        xblk[i][:, c * P:(c + 1) * P], identb[:, :]),
                        wtr, inc=sT)
                if stop == 22:
                    tev = pr.op("V", lambda v: v.tensor_copy(
                        lntrash[:, 0:P], psTR[:, 0:P]), [ttr], inc=sC)
                else:
                    tev = pr.op("V", lambda v, b=b: v.tensor_copy(
                        xlf[:, :].rearrange("p (c f) -> p c f", f=1024)
                        [:, :, b * P:(b + 1) * P],
                        psTR[:, :].rearrange("p (c f) -> p c f", f=P)),
                        [ttr], inc=sC)
                t_xbe[i] = tev
                ttp.append(tev)
            if stop in (16, 21, 22, 26):
                tcp = ttp[-1]
                continue
            for half in range(2):
                tm = None
                for k in range(EXP):
                    tm = pr.op("T", lambda t, k=k, half=half: t.matmul(
                        psL[0:EXP, 0:512],
                        wrsb[:, k * EXP:(k + 1) * EXP],
                        xlf[:, k * 1024 + half * 512:k * 1024 + (half + 1) * 512],
                        start=(k == 0), stop=(k == EXP - 1)),
                        [t_wr, ttp[-1]] + ([tcp] if tcp else []), inc=sT)
                tcp = pr.op("V", lambda v, half=half: v.tensor_copy(
                    l16[0:EXP, half * 512:(half + 1) * 512], psL[0:EXP, 0:512]),
                    [tm] + ([tex_prev] if (half == 0 and tex_prev) else []),
                    inc=sC)
            if stop == 17:
                continue
            tpe = pr.op("V", lambda v: v.tensor_scalar(
                l16[0:EXP, :], l16[0:EXP, :], p8[:, 0:1], None, ALU.subtract),
                [tcp], inc=sC)
            tmx = pr.op("G", lambda g: g.partition_all_reduce(
                ar16[:, :], l16[:, :], 2 * EXP, bass_rust.ReduceOp.max),
                [tpe, tglib] + ([tpv_prev] if tpv_prev else []), inc=sG)
            tsh = pr.op("V", lambda v: v.tensor_sub(
                l16[0:EXP, :], l16[0:EXP, :], ar16[0:EXP, :]), [tmx], inc=sC)
            tmk = pr.op("V", lambda v, cols=cols: v.tensor_single_scalar(
                maskA[:, cols], l16[0:EXP, :], 0.0, ALU.is_equal), [tsh], inc=sC)
            if stop == 18:
                continue
            tex = pr.op("S", lambda s: s.activation(
                e16[0:EXP, :], l16[0:EXP, :], AF.Exp),
                [tsh] + ([tspm_prev] if tspm_prev else []), inc=sS)
            tex_prev = tex
            tse = pr.op("G", lambda g: g.partition_all_reduce(
                ar16[:, :], e16[:, :], 2 * EXP, bass_rust.ReduceOp.add),
                [tex], inc=sG)
            tpv = pr.op("V", lambda v: v.reciprocal(pinv1[:, :], ar16[0:1, :]),
                        [tse] + (t_pinv[-1:] if t_pinv else []), inc=sC)
            tpv_prev = tpv
            t_pinv.append(pr.dma("Y", pinvrow[0:1, sp * 1024:(sp + 1) * 1024],
                                 pinv1[:, :], waits=[tpv], sem=f"spv{tag}"))
            if stop == 19:
                continue
            tsc = pr.op("V", lambda v, cols=cols: v.tensor_tensor_scan(
                posA[:, cols], maskA[:, cols], maskA[:, cols], carry8[:, 0:1],
                ALU.add, ALU.max), [tmk], inc=sC)
            pr.op("V", lambda v, sp=sp: v.tensor_copy(
                carry8[:, :], posA[:, (sp + 1) * 1024 - 1:(sp + 1) * 1024]),
                [tsc], inc=sC)
            tsp = pr.op("V", lambda v, cols=cols: v.tensor_mul(
                e16[0:EXP, :], posA[:, cols], maskA[:, cols]),
                [tsc, tse, tpv], inc=sC)
            tspm_prev = pr.dma("Y", spm_in[:, sp * 1024:(sp + 1) * 1024],
                               e16[0:EXP, :], waits=[tsp], sem=f"ssp{tag}")
            t_spm.append(tspm_prev)

        if stop <= 2 or stop in (16, 17, 18, 19, 21, 22, 26):
            res.close()
            return dummy_out()
        tAG2 = pr.op("G", lambda g: g.collective_compute(
            "AllGather", ALU.bypass, replica_groups=RG,
            ins=[spm_in.ap().opt()], outs=[spm_all.ap().opt()]),
            waits=t_spm, inc=f"cB{tag}")

        tcn = None
        for q in range(4):
            tq = pr.dma("Y", cbuf[:, :], spm_all[:, q * 1024:(q + 1) * 1024],
                        waits=[tAG2] + ([tcn] if tcn else []), sem=f"scb{tag}")
            tr = pr.op("V", lambda v: v.reduce_max(ctmp[:, :], cbuf[:, :], AX),
                       [tq], inc=sC)
            tcn = pr.op("V", lambda v: v.tensor_max(
                cnt64[:, :], cnt64[:, :], ctmp[:, :]), [tr], inc=sC)
        tof = pr.op("T", lambda t: t.matmul(
            psL[0:EXP, 0:1], selm[:, :], cnt64[:, :]), [tcn, tcp], inc=sT)
        tpf = pr.op("T", lambda t: t.matmul(
            psL[0:EXP, 1:2], lselm[:, :], cnt64[:, :]), [tcn], inc=sT)
        tcpo = pr.op("V", lambda v: v.tensor_copy(
            o8pf[:, :], psL[0:EXP, 0:2]), [tof, tpf], inc=sC)
        to8 = pr.op("V", lambda v: v.tensor_scalar(
            o8m1[:, :], o8pf[:, 0:1], -1.0, None, ALU.add), [tcpo], inc=sC)
        tpc = pr.op("T", lambda t: t.matmul(
            psL[:, 2:3], selc[:, :], o8pf[:, 1:2]), [tcpo], inc=sT)
        tpm = pr.op("V", lambda v: v.tensor_scalar(
            prefm1[:, :], psL[:, 2:3], -1.0, None, ALU.add), [tpc], inc=sC)

        if stop <= 3:
            res.close()
            return dummy_out()
        res.close()
        fx = pr.fence()

        ex = ExitStack()
        xe = ex.enter_context
        gpA = xe(nc.sbuf_tensor(f"gpA{tag}", [EXP, NTOK], F32))
        keptmA = xe(nc.sbuf_tensor(f"keptmA{tag}", [EXP, NTOK], BF16))
        wtA = xe(nc.sbuf_tensor(f"wtA{tag}", [EXP, NTOK], F32))
        myexp = xe(nc.sbuf_tensor(f"myexp{tag}", [P, 256], F32))
        spf = xe(nc.sbuf_tensor(f"spf{tag}", [P, 256], F32))
        c1t = xe(nc.sbuf_tensor(f"c1t{tag}", [P, 256], F32))
        c2t = xe(nc.sbuf_tensor(f"c2t{tag}", [P, 256], F32))
        cm8 = xe(nc.sbuf_tensor(f"cm8{tag}", [P, 256], U8))
        negt = xe(nc.sbuf_tensor(f"negt{tag}", [P, 256], F32))
        sel_o = xe(nc.sbuf_tensor(f"selo{tag}", [P, 256], F32))
        sel2_o = xe(nc.sbuf_tensor(f"sel2o{tag}", [P, 256], F32))
        sp16 = xe(nc.sbuf_tensor(f"sp16{tag}", [P, 256], I16))
        tbl1 = xe(nc.sbuf_tensor(f"tbl1{tag}", [P, 5376], I16))
        tb1f = xe(nc.sbuf_tensor(f"tb1f{tag}", [P, 5376], F32))
        evrow = xe(nc.sbuf_tensor(f"evrow{tag}", [1, 5376], I16))
        ones1f = xe(nc.sbuf_tensor(f"ones1f{tag}", [P, 1], F32))
        tbl16 = xe(nc.sbuf_tensor(f"tbl16{tag}", [P, CAP // P], I16))
        spr = xe(nc.sbuf_tensor(f"spr{tag}", [P, 256], F32))

        tgp = pr.op("V", lambda v: v.tensor_scalar(
            gpA[:, :], posA[:, :], o8m1[:, 0:1], None, ALU.add),
            [to8] + fx, inc=sC)
        tk1 = pr.op("V", lambda v: v.tensor_single_scalar(
            keptmA[:, :], gpA[:, :], float(CAP) - 0.5, ALU.is_lt), [tgp], inc=sC)
        tk2 = pr.op("V", lambda v: v.tensor_mul(
            keptmA[:, :], keptmA[:, :], maskA[:, :]), [tk1], inc=sC)
        tw1_ = pr.op("V", lambda v: v.tensor_scalar(
            wtA[:, :], gpA[:, :], e8[:, 0:1], None, ALU.add), [tk1], inc=sC)
        tw1b = tw1_
        for qi in range(1, 4):
            tqm = pr.op("V", lambda v, qi=qi: v.tensor_single_scalar(
                posA[:, :], gpA[:, :], qi * float(CAP // 4) - 0.5, ALU.is_gt),
                [tw1b, tk1], inc=sC)
            tw1b = pr.op("V", lambda v: v.scalar_tensor_tensor(
                wtA[:, :], posA[:, :], float((NCORE - 1) * (CAP // 4)),
                wtA[:, :], ALU.mult, ALU.add), [tqm], inc=sC)
        tw2_ = pr.op("V", lambda v: v.tensor_mul(
            wtA[:, :], wtA[:, :], keptmA[:, :]), [tw1b, tk2], inc=sC)

        tkx = tfx = None
        for j in range(NBLK):
            tkx = pr.op("T", lambda t, j=j: t.matmul(
                psL[:, 32 + j:33 + j], keptmA[:, j * P:(j + 1) * P], on8b[:, :]),
                [tk2, tpm], inc=sT)
            tfx = pr.op("T", lambda t, j=j: t.matmul(
                psL[:, 64 + j:65 + j], wtA[:, j * P:(j + 1) * P], on8f[:, :]),
                [tw2_], inc=sT)
        tke = pr.op("V", lambda v: v.tensor_copy(
            kept_f[:, :], psL[:, 32:32 + NBLK]), [tkx], inc=sC)
        tfe = pr.op("V", lambda v: v.tensor_copy(
            flat32[:, :], psL[:, 64:64 + NBLK]), [tfx], inc=sC)
        tptk = pr.dma("Y", ptok[:, :],
                      bass.AP(pinvrow, 0, [[1, P], [P, NBLK]]), waits=t_pinv,
                      allow_slow_non_contiguous=True, sem=f"spt{tag}")
        taf = pr.op("V", lambda v: v.scalar_tensor_tensor(
            af_t[:, :], ptok[:, :], 0.5, kept_f[:, :], ALU.mult, ALU.mult),
            [tke, tptk], inc=sC)
        tbf = pr.op("V", lambda v: v.tensor_scalar(
            bf_t[:, :], kept_f[:, :], -0.5, 0.5, ALU.mult, ALU.add),
            [tke], inc=sC)

        if stop <= 4:
            ex.close()
            return dummy_out()
        trl = pr.op("G", lambda g: (g.reg_load(greg, co32[0:1, 0:1]),
                                    g.nop())[1],
                    [co_tok, tAG2], inc=sG)
        tme = pr.dma("G", myexp[:, :],
                     bass.AP(spm_all, greg,
                             [[EXP * NTOK, EXP], [256, 16], [1, 256]]),
                     waits=[trl] + fx, sem=f"sme{tag}")
        if stop == 51:
            ex.close()
            return dummy_out()
        tng = pr.op("V", lambda v: v.memset(negt[:, :], -1.0), fx, inc=sC)
        tsf = pr.op("V", lambda v: v.tensor_scalar(
            spf[:, :], myexp[:, :], prefm1[:, 0:1], None, ALU.add),
            [tme, tpm], inc=sC)
        tc1 = pr.op("V", lambda v: v.tensor_single_scalar(
            c1t[:, :], myexp[:, :], 0.5, ALU.is_gt), [tme], inc=sC)
        tc2 = pr.op("V", lambda v: v.tensor_single_scalar(
            c2t[:, :], spf[:, :], float(CAP) - 0.5, ALU.is_lt), [tsf], inc=sC)
        tcm = pr.op("V", lambda v: v.tensor_mul(
            cm8[:, :], c1t[:, :], c2t[:, :]), [tc1, tc2], inc=sC)
        tsl0 = pr.op("V", lambda v: v.tensor_copy(
            sel_o[:, :], negt[:, :]), [tng], inc=sC)
        tsl = pr.op("V", lambda v: v.copy_predicated(
            sel_o[:, :], cm8[:, :], spf[:, :]), [tcm, tsl0], inc=sC)
        tll = pr.op("G", lambda g: (g.load_library(libcfg.local_scatter),
                                    g.nop())[1],
                    pr.fence(), inc=sG)
        tsc_ = None
        for rpass in range(3):
            base = rpass * 1792
            td1 = pr.op("V", lambda v, base=base: v.tensor_scalar(
                spr[:, :], sel_o[:, :], -float(base), None, ALU.add),
                [tsl] + ([tsc_] if tsc_ else []), inc=sC)
            tc1b = pr.op("V", lambda v: v.tensor_single_scalar(
                c1t[:, :], spr[:, :], -0.5, ALU.is_gt), [td1], inc=sC)
            tc2b = pr.op("V", lambda v: v.tensor_single_scalar(
                c2t[:, :], spr[:, :], 1791.5, ALU.is_lt), [td1], inc=sC)
            tcmb = pr.op("V", lambda v: v.tensor_mul(
                cm8[:, :], c1t[:, :], c2t[:, :]), [tc1b, tc2b], inc=sC)
            tslb0 = pr.op("V", lambda v: v.tensor_copy(
                sel2_o[:, :], negt[:, :]), [tng], inc=sC)
            tslb = pr.op("V", lambda v: v.copy_predicated(
                sel2_o[:, :], cm8[:, :], spr[:, :]), [tcmb, tslb0], inc=sC)
            t16b = pr.op("V", lambda v: v.tensor_copy(
                sp16[:, :], sel2_o[:, :]), [tslb], inc=sC)
            tsc_ = pr.op("G", lambda g, base=base: g.local_scatter(
                tbl1[:, base:base + 1792], tok16[:, :], sp16[:, :], P, 1792,
                256), [t16b, tll, tok16_tok] + ([tsc_] if tsc_ else []),
                inc=sG)
        if stop == 52:
            ex.close()
            return dummy_out()
        tcst = pr.op("V", lambda v: v.tensor_copy(tb1f[:, :], tbl1[:, :]),
                     [tsc_], inc=sC)
        tone = pr.op("V", lambda v: v.memset(ones1f[:, :], 1.0), fx, inc=sC)
        tev_ = None
        tevs = []
        for ci in range(11):
            c0 = ci * 512
            cw = 512 if ci < 10 else 256
            wv = [tcst, tone]
            if ci == 0:
                wv += [tke, tfe]
            if tev_ is not None:
                wv.append(tev_)
            tmmc = pr.op("T", lambda t, c0=c0, cw=cw: t.matmul(
                psL[0:1, 0:cw], ones1f[:, :], tb1f[:, c0:c0 + cw],
                start=True, stop=True), wv, inc=sT)
            tev_ = pr.op("V", lambda v, c0=c0, cw=cw:
                         v.tensor_copy(evrow[0:1, c0:c0 + cw],
                                       psL[0:1, 0:cw]), [tmmc], inc=sC)
            tevs.append(tev_)
        if stop == 53:
            ex.close()
            return dummy_out()
        ttr1 = pr.dma("Y", tblrow[0:1, :], evrow[0:1, :], waits=tevs,
                      sem=f"st1{tag}")
        ttr2 = pr.dma("Y", tbl16[:, :],
                      bass.AP(tblrow, 0, [[1, P], [P, CAP // P]]), waits=[ttr1],
                      allow_slow_non_contiguous=True, sem=f"st2{tag}")
        tt32 = pr.op("V", lambda v: v.tensor_copy(
            tblT32[:, :], tbl16[:, :]), [ttr2], inc=sC)

        if stop <= 5:
            ex.close()
            return dummy_out()
        ex.close()
        fx2 = pr.fence()

        me = ExitStack()
        mm = me.enter_context
        w2sb = mm(nc.sbuf_tensor(f"w2sb{tag}", [P, NBLK * DM], BF16))
        dispTM = [mm(nc.sbuf_tensor(f"dtm{tag}{i}", [P, DM], BF16)) for i in range(2)]
        dispF = mm(nc.sbuf_tensor(f"dispF{tag}", [P, EXP * SSPAN], BF16))
        hT = mm(nc.sbuf_tensor(f"hT{tag}", [P, NBLK * SSPAN], BF16))
        zt = [mm(nc.sbuf_tensor(f"zt{tag}{i}", [P, DM], BF16)) for i in range(2)]

        tW2 = pr.dma("G", w2sb[:, :].rearrange("p (m c) -> p m c", c=DM),
                     bass.AP(W2d, 0, [[DM, P], [P * DM, NBLK], [1, DM]]),
                     waits=fx2, sem=f"sw2{tag}")

        t_z = []
        t_gprev = [None, None]
        t_dfprev = None
        t_zdma = [None, None]
        t_relu = [None] * NBLK
        tAGq = []
        for w in range(NSSPAN):
            tdf = []
            for b2 in range(2):
                blk = 2 * w + b2
                wg = tAG1s + [tt32] + fx2
                if t_gprev[b2] is not None:
                    wg.append(t_gprev[b2])
                tg = pr.op("G", lambda g, b2=b2, blk=blk: g.indirect_dma_start(
                    dispTM[b2][:, :], None, xln_all[:, :],
                    IndirectOffsetOnAxis(tblT32[:, blk:blk + 1], 0)),
                    wg, inc=f"sdg{tag}{b2}", amt=16)
                ttr = None
                for c in range(EXP):
                    wtr = [tg]
                    if c == 0 and t_dfprev is not None:
                        wtr.append(t_dfprev)
                    if c == 0 and tdf:
                        wtr.append(tdf[-1])
                    ttr = pr.op("T", lambda t, b2=b2, c=c: t.transpose(
                        psTR[:, c * P:(c + 1) * P],
                        dispTM[b2][:, c * P:(c + 1) * P],
                        identb[:, :]), wtr, inc=sT)
                tdf.append(pr.op("V", lambda v, b2=b2: v.tensor_copy(
                    dispF[:, :].rearrange("p (c f) -> p c f", f=SSPAN)
                    [:, :, b2 * P:(b2 + 1) * P],
                    psTR[:, :].rearrange("p (c f) -> p c f", f=P)),
                    [ttr], inc=sC))
                t_gprev[b2] = tdf[-1]
            for m in range(NBLK):
                pw = psW1[m % 2]
                wv = tdf + [tW1]
                if m >= 2:
                    wv.append(t_relu[m - 2])
                elif t_relu[m - 2 + NBLK] is not None:
                    wv.append(t_relu[m - 2 + NBLK])
                tm = None
                for k in range(EXP):
                    tm = pr.op("T", lambda t, k=k, m=m, pw=pw: t.matmul(
                        pw[:, 0:SSPAN],
                        w1sb[:, k * DFF + m * P:k * DFF + (m + 1) * P],
                        dispF[:, k * SSPAN:(k + 1) * SSPAN],
                        start=(k == 0), stop=(k == EXP - 1)),
                        wv if k == 0 else [], inc=sT)
                if m % 2 == 0:
                    t_relu[m] = pr.op("V", lambda v, m=m, pw=pw: v.tensor_scalar(
                        hT[:, m * SSPAN:(m + 1) * SSPAN], pw[:, 0:SSPAN], 0.0,
                        None, ALU.max), [tm], inc=sC)
                else:
                    t_relu[m] = pr.op("S", lambda s, m=m, pw=pw: s.activation(
                        hT[:, m * SSPAN:(m + 1) * SSPAN], pw[:, 0:SSPAN],
                        AF.Relu), [tm], inc=sS)
            t_dfprev = tm
            for b2 in range(2):
                pz = psW2[b2]
                tz_half = None
                for half in range(2):
                    tmm = None
                    for m in range(NBLK):
                        wv = [t_relu[m], tW2]
                        if half == 1 and tz_half is not None:
                            wv.append(tz_half)
                        if half == 0 and t_zdma[b2] is not None:
                            wv.append(t_zdma[b2])
                        tmm = pr.op("T", lambda t, m=m, b2=b2, pz=pz, half=half:
                                    t.matmul(
                            pz[:, :],
                            hT[:, m * SSPAN + b2 * P:m * SSPAN + (b2 + 1) * P],
                            w2sb[:, m * DM + half * 512:m * DM + (half + 1) * 512],
                            start=(m == 0), stop=(m == NBLK - 1)),
                            wv, inc=sT)
                    if half == 0:
                        tz_half = pr.op("S", lambda s, b2=b2, half=half, pz=pz:
                                        s.copy(
                            zt[b2][:, half * 512:(half + 1) * 512], pz[:, :]),
                            [tmm], inc=sS)
                    else:
                        tz_half = pr.op("V", lambda v, b2=b2, half=half, pz=pz:
                                        v.tensor_copy(
                            zt[b2][:, half * 512:(half + 1) * 512], pz[:, :]),
                            [tmm], inc=sC)
                t_zdma[b2] = pr.dma(
                    "Y", eout_in[(2 * w + b2) * P:(2 * w + b2 + 1) * P, :],
                    zt[b2][:, :], waits=[tz_half], sem=f"szw{tag}{b2}")
                t_z.append(t_zdma[b2])
            if (w + 1) % (NSSPAN // 4) == 0 and w + 1 < NSSPAN:
                qq = (w + 1) // (NSSPAN // 4) - 1
                tAGq.append(pr.op("G", lambda g, qq=qq: g.collective_compute(
                    "AllGather", ALU.bypass, replica_groups=RG,
                    ins=[eout_in[qq * QCAP:(qq + 1) * QCAP, :].opt()],
                    outs=[eout_all[qq * NCORE * QCAP:
                                   (qq + 1) * NCORE * QCAP, :].opt()]),
                    waits=t_z[qq * (NSSPAN // 2):] +
                          ([tAG2] if qq == 0 else []), inc=f"cC{tag}"))

        tAG3 = pr.op("G", lambda g: g.collective_compute(
            "AllGather", ALU.bypass, replica_groups=RG,
            ins=[eout_in[3 * QCAP:CAP, :].opt()],
            outs=[eout_all[3 * NCORE * QCAP:4 * NCORE * QCAP, :].opt()]),
            waits=t_z[3 * (NSSPAN // 2):], inc=f"cC{tag}")

        if stop <= 6:
            me.close()
            return dummy_out()
        me.close()
        fx3 = pr.fence()

        ce = ExitStack()
        cm_ = ce.enter_context
        NCR = 4
        eoutg = [cm_(nc.sbuf_tensor(f"eog{tag}{i}", [P, DM], BF16)) for i in range(NCR)]
        xlt = [cm_(nc.sbuf_tensor(f"xlt{tag}{i}", [P, DM], BF16)) for i in range(NCR)]
        rsb = [cm_(nc.sbuf_tensor(f"rsb{tag}{i}", [P, DM], F32)) for i in range(NCR)]
        ctm = [cm_(nc.sbuf_tensor(f"ctm{tag}{i}", [P, DM], F32)) for i in range(NCR)]
        ott = [cm_(nc.sbuf_tensor(f"ott{tag}{i}", [P, DM], F32)) for i in range(NCR)]
        touts = []
        tprevc = [None] * NCR
        for j in range(NBLK):
            i = j % NCR
            wg = tAGq + [tAG3, tfe, taf, tbf] + fx3
            if tprevc[i] is not None:
                wg.append(tprevc[i])
                wg.append(touts[j - NCR])
            tg = pr.op("G", lambda g, j=j, i=i: g.indirect_dma_start(
                eoutg[i][:, :], None, eout_all[:, :],
                IndirectOffsetOnAxis(flat32[:, j:j + 1], 0)),
                wg, inc=f"seg{tag}{i}", amt=16)
            tx = pr.dma("Y", xlt[i][:, :], xln_in[j * P:(j + 1) * P, :],
                        waits=wg, sem=f"sxl{tag}{i}")
            trs = pr.dma("Y", rsb[i][:, :], src[j * P:(j + 1) * P, :], waits=wg,
                         sem=f"srs{tag}{i}")
            t1 = pr.op("V", lambda v, j=j, i=i: v.scalar_tensor_tensor(
                ctm[i][:, :], eoutg[i][:, :], af_t[:, j:j + 1], rsb[i][:, :],
                ALU.mult, ALU.add), [tg, trs], inc=sC)
            t2 = pr.op("V", lambda v, j=j, i=i: v.scalar_tensor_tensor(
                ott[i][:, :], xlt[i][:, :], bf_t[:, j:j + 1], ctm[i][:, :],
                ALU.mult, ALU.add), [t1, tx], inc=sC)
            tprevc[i] = t2
            touts.append(pr.dma("Y", dst[j * P:(j + 1) * P, :], ott[i][:, :],
                                waits=[t2], sem=f"sow{tag}{i}"))
        ce.close()
        fes.close()
        return touts, t_gprev[1] or t_gprev[0]


    def attn_phase(src, dst, deps, src_tok=None, ptr_tok=None):
        sC, sS, sT, sG = "vX", "sX", "tX", "gX"
        ASP = 512
        NASP = NTOK // ASP
        CPA = ASP // P
        aes = ExitStack()
        ae = aes.enter_context
        wqs = ae(nc.sbuf_tensor("wqs", [P, EXP * 512], BF16))
        wks = ae(nc.sbuf_tensor("wks", [P, EXP * 512], BF16))
        wvs = ae(nc.sbuf_tensor("wvs", [P, EXP * 1024], BF16))
        wgs = ae(nc.sbuf_tensor("wgs", [P, EXP * 1024], BF16))
        wos = ae(nc.sbuf_tensor("wos", [P, EXP * 1024], BF16))
        wbs = ae(nc.sbuf_tensor("wbs", [P, EXP * H], BF16))
        xr2, xl2 = xringT, xlnbT
        xaf = xafT
        qf = ae(nc.sbuf_tensor("qf", [P, H * ASP], BF16))
        kfr = ae(nc.sbuf_tensor("kfr", [P, H * ASP], BF16))
        ktok = ae(nc.sbuf_tensor("ktok", [P, CPA * 512], BF16))
        vbt = ae(nc.sbuf_tensor("vbt", [P, CPA * 1024], BF16))
        gf = ae(nc.sbuf_tensor("gf", [P, 2 * H * ASP], BF16))
        betaT = ae(nc.sbuf_tensor("betaT", [P, CPA * H], F32))
        negbT = ae(nc.sbuf_tensor("negbT", [P, CPA * H], F32))
        kss = ae(nc.sbuf_tensor("kss", [P, CPA * H], F32))
        krs = ae(nc.sbuf_tensor("krs", [P, CPA * H], F32))
        kfn2 = [ae(nc.sbuf_tensor(f"kfn{i}", [P, 512], BF16)) for i in range(2)]
        sNB2 = [ae(nc.sbuf_tensor(f"sNB{i}", [P, 512], BF16)) for i in range(2)]
        sNBT2 = [ae(nc.sbuf_tensor(f"sNBT{i}", [P, 512], BF16)) for i in range(2)]
        powA2 = [[ae(nc.sbuf_tensor(f"powA{i}{jj}", [P, 512], BF16))
                  for jj in range(2)] for i in range(2)]
        powT2 = [[ae(nc.sbuf_tensor(f"powT{i}{jj}", [P, 512], BF16))
                  for jj in range(2)] for i in range(2)]
        sX2 = [ae(nc.sbuf_tensor(f"sX{i}", [P, 512], BF16)) for i in range(2)]
        nbk2 = [ae(nc.sbuf_tensor(f"nbk{i}", [P, 512], BF16)) for i in range(2)]
        sW1c2 = [ae(nc.sbuf_tensor(f"sW1c{i}", [P, 1024], BF16)) for i in range(2)]
        sW2cT2 = [ae(nc.sbuf_tensor(f"sW2cT{i}", [P, 512], BF16)) for i in range(2)]
        smk2 = [ae(nc.sbuf_tensor(f"smk{i}", [P, 512], BF16)) for i in range(2)]
        sD = ae(nc.sbuf_tensor("sD", [P, 1024], BF16))
        sS0 = ae(nc.sbuf_tensor("sS0", [P, 1024], BF16))
        sSf = ae(nc.sbuf_tensor("sSf", [P, 1024], F32))
        oTs = [ae(nc.sbuf_tensor(f"oTs{i}", [P, CPA * 512], BF16)) for i in range(2)]
        sqb = ae(nc.sbuf_tensor("sqb", [P, CPA * 512], BF16))
        rr1 = ae(nc.sbuf_tensor("rr1", [1, CPA * 512], F32))
        epsT = ae(nc.sbuf_tensor("epsT", [1, 1], F32))
        scT = ae(nc.sbuf_tensor("scT", [1, 1], F32))
        rrb = ae(nc.sbuf_tensor("rrb", [1, CPA * 512], BF16))
        rbc = ae(nc.sbuf_tensor("rbc", [P, CPA * 512], BF16))
        sgt = [ae(nc.sbuf_tensor(f"sgt{i}", [P, 512], BF16)) for i in range(2)]
        rs2 = [ae(nc.sbuf_tensor(f"rs2{i}", [P, DM], F32)) for i in range(2)]
        yout = [ae(nc.sbuf_tensor(f"yout{i}", [P, DM], F32)) for i in range(2)]
        psS = ae(nc.psum_tensor("psS", [P, 512], F32))
        psX = ae(nc.psum_tensor("psX", [P, 512], F32))
        psDW = ae(nc.psum_tensor("psDW", [P, 512], F32))
        psQT = ae(nc.psum_tensor("psQT", [P, 512], F32))
        psOp = ae(nc.psum_tensor("psOp", [P, 1024], F32))
        psEp = ae(nc.psum_tensor("psEp", [P, 512], F32))
        psT2 = psTRg

        tw = {}
        for nm, sb, dr, cw in [("q", wqs, Wqd, 512), ("k", wks, Wkd, 512),
                               ("v", wvs, Wvd, 1024), ("g", wgs, Wgd, 1024),
                               ("o", wos, Wod, 1024), ("b", wbs, Wbd, H)]:
            tw[nm] = pr.dma(
                "G", sb[:, :].rearrange("p (k c) -> p k c", c=cw),
                bass.AP(dr, 0, [[cw, P], [P * cw, EXP], [1, cw]]),
                waits=deps, sem=f"aw{nm}")

        tS0m = pr.op("V", lambda v: v.memset(sS0[:, :], 0.0), deps, inc=sC)
        teps = pr.op("V", lambda v: v.memset(epsT[:, :], EPS_ON), deps, inc=sC)
        tsc_c = pr.op("V", lambda v: v.memset(scT[:, :], 1.0 / DV), deps, inc=sC)
        tSadd = pr.op("V", lambda v: v.memset(sSf[:, :], 0.0), deps, inc=sC)
        touts = []
        tSupd = [None] * H
        tS0c = tS0m
        tDc = None
        tXc = None
        tOev = None
        tWo = None
        tYev = [None, None]
        tYw = [None, None]
        tRs = [None, None]
        tNBT_prev = None
        tmkm_last = None
        tneglast = None
        tgsil = None
        tvev = None
        tr1_p = tr1b_p = None
        tbc_p = None
        tEfree = tQfree = tXfree = tDWfree = None
        tknG = [None, None]
        tkng = [None] * CPA
        lnring = [None, None]
        prev_evac = [None]

        for a in range(NASP):
            def asink(j, xb, tx, a=a):
                bb = j - 4 * a
                tev = None
                for wv in range(2):
                    trs = None
                    for cc4 in range(4):
                        cch = wv * 4 + cc4
                        wtr = [tx]
                        if cc4 == 0:
                            wtr += [prev_evac[0], ptr_tok]
                        trs = pr.op("T", lambda t, cch=cch, cc4=cc4, xb=xb:
                                    t.transpose(
                            psT2[:, 512 + cc4 * P:512 + (cc4 + 1) * P],
                            xb[:, cch * P:(cch + 1) * P],
                            identb[:, :]), wtr, inc=sT)
                    tev = pr.op("V", lambda v, bb=bb, wv=wv: v.tensor_copy(
                        xaf[:, :].rearrange("p (c f) -> p c f", f=ASP)
                        [:, 4 * wv:4 * wv + 4, bb * P:(bb + 1) * P],
                        psT2[:, 512:1024].rearrange("p (c f) -> p c f", f=P)),
                        [trs], inc=sC)
                    prev_evac[0] = tev
                return tev

            abdeps = None
            if src_tok is not None:
                abdeps = [None] * NBLK
                for jj in range(NBLK):
                    abdeps[jj] = [src_tok[jj]]
            t_ln, t_lns = emit_ln(src, deps, asink, xr2, xl2, sC, sS,
                                  blk0=4 * a, nblk=4, ring_tok=lnring,
                                  blk_deps=abdeps)
            lnring = [t_lns[4 * a + 2], t_lns[4 * a + 3]]
            txaf = t_ln[4 * a + 3]

            projps = [(psDW, 0), (psX, 0), (psOp, 0), (psOp, 512)]
            pv = [[t for t in (tDWfree,) if t is not None],
                  [t for t in (tXfree,) if t is not None],
                  [t for t in (tYev[0], tYev[1], tOev, tr1b_p) if t is not None],
                  [t for t in (tYev[0], tYev[1], tOev, tr1b_p) if t is not None]]
            nps = [0]

            def proj(outsb, outcol, wsb, wcol, scale, toks, slots=(0, 1)):
                slot = slots[nps[0] % len(slots)]
                ps, off = projps[slot]
                nps[0] += 1
                cwid = wsb.shape[1] // EXP
                tm = None
                for k in range(EXP):
                    w = list(toks) + [txaf] + (pv[slot] if k == 0 else [])
                    tm = pr.op("T", lambda t, k=k, ps=ps, off=off: t.matmul(
                        ps[:, off:off + 512],
                        wsb[:, k * cwid + wcol:k * cwid + wcol + P],
                        xaf[:, k * ASP:k * ASP + 512],
                        start=(k == 0), stop=(k == EXP - 1)),
                        w if k == 0 else [], inc=sT)
                if scale is None:
                    if nps[0] % 2 == 0:
                        te = pr.op("S", lambda s, ps=ps, off=off: s.copy(
                            outsb[:, outcol:outcol + 512], ps[:, off:off + 512]),
                            [tm], inc=sS)
                    else:
                        te = pr.op("V", lambda v, ps=ps, off=off: v.tensor_copy(
                            outsb[:, outcol:outcol + 512], ps[:, off:off + 512]),
                            [tm], inc=sC)
                else:
                    te = pr.op("S", lambda s, ps=ps, off=off: s.mul(
                        outsb[:, outcol:outcol + 512], ps[:, off:off + 512],
                        scale), [tm], inc=sS)
                pv[slot] = [te]
                return te, slot

            tqev = tkev = tgev = None
            for m in range(H):
                tqev, _sl = proj(qf, m * ASP, wqs, m * P, SCALE_Q, [tw["q"]])
            for m in range(H):
                tkev, _sl = proj(kfr, m * ASP, wks, m * P, None, [tw["k"]])
            for m in range(2 * H):
                tgev, _sl = proj(gf, m * ASP, wgs, m * P, None,
                                 [tw["g"], tgsil], slots=(2, 3))
            for bb in range(CPA):
                tmb = None
                for k in range(EXP):
                    w = [tw["b"], txaf]
                    if k == 0:
                        w += [tmkm_last, tneglast, tbc_p, tEfree]
                    tmb = pr.op("T", lambda t, k=k, bb=bb: t.matmul(
                        psEp[:, bb * H:(bb + 1) * H],
                        xaf[:, k * ASP + bb * P:k * ASP + (bb + 1) * P],
                        wbs[:, k * H:(k + 1) * H],
                        start=(k == 0), stop=(k == EXP - 1)),
                        w if k == 0 else [], inc=sT)
                tsig = pr.op("S", lambda s, bb=bb: s.activation(
                    betaT[:, bb * H:(bb + 1) * H], psEp[:, bb * H:(bb + 1) * H],
                    AF.Sigmoid), [tmb], inc=sS)
                tneglast = pr.op("V", lambda v, bb=bb: v.tensor_scalar(
                    negbT[:, bb * H:(bb + 1) * H],
                    betaT[:, bb * H:(bb + 1) * H], -1.0, None, ALU.mult),
                    [tsig], inc=sC)
                for half in range(2):
                    slot = nps[0] % 2
                    ps, off = projps[slot]
                    nps[0] += 1
                    tm = None
                    for k in range(EXP):
                        w = [tw["v"], txaf] + (pv[slot] if k == 0 else [])
                        tm = pr.op("T", lambda t, k=k, bb=bb, half=half, ps=ps,
                                   off=off: t.matmul(
                            ps[:, off:off + 512],
                            xaf[:, k * ASP + bb * P:k * ASP + (bb + 1) * P],
                            wvs[:, k * 1024 + half * 512:
                                k * 1024 + (half + 1) * 512],
                            start=(k == 0), stop=(k == EXP - 1)),
                            w if k == 0 else [], inc=sT)
                    for hh in range(2):
                        h = half * 2 + hh
                        tvev = pr.op("V", lambda v, bb=bb, h=h, hh=hh, ps=ps,
                                     off=off: v.tensor_scalar(
                            vbt[:, bb * 1024 + h * 256:bb * 1024 + (h + 1) * 256],
                            ps[:, off + hh * 256:off + (hh + 1) * 256],
                            betaT[:, bb * H + h:bb * H + h + 1], None,
                            ALU.mult), [tm, tsig], inc=sC)
                    pv[slot] = [tvev]

            tkng_new = [None] * CPA
            tDWfree = pv[0][0] if pv[0] else tDWfree
            tXfree = pv[1][0] if pv[1] else tXfree
            for cl in range(CPA):
                c = a * CPA + cl
                par = c % 2
                kfn = kfn2[par]
                sNB, sNBT, sX = sNB2[par], sNBT2[par], sX2[par]
                powA, powT = powA2[par], powT2[par]
                nbk, sW1c, sW2cT, smk = (nbk2[par], sW1c2[par], sW2cT2[par],
                                         smk2[par])
                tts = []
                for h in range(H):
                    w = [tkev]
                    if h == 0:
                        w += [tkng[cl], tNBT_prev, tknG[par]]
                    tts.append(pr.op("T", lambda t, h=h, cl=cl: t.transpose(
                        psT2[:, h * P:(h + 1) * P],
                        kfr[:, h * ASP + cl * P:h * ASP + (cl + 1) * P],
                        identb[:, :]), w if h == 0 else [], inc=sT))
                tka = None
                for h in range(H):
                    tka = pr.op("S", lambda s, h=h, cl=cl: s.activation(
                        lntrash[:, 0:P], psT2[:, h * P:(h + 1) * P], AF.Square,
                        accum_out=kss[:, cl * H + h:cl * H + h + 1]),
                        [tts[h]], inc=sS)
                sl = slice(cl * H, (cl + 1) * H)
                pr.op("V", lambda v, sl=sl: v.tensor_scalar(
                    krs[:, sl], kss[:, sl], 1.0, EPS_K, ALU.mult, ALU.add),
                    [tka], inc=sC)
                tk2 = pr.op("V", lambda v, sl=sl: v.reciprocal(
                    krs[:, sl], krs[:, sl]), inc=sC)
                tk3 = pr.op("S", lambda s, sl=sl: s.activation(
                    krs[:, sl], krs[:, sl], AF.Sqrt), [tk2], inc=sS)
                tktk = None
                for h in range(H):
                    tktk = pr.op("V", lambda v, h=h, cl=cl: v.tensor_scalar(
                        ktok[:, cl * 512 + h * P:cl * 512 + (h + 1) * P],
                        psT2[:, h * P:(h + 1) * P],
                        krs[:, cl * H + h:cl * H + h + 1], None, ALU.mult),
                        [tk3], inc=sC)
                tts2 = []
                for h in range(H):
                    tts2.append(pr.op("T", lambda t, h=h, cl=cl: t.transpose(
                        psT2[:, h * P:(h + 1) * P],
                        ktok[:, cl * 512 + h * P:cl * 512 + (h + 1) * P],
                        identb[:, :]), [tktk] if h == 0 else [], inc=sT))
                tkfn = pr.op("S", lambda s, kfn=kfn: s.copy(
                    kfn[:, :], psT2[:, 0:512]), [tts2[-1]], inc=sS)
                tknG[par] = tkfn
                tG = None
                for h in range(H):
                    w = [tkfn]
                    if h == 0:
                        w += [tneglast, tEfree]
                    tG = pr.op("T", lambda t, h=h, kfn=kfn: t.matmul(
                        psEp[:, h * P:(h + 1) * P], kfn[:, h * P:(h + 1) * P],
                        kfn[:, h * P:(h + 1) * P]), w if h == 0 else [], inc=sT)
                tNB = None
                for h in range(H):
                    tNB = pr.op("V", lambda v, h=h, cl=cl, sNB=sNB:
                                v.scalar_tensor_tensor(
                        sNB[:, h * P:(h + 1) * P], psEp[:, h * P:(h + 1) * P],
                        negbT[:, cl * H + h:cl * H + h + 1], slm[:, :],
                        ALU.mult, ALU.mult), [tG], inc=sC)
                tts3 = []
                for h in range(H):
                    tts3.append(pr.op("T", lambda t, h=h, sNB=sNB: t.transpose(
                        psT2[:, h * P:(h + 1) * P], sNB[:, h * P:(h + 1) * P],
                        identb[:, :]), [tNB, tkfn] if h == 0 else [], inc=sT))
                tNBT = pr.op("S", lambda s, sNBT=sNBT: s.copy(
                    sNBT[:, :], psT2[:, 0:512]), [tts3[-1]], inc=sS)
                tNBT_prev = tNBT
                tX = pr.op("V", lambda v, sX=sX, sNBT=sNBT: v.tensor_add(
                    sX[:, :], sNBT[:, :], id4[:, :]), [tNBT], inc=sC)
                pA, pAT = sNB, sNBT
                tpa = tNBT
                tpat = tNBT
                for j in range(1, 7):
                    tsq = None
                    for h in range(H):
                        w = [tpa, tpat]
                        if h == 0:
                            w.append(tEfree)
                        tsq = pr.op("T", lambda t, h=h, pA=pA, pAT=pAT: t.matmul(
                            psEp[:, h * P:(h + 1) * P],
                            pAT[:, h * P:(h + 1) * P],
                            pA[:, h * P:(h + 1) * P]), w if h == 0 else [],
                            inc=sT)
                    npA = powA[j % 2]
                    tpa_n = pr.op("S", lambda s, npA=npA: s.copy(
                        npA[:, :], psEp[:, 0:512]), [tsq], inc=sS)
                    tEfree = tpa_n
                    if j < 6:
                        tsqt = None
                        for h in range(H):
                            w = [tpa, tpat]
                            if h == 0:
                                w.append(tQfree)
                            tsqt = pr.op("T", lambda t, h=h, pA=pA, pAT=pAT:
                                         t.matmul(
                                psQT[:, h * P:(h + 1) * P],
                                pA[:, h * P:(h + 1) * P],
                                pAT[:, h * P:(h + 1) * P]),
                                w if h == 0 else [], inc=sT)
                        npAT = powT[j % 2]
                        tpat_n = pr.op("S", lambda s, npAT=npAT: s.copy(
                            npAT[:, :], psQT[:, 0:512]), [tsqt], inc=sS)
                        tQfree = tpat_n
                    txm = None
                    for h in range(H):
                        w = [tpa_n, tX]
                        if h == 0:
                            w.append(tXfree)
                        txm = pr.op("T", lambda t, h=h, sX=sX: t.matmul(
                            psX[:, h * P:(h + 1) * P], identb[:, :],
                            sX[:, h * P:(h + 1) * P], start=True, stop=False),
                            w if h == 0 else [], inc=sT)
                        txm = pr.op("T", lambda t, h=h, npA=npA, sX=sX: t.matmul(
                            psX[:, h * P:(h + 1) * P],
                            npA[:, h * P:(h + 1) * P],
                            sX[:, h * P:(h + 1) * P], start=False, stop=True),
                            [], inc=sT)
                    tX = pr.op("S", lambda s, sX=sX: s.copy(
                        sX[:, :], psX[:, 0:512]), [txm], inc=sS)
                    tXfree = tX
                    if j < 6:
                        pA, pAT = npA, npAT
                        tpa, tpat = tpa_n, tpat_n
                    else:
                        tpa = tpa_n
                tXc = tX
                tw1 = None
                for hh in range(2):
                    tw1m = None
                    for h in (2 * hh, 2 * hh + 1):
                        w = [tX, tvev]
                        if h == 2 * hh:
                            w.append(tXfree)
                        tw1m = pr.op("T", lambda t, h=h, hh=hh, cl=cl, sX=sX:
                                     t.matmul(
                            psX[:, (h % 2) * 256:(h % 2) * 256 + 256],
                            sX[:, h * P:(h + 1) * P],
                            vbt[:, cl * 1024 + h * 256:cl * 1024 + (h + 1) * 256]),
                            w if h == 2 * hh else [], inc=sT)
                    tw1 = pr.op("S", lambda s, hh=hh, sW1c=sW1c: s.copy(
                        sW1c[:, hh * 512:(hh + 1) * 512], psX[:, 0:512]),
                        [tw1m] + ([tw1] if tw1 else []), inc=sS)
                    tXfree = tw1
                tnbk = None
                for h in range(H):
                    tnbk = pr.op("V", lambda v, h=h, cl=cl, nbk=nbk:
                                 v.tensor_scalar(
                        nbk[:, h * P:(h + 1) * P],
                        ktok[:, cl * 512 + h * P:cl * 512 + (h + 1) * P],
                        negbT[:, cl * H + h:cl * H + h + 1], None, ALU.mult),
                        [tktk], inc=sC)
                tw2m = None
                for h in range(H):
                    w = [tnbk, tX, tpa, tEfree]
                    tw2m = pr.op("T", lambda t, h=h, nbk=nbk, sX=sX: t.matmul(
                        psEp[:, h * P:(h + 1) * P], nbk[:, h * P:(h + 1) * P],
                        sX[:, h * P:(h + 1) * P]), w if h == 0 else [], inc=sT)
                tw2 = pr.op("S", lambda s, sW2cT=sW2cT: s.copy(
                    sW2cT[:, :], psEp[:, 0:512]), [tw2m], inc=sS)
                tEfree = tw2
                tmkm = None
                for h in range(H):
                    w = [tw2, tkfn, tqev]
                    tmkm = pr.op("T", lambda t, h=h, cl=cl, kfn=kfn: t.matmul(
                        psEp[:, h * P:(h + 1) * P], kfn[:, h * P:(h + 1) * P],
                        qf[:, h * ASP + cl * P:h * ASP + (cl + 1) * P]),
                        w if h == 0 else [], inc=sT)
                tmkm_last = tmkm
                tmk = pr.op("V", lambda v, smk=smk: v.tensor_mul(
                    smk[:, :], psEp[:, 0:512], ut4[:, :]), [tmkm], inc=sC)
                tEfree = tmk
                if c > 0:
                    tS0c = pr.op("S", lambda s: s.copy(
                        sS0[:, :], sSf[:, :]), [tSadd, tS0c, tOev], inc=sS)
                tDc = None
                for hh in range(2):
                    tDm = None
                    for h in (2 * hh, 2 * hh + 1):
                        w = [tS0c, tw1, tw2]
                        if h == 2 * hh:
                            w.append(tDWfree)
                        tDm = pr.op("T", lambda t, h=h, sW1c=sW1c: t.matmul(
                            psDW[:, (h % 2) * 256:(h % 2) * 256 + 256],
                            identb[:, :],
                            sW1c[:, h * 256:(h + 1) * 256], start=True,
                            stop=False), w if h == 2 * hh else [], inc=sT)
                        tDm = pr.op("T", lambda t, h=h, sW2cT=sW2cT: t.matmul(
                            psDW[:, (h % 2) * 256:(h % 2) * 256 + 256],
                            sW2cT[:, h * P:(h + 1) * P],
                            sS0[:, h * 256:(h + 1) * 256], start=False,
                            stop=True), [], inc=sT)
                    tDh = pr.op("V", lambda v, hh=hh: v.tensor_copy(
                        sD[:, hh * 512:(hh + 1) * 512], psDW[:, 0:512]),
                        [tDm] + ([tDc] if tDc else []), inc=sC)
                    tDWfree = tDh
                    tDc = tDh
                for hh in range(2):
                    tSm = None
                    for h in (2 * hh, 2 * hh + 1):
                        w = [tDc, tktk]
                        if h == 2 * hh:
                            w.append(tSadd)
                        tSm = pr.op("T", lambda t, h=h, cl=cl: t.matmul(
                            psS[:, (h % 2) * 256:(h % 2) * 256 + 256],
                            ktok[:, cl * 512 + h * P:cl * 512 + (h + 1) * P],
                            sD[:, h * 256:(h + 1) * 256]),
                            w if h == 2 * hh else [], inc=sT)
                    tSadd = pr.op("V", lambda v, hh=hh: v.tensor_add(
                        sSf[:, hh * 512:(hh + 1) * 512],
                        sSf[:, hh * 512:(hh + 1) * 512], psS[:, 0:512]),
                        [tSm], inc=sC)
                tkng_new[cl] = tSadd
                tOm = None
                for h in range(H):
                    for f in range(2):
                        w = [tS0c, tDc, tmk, tqev]
                        if h == 0 and f == 0:
                            w += [t for t in (tOev, tYev[0], tYev[1], tWo)
                                  if t is not None]
                        tOm = pr.op("T", lambda t, h=h, f=f, cl=cl, smk=smk:
                                    t.matmul(
                            psOp[:, (2 * h + f) * P:(2 * h + f + 1) * P],
                            sS0[:, h * 256 + f * P:h * 256 + (f + 1) * P],
                            qf[:, h * ASP + cl * P:h * ASP + (cl + 1) * P],
                            start=True, stop=False), w if (h == 0 and f == 0)
                            else [], inc=sT)
                        tOm = pr.op("T", lambda t, h=h, f=f, smk=smk: t.matmul(
                            psOp[:, (2 * h + f) * P:(2 * h + f + 1) * P],
                            sD[:, h * 256 + f * P:h * 256 + (f + 1) * P],
                            smk[:, h * P:(h + 1) * P],
                            start=False, stop=True), [], inc=sT)
                for f in range(2):
                    tOev = pr.op("V", lambda v, f=f, cl=cl: v.tensor_copy(
                        oTs[f][:, cl * 512:(cl + 1) * 512]
                        .rearrange("p (c2 t) -> p c2 t", t=P),
                        psOp[:, :].rearrange("p (h2 s f2) -> p h2 s f2",
                                             s=2, f2=P)[:, :, f, :]),
                        [tOm], inc=sC)
            tkng = tkng_new
            tcs = None
            for f in range(2):
                tsq = pr.op("S", lambda s, f=f: s.activation(
                    sqb[:, :], oTs[f][:, :], AF.Square),
                    [tOev] + ([tcs] if tcs else []), inc=sS)
                for q4 in range(CPA):
                    r0 = 32 * (q4 // 2)
                    off = 512 * (q4 % 2)
                    w = [tsq]
                    if f == 0 and q4 == 0:
                        w += [tOev]
                    tcs = pr.op("T", lambda t, q4=q4, f=f, r0=r0, off=off:
                                t.matmul(
                        psOp[r0:r0 + 1, off:off + 512], oncb[:, 0:1],
                        sqb[:, q4 * 512:(q4 + 1) * 512],
                        start=(f == 0), stop=(f == 1)), w, inc=sT)
            tr1 = pr.op("S", lambda s: s.activation(
                rr1[0:1, 0:1024], psOp[0:1, :], AF.Sqrt,
                bias=epsT[0:1, 0:1], scale=scT[0:1, 0:1]),
                [tcs, teps, tsc_c], inc=sS)
            tr1b = pr.op("S", lambda s: s.activation(
                rr1[0:1, 1024:2048], psOp[32:33, :], AF.Sqrt,
                bias=epsT[0:1, 0:1], scale=scT[0:1, 0:1]), [tcs], inc=sS)
            tr2 = pr.op("V", lambda v: v.reciprocal(rr1[0:1, :], rr1[0:1, :]),
                        [tr1, tr1b], inc=sC)
            tr4 = pr.op("S", lambda s: s.copy(rrb[0:1, :], rr1[0:1, :]),
                        [tr2], inc=sS)
            tr1_p, tr1b_p = tr1, tr1b
            tbc = None
            for cl in range(CPA):
                tbm = None
                for h in range(H):
                    w = [tr4]
                    if h == 0:
                        w.append(tbc if tbc is not None else tcs)
                    tbm = pr.op("T", lambda t, cl=cl, h=h: t.matmul(
                        psOp[:, h * P:(h + 1) * P], on1b[:, :],
                        rrb[0:1, cl * 512 + h * P:cl * 512 + (h + 1) * P]),
                        w if h == 0 else [], inc=sT)
                tbc = pr.op("V", lambda v, cl=cl: v.tensor_copy(
                    rbc[:, cl * 512:(cl + 1) * 512], psOp[:, 0:512]),
                    [tbm], inc=sC)
                tbc_p = tbc
            tognew = None
            togs = [None, None]
            for ch2 in range(2 * H):
                h, f = ch2 // 2, ch2 % 2
                i = ch2 % 2
                w = [tgev, tbc]
                if togs[i] is not None:
                    w.append(togs[i])
                tsg = pr.op("S", lambda s, ch2=ch2, i=i: s.activation(
                    sgt[i][:, :], gf[:, ch2 * ASP:(ch2 + 1) * ASP], AF.Silu),
                    w, inc=sS)
                tsg2 = pr.op("V", lambda v, h=h, i=i: v.tensor_mul(
                    sgt[i][:, :].rearrange("p (c t) -> p c t", t=P),
                    sgt[i][:, :].rearrange("p (c t) -> p c t", t=P),
                    rbc[:, :].rearrange("p (c hh t) -> p c hh t", hh=H, t=P)
                    [:, :, h, :]), [tsg, tbc], inc=sC)
                tognew = pr.op("V", lambda v, h=h, f=f, i=i:
                               v.scalar_tensor_tensor(
                    oTs[f][:, :].rearrange("p (c hh t) -> p c hh t", hh=H, t=P)
                    [:, :, h, :],
                    oTs[f][:, :].rearrange("p (c hh t) -> p c hh t", hh=H, t=P)
                    [:, :, h, :],
                    nwsb[:, f:f + 1],
                    sgt[i][:, :].rearrange("p (c t) -> p c t", t=P),
                    ALU.mult, ALU.mult), [tsg2, tOev], inc=sC)
                togs[i] = tognew
            tgsil = tognew
            for cl in range(CPA):
                j = 4 * a + cl
                i = cl % 2
                wv = [tYev[i]] if tYev[i] else []
                if src_tok is not None:
                    wv.append(src_tok[j])
                tRs[i] = pr.dma("Y", rs2[i][:, :], src[j * P:(j + 1) * P, :],
                                waits=wv + [tognew], sem=f"ars{i}")
                tWo = None
                for half in range(2):
                    for ch2 in range(2 * H):
                        h, f = ch2 // 2, ch2 % 2
                        w2 = [tognew, tw["o"]]
                        if ch2 == 0 and half == 0:
                            w2 += [t for t in (tYev[0], tYev[1], tbc, tr1b)
                                   if t is not None]
                        tWo = pr.op("T", lambda t, cl=cl, half=half, h=h, f=f,
                                    ch2=ch2: t.matmul(
                            psOp[:, half * 512:(half + 1) * 512],
                            oTs[f][:, cl * 512 + h * P:cl * 512 + (h + 1) * P],
                            wos[:, ch2 * 1024 + half * 512:
                                ch2 * 1024 + (half + 1) * 512],
                            start=(ch2 == 0), stop=(ch2 == 2 * H - 1)),
                            w2 if (ch2 == 0 and half == 0) else [], inc=sT)
                tye = None
                for half in range(2):
                    w2 = [tWo, tRs[i]]
                    if tYw[i] is not None:
                        w2.append(tYw[i])
                    tye = pr.op("V", lambda v, i=i, half=half: v.tensor_add(
                        yout[i][:, half * 512:(half + 1) * 512],
                        rs2[i][:, half * 512:(half + 1) * 512],
                        psOp[:, half * 512:(half + 1) * 512]), w2, inc=sC)
                tYev[i] = tye
                tYw[i] = pr.dma("Y", dst[j * P:(j + 1) * P, :], yout[i][:, :],
                                waits=[tye], sem=f"ayw{i}")
                touts.append(tYw[i])
        aes.close()
        return touts
    if mode == "ffn1":
        touts, _ = ffn_phase(x_in, out_t, "A", tcon)
    elif mode == "attn":
        t1, tp1 = ffn_phase(x_in, resid1, "A", tcon)
        touts = attn_phase(resid1, out_t, pr.fence(), src_tok=t1, ptr_tok=tp1)
    elif mode == "full":
        t1, tp1 = ffn_phase(x_in, resid1, "A", tcon)
        t2 = attn_phase(resid1, resid2, pr.fence(), src_tok=t1, ptr_tok=tp1)
        touts, _ = ffn_phase(resid2, out_t, "B", pr.fence(), src_tok=t2)
    elif mode.startswith("m"):
        touts, _ = ffn_phase(x_in, out_t, "A", tcon, stop=int(mode[1:]))
    else:
        raise NotImplementedError(mode)

    pr.op("Y", lambda e: None, waits=pr.fence())
    pr.emit()
    es_top.close()
    return nc


def _consts(core):
    ident = np.eye(P).astype(ml_dtypes.bfloat16)
    slmask = np.tril(np.ones((P, P)), -1).astype(ml_dtypes.bfloat16)
    utmask = np.triu(np.ones((P, P)), 0).astype(ml_dtypes.bfloat16)
    ones8f = np.ones((EXP, 1), np.float32)
    ones8b = np.ones((EXP, 1)).astype(ml_dtypes.bfloat16)
    ones1b = np.ones((1, P)).astype(ml_dtypes.bfloat16)
    onescb = np.ones((P, 1)).astype(ml_dtypes.bfloat16)
    ident4 = np.tile(np.eye(P), (1, 4)).astype(ml_dtypes.bfloat16)
    ut4 = np.tile(np.triu(np.ones((P, P)), 0), (1, 4)).astype(ml_dtypes.bfloat16)
    pert8 = (np.arange(EXP).reshape(EXP, 1) * PERT).astype(np.float32)
    eoff8 = (np.arange(EXP).reshape(EXP, 1) * float(CAP // 4)).astype(np.float32)
    selmat = np.zeros((NCORE * EXP, EXP), np.float32)
    for c in range(core):
        for e in range(EXP):
            selmat[8 * c + e, e] = 1.0
    lselmat = np.zeros((NCORE * EXP, NCORE), np.float32)
    for cdst in range(NCORE):
        for csrc in range(cdst):
            lselmat[8 * csrc + core, cdst] = 1.0
    selcol = np.zeros((EXP, P), np.float32)
    for pp in range(P):
        selcol[pp // 16, pp] = 1.0
    pp = np.arange(P).reshape(P, 1)
    tokid = ((pp % 16 // 4) * (NCORE * 1024) + pp // 16 * 1024
             + (pp % 4) * 256
             + np.arange(256).reshape(1, 256)).astype(np.int16)
    coreoff = np.array([[core * NTOK]], np.int32)
    return dict(ident=ident, slmask=slmask, utmask=utmask, ones8f=ones8f,
                ones8b=ones8b, ones1b=ones1b, onescb=onescb, ident4=ident4,
                ut4c=ut4, pert8=pert8, eoff8=eoff8,
                selmat=selmat, lselmat=lselmat, selcol=selcol, tokid=tokid,
                coreoff=coreoff)


_BUILD_CACHE = {}


def kernel(L0_attn_x, L0_attn_Wq, L0_attn_Wk, L0_attn_Wv, L0_attn_Wb,
           L0_attn_Wg, L0_attn_Wo, L0_attn_norm_w, L0_ffn_Wr, L0_ffn_W1,
           L0_ffn_W2, _mode="full"):
    if _mode not in _BUILD_CACHE:
        ncb = build(_mode)
        mybir.codegen_inst_isa_subclasses(ncb)
        _BUILD_CACHE[_mode] = ncb
    ncb = _BUILD_CACHE[_mode]
    x = np.ascontiguousarray(np.asarray(L0_attn_x, np.float32))
    w = {k: np.ascontiguousarray(np.asarray(v, np.float32)) for k, v in dict(
        Wq=L0_attn_Wq, Wk=L0_attn_Wk, Wv=L0_attn_Wv, Wb=L0_attn_Wb,
        Wg=L0_attn_Wg, Wo=L0_attn_Wo, NW=L0_attn_norm_w, Wr=L0_ffn_Wr).items()}
    W1 = np.ascontiguousarray(np.asarray(L0_ffn_W1, np.float32))
    W2 = np.ascontiguousarray(np.asarray(L0_ffn_W2, np.float32))
    in_maps = []
    for c in range(NCORE):
        m = dict(x=x[c], W1e=W1[c], W2e=W2[c], **w, **_consts(c))
        in_maps.append(m)
    res = run_bass_kernel_spmd(ncb, in_maps, core_ids=list(range(NCORE)))
    return np.stack([np.asarray(res.results[c]["out"]) for c in range(NCORE)], 0)


# version 14
# speedup vs baseline: 1.0173x; 1.0082x over previous
import numpy as np
import ml_dtypes
from contextlib import ExitStack

import concourse.bass as bass
import concourse.mybir as mybir
import bass_rust
import concourse.library_config as libcfg
from concourse.bass import IndirectOffsetOnAxis
from concourse.bass_utils import run_bass_kernel_spmd

F32 = mybir.dt.float32
BF16 = mybir.dt.bfloat16
I16 = mybir.dt.int16
I32 = mybir.dt.int32
U8 = mybir.dt.uint8
ALU = mybir.AluOpType
AF = mybir.ActivationFunctionType
AX = mybir.AxisListType.X

P = 128
DM = 1024
NTOK = 4096
NBLK = 32
EXP = 8
CAP = 5120
DFF = 4096
H = 4
DK = 128
DV = 256
NCHUNK = 32
NSPAN = 4
SSPAN = 256
NSSPAN = CAP // SSPAN
NCORE = 8
EPS_LN = 1e-5
EPS_ON = 1e-5
EPS_K = 1e-12
SCALE_Q = float(DK) ** -0.5
PAD_NEG = -3.0e38
PERT = 1e-6


class Prog:

    def __init__(self, nc, es):
        self.nc = nc
        self.es = es
        self.ops = {e: [] for e in "TVSGY"}
        self.sems = {}
        self.cnt = {}
        self.last = {}

    def sem(self, name):
        if name not in self.sems:
            self.sems[name] = self.es.enter_context(self.nc.semaphore(name))
            self.cnt[name] = 0
        return self.sems[name]

    BULK = ("q_y", "q_g", "q_s")

    def op(self, eng, fn, waits=(), inc=None, amt=1):
        ws = []
        for w in waits:
            if w is None:
                continue
            s, v = w
            if s in self.BULK:
                v = self.cnt.get(s, 0)
            ws.append((s, v))
        if eng in "VSG" and amt == 1 and self.last.get(eng) is not None:
            ws.append(self.last[eng])
        tok = None
        if inc is not None:
            self.sem(inc)
            self.cnt[inc] += amt
            tok = (inc, self.cnt[inc])
            if eng in "VSG" and amt == 1:
                self.last[eng] = tok
        self.ops[eng].append((fn, ws, inc, amt))
        return tok

    def dma(self, eng, out, in_, waits=(), sem=None, **kw):
        q = sem or {"Y": "q_y", "G": "q_g", "S": "q_s"}[eng]
        return self.op(eng, lambda e: e.dma_start(out=out, in_=in_, **kw),
                       waits, inc=q, amt=16)

    def fence(self):
        return [(s, self.cnt[s]) for s in self.cnt if self.cnt[s] > 0]

    def emit(self):
        nc = self.nc
        with nc.Block() as blk:
            decos = {"T": blk.tensor, "V": blk.vector, "S": blk.scalar,
                     "G": blk.gpsimd, "Y": blk.sync}
            for eng in "TVSGY":
                if not self.ops[eng]:
                    continue
                ops = self.ops[eng]

                def run(e, ops=ops):
                    seen = {}
                    for fn, ws, inc, amt in ops:
                        for (s, v) in ws:
                            if seen.get(s, -1) < v:
                                e.wait_ge(self.sems[s], v)
                                seen[s] = v
                        inst = fn(e)
                        if inc is not None:
                            inst.then_inc(self.sems[inc], amt)
                decos[eng](run)


def build(mode="full"):
    nc = bass.Bass("TRN2", target_bir_lowering=False)

    x_in = nc.dram_tensor("x", [NTOK, DM], F32, kind="ExternalInput")
    Wqd = nc.dram_tensor("Wq", [DM, H * DK], F32, kind="ExternalInput")
    Wkd = nc.dram_tensor("Wk", [DM, H * DK], F32, kind="ExternalInput")
    Wvd = nc.dram_tensor("Wv", [DM, H * DV], F32, kind="ExternalInput")
    Wbd = nc.dram_tensor("Wb", [DM, H], F32, kind="ExternalInput")
    Wgd = nc.dram_tensor("Wg", [DM, H * DV], F32, kind="ExternalInput")
    Wod = nc.dram_tensor("Wo", [H * DV, DM], F32, kind="ExternalInput")
    NWd = nc.dram_tensor("NW", [DV], F32, kind="ExternalInput")
    Wrd = nc.dram_tensor("Wr", [DM, EXP], F32, kind="ExternalInput")
    W1d = nc.dram_tensor("W1e", [DM, DFF], F32, kind="ExternalInput")
    W2d = nc.dram_tensor("W2e", [DFF, DM], F32, kind="ExternalInput")
    identd = nc.dram_tensor("ident", [P, P], BF16, kind="ExternalInput")
    slmaskd = nc.dram_tensor("slmask", [P, P], BF16, kind="ExternalInput")
    utmaskd = nc.dram_tensor("utmask", [P, P], BF16, kind="ExternalInput")
    ones8fd = nc.dram_tensor("ones8f", [EXP, 1], F32, kind="ExternalInput")
    ones8bd = nc.dram_tensor("ones8b", [EXP, 1], BF16, kind="ExternalInput")
    ones1bd = nc.dram_tensor("ones1b", [1, P], BF16, kind="ExternalInput")
    onescbd = nc.dram_tensor("onescb", [P, 1], BF16, kind="ExternalInput")
    pert8d = nc.dram_tensor("pert8", [EXP, 1], F32, kind="ExternalInput")
    eoff8d = nc.dram_tensor("eoff8", [EXP, 1], F32, kind="ExternalInput")
    selmatd = nc.dram_tensor("selmat", [NCORE * EXP, EXP], F32, kind="ExternalInput")
    lselmatd = nc.dram_tensor("lselmat", [NCORE * EXP, NCORE], F32, kind="ExternalInput")
    selcold = nc.dram_tensor("selcol", [EXP, P], F32, kind="ExternalInput")
    tokidd = nc.dram_tensor("tokid", [P, 256], I16, kind="ExternalInput")
    ident4d = nc.dram_tensor("ident4", [P, 512], BF16, kind="ExternalInput")
    ut4d = nc.dram_tensor("ut4c", [P, 512], BF16, kind="ExternalInput")
    coreoffd = nc.dram_tensor("coreoff", [1, 1], I32, kind="ExternalInput")
    out_t = nc.dram_tensor("out", [NTOK, DM], F32, kind="ExternalOutput")

    resid1 = nc.dram_tensor("resid1", [NTOK, DM], F32)
    resid2 = nc.dram_tensor("resid2", [NTOK, DM], F32)
    xln_in = nc.dram_tensor("xln_in", [NTOK, DM], BF16)
    xln_all = nc.dram_tensor("xln_all", [NCORE * NTOK, DM], BF16, addr_space="Shared")
    spm_in = nc.dram_tensor("spm_in", [EXP, NTOK], F32)
    spm_all = nc.dram_tensor("spm_all", [NCORE * EXP, NTOK], F32, addr_space="Shared")
    eout_in = nc.dram_tensor("eout_in", [CAP, DM], BF16)
    QCAP = CAP // 4
    eout_all = nc.dram_tensor("eout_all", [4 * NCORE * QCAP, DM], BF16, addr_space="Shared")
    tblrow = nc.dram_tensor("tblrow", [1, 5376], I16)
    tblD = nc.dram_tensor("tblD", [P, 5376], I16)
    pinvrow = nc.dram_tensor("pinvrow", [1, NTOK], F32)
    RG = [list(range(NCORE))]

    es_top = ExitStack()
    es = es_top.enter_context
    pr = Prog(nc, es_top)
    greg = es(nc.gpsimd.register("greg"))

    identb = es(nc.sbuf_tensor("identb", [P, P], BF16))
    slm = es(nc.sbuf_tensor("slm", [P, P], BF16))
    utm = es(nc.sbuf_tensor("utm", [P, P], BF16))
    on8f = es(nc.sbuf_tensor("on8f", [EXP, 1], F32))
    on8b = es(nc.sbuf_tensor("on8b", [EXP, 1], BF16))
    on1b = es(nc.sbuf_tensor("on1b", [1, P], BF16))
    oncb = es(nc.sbuf_tensor("oncb", [P, 1], BF16))
    p8 = es(nc.sbuf_tensor("p8", [EXP, 1], F32))
    e8 = es(nc.sbuf_tensor("e8", [EXP, 1], F32))
    selm = es(nc.sbuf_tensor("selm", [NCORE * EXP, EXP], F32))
    lselm = es(nc.sbuf_tensor("lselm", [NCORE * EXP, NCORE], F32))
    selc = es(nc.sbuf_tensor("selc", [EXP, P], F32))
    tok16 = es(nc.sbuf_tensor("tok16", [P, 256], I16))
    id4 = es(nc.sbuf_tensor("id4", [P, 512], BF16))
    ut4 = es(nc.sbuf_tensor("ut4", [P, 512], BF16))
    wrsb = es(nc.sbuf_tensor("wrsb", [P, EXP * EXP], BF16))
    nwsb = es(nc.sbuf_tensor("nwsb", [P, 2], F32))
    co32 = es(nc.sbuf_tensor("co32", [1, 1], I32))
    flat32 = es(nc.sbuf_tensor("flat32", [P, NBLK], I32))
    af_t = es(nc.sbuf_tensor("af_t", [P, NBLK], F32))
    bf_t = es(nc.sbuf_tensor("bf_t", [P, NBLK], F32))
    msum = es(nc.sbuf_tensor("msum", [P, NBLK], F32))
    sqsum = es(nc.sbuf_tensor("sqsum", [P, NBLK], F32))
    mval_t = es(nc.sbuf_tensor("mval_t", [P, NBLK], F32))
    var_t = es(nc.sbuf_tensor("var_t", [P, NBLK], F32))
    rcp_t = es(nc.sbuf_tensor("rcp_t", [P, NBLK], F32))
    rstd_t = es(nc.sbuf_tensor("rstd_t", [P, NBLK], F32))
    nm_t = es(nc.sbuf_tensor("nm_t", [P, NBLK], F32))
    lntrash = es(nc.sbuf_tensor("lntrash", [P, DM], BF16))
    xringT = [es(nc.sbuf_tensor(f"xringT{i}", [P, DM], F32)) for i in range(2)]
    xlnbT = [es(nc.sbuf_tensor(f"xlnbT{i}", [P, DM], BF16)) for i in range(2)]
    xafT = es(nc.sbuf_tensor("xafT", [P, EXP * 512], BF16))
    psTRg = es(nc.psum_tensor("psTRg", [P, 1024], BF16))
    tblT32 = es(nc.sbuf_tensor("tblT32", [P, CAP // P], I32))

    tcon = []
    for dst, src in [(identb, identd), (slm, slmaskd), (utm, utmaskd),
                     (on8f, ones8fd), (on8b, ones8bd), (on1b, ones1bd),
                     (oncb, onescbd),
                     (p8, pert8d), (e8, eoff8d), (selm, selmatd),
                     (lselm, lselmatd), (selc, selcold), (tok16, tokidd),
                     (co32, coreoffd), (id4, ident4d), (ut4, ut4d)]:
        tcon.append(pr.dma("Y", dst[:, :], src[:, :]))
    t_wr = pr.dma("G", wrsb[:, :].rearrange("p (k c) -> p k c", c=EXP),
                  bass.AP(Wrd, 0, [[EXP, P], [P * EXP, EXP], [1, EXP]]),
                  sem="s_wr")
    tcon.append(t_wr)
    t_nw = pr.dma("Y", nwsb[:, :], bass.AP(NWd, 0, [[1, P], [P, 2]]),
                  allow_slow_non_contiguous=True)
    tcon.append(t_nw)
    import os as _os
    if _os.environ.get("SKIPLIB0"):
        t_lib0 = None
    else:
        t_lib0 = pr.op("G", lambda g: (g.load_library(libcfg.attn),
                                       g.nop())[1],
                       pr.fence(), inc="slib")
        tcon.append(t_lib0)
    tok16_tok = tcon[11]
    co_tok = tcon[12]

    def emit_ln(src, deps, sink, xring, xlnb, sC, sS, blk0=0, nblk=NBLK,
                ring_tok=None, blk_deps=None):
        GRP = 2
        toks = [None] * (blk0 + nblk)
        t_s = [None] * (blk0 + nblk)
        t_x = [None] * (blk0 + nblk)
        t_v = [None] * (blk0 + nblk)
        for g in range(blk0 // GRP, (blk0 + nblk) // GRP):
            for j in range(GRP * g, GRP * g + GRP):
                rb = xring[j % GRP]
                w = list(deps) if blk_deps is None else list(blk_deps[j])
                if j - GRP >= blk0:
                    w += [toks[j - GRP], t_s[j - GRP]]
                elif ring_tok is not None and ring_tok[j % GRP] is not None:
                    w += [ring_tok[j % GRP]]
                t_x[j] = pr.dma("Y", rb[:, :], src[j * P:(j + 1) * P, :],
                                waits=w, sem=f"lnx{j % GRP}")
                t_v[j] = pr.op("V", lambda v, j=j, rb=rb: v.reduce_sum(
                    msum[:, j:j + 1], rb[:, :], AX), [t_x[j]], inc=sC)
                t_s[j] = pr.op("S", lambda s, j=j, rb=rb: s.activation(
                    lntrash[:, :], rb[:, :], AF.Square,
                    accum_out=sqsum[:, j:j + 1]),
                    [t_x[j], t_s[j - 1] if j else None], inc=sS)
            j = GRP * g + GRP - 1
            sl = slice(j - GRP + 1, j + 1)
            pr.op("V", lambda v, sl=sl: v.tensor_scalar(
                mval_t[:, sl], msum[:, sl], 1.0 / DM, None, ALU.mult),
                [t_v[j]], inc=sC)
            pr.op("V", lambda v, sl=sl: v.tensor_scalar(
                var_t[:, sl], sqsum[:, sl], 1.0 / DM, EPS_LN, ALU.mult,
                ALU.add), [t_s[j]], inc=sC)
            pr.op("V", lambda v, sl=sl: v.tensor_mul(
                rcp_t[:, sl], mval_t[:, sl], mval_t[:, sl]), inc=sC)
            pr.op("V", lambda v, sl=sl: v.tensor_sub(
                var_t[:, sl], var_t[:, sl], rcp_t[:, sl]), inc=sC)
            tv2 = pr.op("V", lambda v, sl=sl: v.reciprocal(
                rcp_t[:, sl], var_t[:, sl]), inc=sC)
            tq = pr.op("S", lambda s, sl=sl: s.activation(
                rstd_t[:, sl], rcp_t[:, sl], AF.Sqrt), [tv2], inc=sS)
            pr.op("V", lambda v, sl=sl: v.tensor_mul(
                nm_t[:, sl], mval_t[:, sl], rstd_t[:, sl]), [tq], inc=sC)
            tgp = pr.op("V", lambda v, sl=sl: v.tensor_scalar(
                nm_t[:, sl], nm_t[:, sl], -1.0, None, ALU.mult), inc=sC)
            for jj in range(j - GRP + 1, j + 1):
                xb = xlnb[jj % 2]
                w2 = [tgp, t_x[jj]]
                if jj - 2 >= blk0 and toks[jj - 2] is not None:
                    w2.append(toks[jj - 2])
                if jj % 2 == 1:
                    tx = pr.op("S", lambda s, jj=jj, xb=xb: s.activation(
                        xb[:, :], xring[jj % GRP][:, :], AF.Identity,
                        scale=rstd_t[:, jj:jj + 1],
                        bias=nm_t[:, jj:jj + 1]), w2, inc=sS)
                else:
                    tx = pr.op("V", lambda v, jj=jj, xb=xb: v.tensor_scalar(
                        xb[:, :], xring[jj % GRP][:, :], rstd_t[:, jj:jj + 1],
                        nm_t[:, jj:jj + 1], ALU.mult, ALU.add), w2, inc=sC)
                toks[jj] = sink(jj, xb, tx)
        return toks, t_s

    def ffn_phase(src, dst, tag, deps, stop=99, src_tok=None):
        sC, sS, sT, sG, cc = f"v{tag}", f"s{tag}", f"t{tag}", f"g{tag}", "cc"

        def dummy_out():
            tz = pr.op("V", lambda v: v.memset(mval_t[:, :], 0.0), pr.fence(),
                       inc=sC)
            touts = []
            for j in range(NBLK):
                touts.append(pr.dma("Y", dst[j * P:(j + 1) * P, 0:NBLK],
                                    mval_t[:, :], waits=[tz]))
            fes.close()
            return touts, None
        fes = ExitStack()
        fe = fes.enter_context
        w1sb = fe(nc.sbuf_tensor(f"w1sb{tag}", [P, EXP * DFF], BF16))
        xring, xlnb = xringT, xlnbT
        maskA = fe(nc.sbuf_tensor(f"maskA{tag}", [EXP, NTOK], BF16))
        posA = fe(nc.sbuf_tensor(f"posA{tag}", [EXP, NTOK], I16))
        carry8 = fe(nc.sbuf_tensor(f"carry8{tag}", [EXP, 1], F32))
        pinv1 = fe(nc.sbuf_tensor(f"pinv1{tag}", [1, 1024], F32))
        cnt64 = fe(nc.sbuf_tensor(f"cnt64{tag}", [NCORE * EXP, 1], F32))
        ctmp = fe(nc.sbuf_tensor(f"ctmp{tag}", [NCORE * EXP, 1], F32))
        o8pf = fe(nc.sbuf_tensor(f"o8pf{tag}", [EXP, 2], F32))
        o8m1 = fe(nc.sbuf_tensor(f"o8m1{tag}", [EXP, 1], F32))
        kept_f = fe(nc.sbuf_tensor(f"keptf{tag}", [P, NBLK], F32))
        ptok = fe(nc.sbuf_tensor(f"ptok{tag}", [P, NBLK], F32))
        prefm1 = fe(nc.sbuf_tensor(f"prefm1{tag}", [P, 1], F32))
        psL = fe(nc.psum_tensor(f"psL{tag}", [P, 512], F32))
        psTR = psTRg
        psW1 = [fe(nc.psum_tensor(f"psW1{tag}{i}", [P, 512], F32)) for i in range(2)]
        psW2 = [fe(nc.psum_tensor(f"psW2{tag}{i}", [P, 512], F32)) for i in range(2)]

        libw = []
        if tag != "A":
            libw = [pr.op("G", lambda g: (g.load_library(libcfg.attn),
                                          g.nop())[1],
                          pr.fence(), inc=sG)]
        tW1 = pr.dma("G", w1sb[:, :].rearrange("p (k c) -> p k c", c=DFF),
                     bass.AP(W1d, 0, [[DFF, P], [P * DFF, EXP], [1, DFF]]),
                     waits=deps + libw, sem=f"sw1{tag}")
        pr.op("V", lambda v: v.memset(carry8[:, :], 0.0), deps, inc=sC)
        pr.op("V", lambda v: v.memset(cnt64[:, :], 0.0), inc=sC)

        def ln_sink(j, xb, tx):
            return pr.dma("Y", xln_in[j * P:(j + 1) * P, :], xb[:, :],
                          waits=[tx], sem=f"xw{tag}{j % 2}")
        bdeps = None if src_tok is None else [[src_tok[j]] for j in range(NBLK)]
        t_xlnw, _lns = emit_ln(src, deps, ln_sink, xring, xlnb, sC, sS,
                               blk_deps=bdeps)

        if stop <= 1:
            return dummy_out()
        if stop == 15:
            pass
        tAG1 = None
        if stop == 15:
            return dummy_out()

        res = ExitStack()
        re_ = res.enter_context
        xlf = re_(nc.sbuf_tensor(f"xlf{tag}", [P, EXP * 1024], BF16))
        l16 = re_(nc.sbuf_tensor(f"l16{tag}", [2 * EXP, 1024], F32))
        e16 = re_(nc.sbuf_tensor(f"e16{tag}", [2 * EXP, 1024], F32))
        ar16 = re_(nc.sbuf_tensor(f"ar16{tag}", [2 * EXP, 1024], F32))
        cbuf = re_(nc.sbuf_tensor(f"cbuf{tag}", [NCORE * EXP, 1024], F32))
        pr.op("V", lambda v: v.memset(l16[:, :], PAD_NEG), deps, inc=sC)
        pr.op("V", lambda v: v.memset(e16[:, :], 0.0), inc=sC)
        tglib = t_lib0 if tag == "A" else libw[0]
        if stop == 25:
            res.close()
            return dummy_out()
        t_spm = []
        t_pinv = []
        tcp = None
        tex_prev = None
        tpv_prev = None
        tspm_prev = None
        xblk = [re_(nc.sbuf_tensor(f"xblk{tag}{i}", [P, DM], BF16))
                for i in range(2)]
        t_xbe = [None, None]
        tAG1s = []
        for sp in range(NSPAN):
            cols = slice(sp * 1024, (sp + 1) * 1024)
            tAG1s.append(pr.op("G", lambda g, sp=sp: g.collective_compute(
                "AllGather", ALU.bypass, replica_groups=RG,
                ins=[xln_in[sp * 1024:(sp + 1) * 1024, :].opt()],
                outs=[xln_all[sp * NCORE * 1024:
                              (sp + 1) * NCORE * 1024, :].opt()]),
                waits=t_xlnw[sp * 8:(sp + 1) * 8], inc=f"cA{tag}"))
            ttp = []
            for b in range(8):
                i = b % 2
                w = [t_xlnw[sp * 8 + b]] + list(deps)
                if t_xbe[i] is not None:
                    w.append(t_xbe[i])
                if sp > 0 and b < 2:
                    w.append(tcp)
                txb = pr.dma("Y", xblk[i][:, :],
                             xln_in[(sp * 8 + b) * P:(sp * 8 + b + 1) * P, :],
                             waits=w, sem=f"sxb{tag}{i}")
                if stop in (21, 26):
                    t_xbe[i] = txb
                    ttp.append(txb)
                    continue
                ttr = None
                for c in range(EXP):
                    wtr = [txb]
                    if c == 0 and ttp:
                        wtr.append(ttp[-1])
                    ttr = pr.op("T", lambda t, i=i, c=c: t.transpose(
                        psTR[:, c * P:(c + 1) * P],
                        xblk[i][:, c * P:(c + 1) * P], identb[:, :]),
                        wtr, inc=sT)
                if stop == 22:
                    tev = pr.op("V", lambda v: v.tensor_copy(
                        lntrash[:, 0:P], psTR[:, 0:P]), [ttr], inc=sC)
                else:
                    tev = pr.op("V", lambda v, b=b: v.tensor_copy(
                        xlf[:, :].rearrange("p (c f) -> p c f", f=1024)
                        [:, :, b * P:(b + 1) * P],
                        psTR[:, :].rearrange("p (c f) -> p c f", f=P)),
                        [ttr], inc=sC)
                t_xbe[i] = tev
                ttp.append(tev)
            if stop in (16, 21, 22, 26):
                tcp = ttp[-1]
                continue
            for half in range(2):
                tm = None
                for k in range(EXP):
                    tm = pr.op("T", lambda t, k=k, half=half: t.matmul(
                        psL[0:EXP, 0:512],
                        wrsb[:, k * EXP:(k + 1) * EXP],
                        xlf[:, k * 1024 + half * 512:k * 1024 + (half + 1) * 512],
                        start=(k == 0), stop=(k == EXP - 1)),
                        [t_wr, ttp[-1]] + ([tcp] if tcp else []), inc=sT)
                tcp = pr.op("V", lambda v, half=half: v.tensor_copy(
                    l16[0:EXP, half * 512:(half + 1) * 512], psL[0:EXP, 0:512]),
                    [tm] + ([tex_prev] if (half == 0 and tex_prev) else []),
                    inc=sC)
            if stop == 17:
                continue
            tpe = pr.op("V", lambda v: v.tensor_scalar(
                l16[0:EXP, :], l16[0:EXP, :], p8[:, 0:1], None, ALU.subtract),
                [tcp], inc=sC)
            tmx = pr.op("G", lambda g: g.partition_all_reduce(
                ar16[:, :], l16[:, :], 2 * EXP, bass_rust.ReduceOp.max),
                [tpe, tglib] + ([tpv_prev] if tpv_prev else []), inc=sG)
            tsh = pr.op("V", lambda v: v.tensor_sub(
                l16[0:EXP, :], l16[0:EXP, :], ar16[0:EXP, :]), [tmx], inc=sC)
            tmk = pr.op("V", lambda v, cols=cols: v.tensor_single_scalar(
                maskA[:, cols], l16[0:EXP, :], 0.0, ALU.is_equal), [tsh], inc=sC)
            if stop == 18:
                continue
            tex = pr.op("S", lambda s: s.activation(
                e16[0:EXP, :], l16[0:EXP, :], AF.Exp),
                [tsh] + ([tspm_prev] if tspm_prev else []), inc=sS)
            tex_prev = tex
            tse = pr.op("G", lambda g: g.partition_all_reduce(
                ar16[:, :], e16[:, :], 2 * EXP, bass_rust.ReduceOp.add),
                [tex], inc=sG)
            tpv = pr.op("V", lambda v: v.reciprocal(pinv1[:, :], ar16[0:1, :]),
                        [tse] + (t_pinv[-1:] if t_pinv else []), inc=sC)
            tpv_prev = tpv
            t_pinv.append(pr.dma("Y", pinvrow[0:1, sp * 1024:(sp + 1) * 1024],
                                 pinv1[:, :], waits=[tpv], sem=f"spv{tag}"))
            if stop == 19:
                continue
            tsc = pr.op("V", lambda v, cols=cols: v.tensor_tensor_scan(
                posA[:, cols], maskA[:, cols], maskA[:, cols], carry8[:, 0:1],
                ALU.add, ALU.max), [tmk], inc=sC)
            pr.op("V", lambda v, sp=sp: v.tensor_copy(
                carry8[:, :], posA[:, (sp + 1) * 1024 - 1:(sp + 1) * 1024]),
                [tsc], inc=sC)
            tsp = pr.op("V", lambda v, cols=cols: v.tensor_mul(
                e16[0:EXP, :], posA[:, cols], maskA[:, cols]),
                [tsc, tse, tpv], inc=sC)
            tspm_prev = pr.dma("Y", spm_in[:, sp * 1024:(sp + 1) * 1024],
                               e16[0:EXP, :], waits=[tsp], sem=f"ssp{tag}")
            t_spm.append(tspm_prev)

        if stop <= 2 or stop in (16, 17, 18, 19, 21, 22, 26):
            res.close()
            return dummy_out()
        tAG2 = pr.op("G", lambda g: g.collective_compute(
            "AllGather", ALU.bypass, replica_groups=RG,
            ins=[spm_in.ap().opt()], outs=[spm_all.ap().opt()]),
            waits=t_spm, inc=f"cB{tag}")

        tcn = None
        for q in range(4):
            tq = pr.dma("Y", cbuf[:, :], spm_all[:, q * 1024:(q + 1) * 1024],
                        waits=[tAG2] + ([tcn] if tcn else []), sem=f"scb{tag}")
            tr = pr.op("V", lambda v: v.reduce_max(ctmp[:, :], cbuf[:, :], AX),
                       [tq], inc=sC)
            tcn = pr.op("V", lambda v: v.tensor_max(
                cnt64[:, :], cnt64[:, :], ctmp[:, :]), [tr], inc=sC)
        tof = pr.op("T", lambda t: t.matmul(
            psL[0:EXP, 0:1], selm[:, :], cnt64[:, :]), [tcn, tcp], inc=sT)
        tpf = pr.op("T", lambda t: t.matmul(
            psL[0:EXP, 1:2], lselm[:, :], cnt64[:, :]), [tcn], inc=sT)
        tcpo = pr.op("V", lambda v: v.tensor_copy(
            o8pf[:, :], psL[0:EXP, 0:2]), [tof, tpf], inc=sC)
        to8 = pr.op("V", lambda v: v.tensor_scalar(
            o8m1[:, :], o8pf[:, 0:1], -1.0, None, ALU.add), [tcpo], inc=sC)
        tpc = pr.op("T", lambda t: t.matmul(
            psL[:, 2:3], selc[:, :], o8pf[:, 1:2]), [tcpo], inc=sT)
        tpm = pr.op("V", lambda v: v.tensor_scalar(
            prefm1[:, :], psL[:, 2:3], -1.0, None, ALU.add), [tpc], inc=sC)

        if stop <= 3:
            res.close()
            return dummy_out()
        res.close()
        fx = pr.fence()

        ex = ExitStack()
        xe = ex.enter_context
        gpA = xe(nc.sbuf_tensor(f"gpA{tag}", [EXP, NTOK], F32))
        keptmA = xe(nc.sbuf_tensor(f"keptmA{tag}", [EXP, NTOK], BF16))
        wtA = xe(nc.sbuf_tensor(f"wtA{tag}", [EXP, NTOK], F32))
        myexp = xe(nc.sbuf_tensor(f"myexp{tag}", [P, 256], F32))
        spf = xe(nc.sbuf_tensor(f"spf{tag}", [P, 256], F32))
        c1t = xe(nc.sbuf_tensor(f"c1t{tag}", [P, 256], F32))
        c2t = xe(nc.sbuf_tensor(f"c2t{tag}", [P, 256], F32))
        cm8 = xe(nc.sbuf_tensor(f"cm8{tag}", [P, 256], U8))
        negt = xe(nc.sbuf_tensor(f"negt{tag}", [P, 256], F32))
        sel_o = xe(nc.sbuf_tensor(f"selo{tag}", [P, 256], F32))
        sel2_o = xe(nc.sbuf_tensor(f"sel2o{tag}", [P, 256], F32))
        sp16 = xe(nc.sbuf_tensor(f"sp16{tag}", [P, 256], I16))
        tbl1 = xe(nc.sbuf_tensor(f"tbl1{tag}", [P, 5376], I16))
        tb1f = xe(nc.sbuf_tensor(f"tb1f{tag}", [P, 5376], F32))
        evrow = xe(nc.sbuf_tensor(f"evrow{tag}", [1, 5376], I16))
        ones1f = xe(nc.sbuf_tensor(f"ones1f{tag}", [P, 1], F32))
        tbl16 = xe(nc.sbuf_tensor(f"tbl16{tag}", [P, CAP // P], I16))
        spr = xe(nc.sbuf_tensor(f"spr{tag}", [P, 256], F32))

        tgp = pr.op("V", lambda v: v.tensor_scalar(
            gpA[:, :], posA[:, :], o8m1[:, 0:1], None, ALU.add),
            [to8] + fx, inc=sC)
        tk1 = pr.op("V", lambda v: v.tensor_single_scalar(
            keptmA[:, :], gpA[:, :], float(CAP) - 0.5, ALU.is_lt), [tgp], inc=sC)
        tk2 = pr.op("V", lambda v: v.tensor_mul(
            keptmA[:, :], keptmA[:, :], maskA[:, :]), [tk1], inc=sC)
        tw1_ = pr.op("V", lambda v: v.tensor_scalar(
            wtA[:, :], gpA[:, :], e8[:, 0:1], None, ALU.add), [tk1], inc=sC)
        tw1b = tw1_
        for qi in range(1, 4):
            tqm = pr.op("V", lambda v, qi=qi: v.tensor_single_scalar(
                posA[:, :], gpA[:, :], qi * float(CAP // 4) - 0.5, ALU.is_gt),
                [tw1b, tk1], inc=sC)
            tw1b = pr.op("V", lambda v: v.scalar_tensor_tensor(
                wtA[:, :], posA[:, :], float((NCORE - 1) * (CAP // 4)),
                wtA[:, :], ALU.mult, ALU.add), [tqm], inc=sC)
        tw2_ = pr.op("V", lambda v: v.tensor_mul(
            wtA[:, :], wtA[:, :], keptmA[:, :]), [tw1b, tk2], inc=sC)

        tkx = tfx = None
        for j in range(NBLK):
            tkx = pr.op("T", lambda t, j=j: t.matmul(
                psL[:, 32 + j:33 + j], keptmA[:, j * P:(j + 1) * P], on8b[:, :]),
                [tk2, tpm], inc=sT)
            tfx = pr.op("T", lambda t, j=j: t.matmul(
                psL[:, 64 + j:65 + j], wtA[:, j * P:(j + 1) * P], on8f[:, :]),
                [tw2_], inc=sT)
        tke = pr.op("V", lambda v: v.tensor_copy(
            kept_f[:, :], psL[:, 32:32 + NBLK]), [tkx], inc=sC)
        tfe = pr.op("V", lambda v: v.tensor_copy(
            flat32[:, :], psL[:, 64:64 + NBLK]), [tfx], inc=sC)
        tptk = pr.dma("Y", ptok[:, :],
                      bass.AP(pinvrow, 0, [[1, P], [P, NBLK]]), waits=t_pinv,
                      allow_slow_non_contiguous=True, sem=f"spt{tag}")
        taf = pr.op("V", lambda v: v.scalar_tensor_tensor(
            af_t[:, :], ptok[:, :], 0.5, kept_f[:, :], ALU.mult, ALU.mult),
            [tke, tptk], inc=sC)
        tbf = pr.op("V", lambda v: v.tensor_scalar(
            bf_t[:, :], kept_f[:, :], -0.5, 0.5, ALU.mult, ALU.add),
            [tke], inc=sC)

        if stop <= 4:
            ex.close()
            return dummy_out()
        trl = pr.op("G", lambda g: (g.reg_load(greg, co32[0:1, 0:1]),
                                    g.nop())[1],
                    [co_tok, tAG2], inc=sG)
        tme = pr.dma("G", myexp[:, :],
                     bass.AP(spm_all, greg,
                             [[EXP * NTOK, EXP], [256, 16], [1, 256]]),
                     waits=[trl] + fx, sem=f"sme{tag}")
        if stop == 51:
            ex.close()
            return dummy_out()
        tng = pr.op("V", lambda v: v.memset(negt[:, :], -1.0), fx, inc=sC)
        tsf = pr.op("V", lambda v: v.tensor_scalar(
            spf[:, :], myexp[:, :], prefm1[:, 0:1], None, ALU.add),
            [tme, tpm], inc=sC)
        tc1 = pr.op("V", lambda v: v.tensor_single_scalar(
            c1t[:, :], myexp[:, :], 0.5, ALU.is_gt), [tme], inc=sC)
        tc2 = pr.op("V", lambda v: v.tensor_single_scalar(
            c2t[:, :], spf[:, :], float(CAP) - 0.5, ALU.is_lt), [tsf], inc=sC)
        tcm = pr.op("V", lambda v: v.tensor_mul(
            cm8[:, :], c1t[:, :], c2t[:, :]), [tc1, tc2], inc=sC)
        tsl0 = pr.op("V", lambda v: v.tensor_copy(
            sel_o[:, :], negt[:, :]), [tng], inc=sC)
        tsl = pr.op("V", lambda v: v.copy_predicated(
            sel_o[:, :], cm8[:, :], spf[:, :]), [tcm, tsl0], inc=sC)
        tll = pr.op("G", lambda g: (g.load_library(libcfg.local_scatter),
                                    g.nop())[1],
                    pr.fence(), inc=sG)
        tsc_ = None
        for rpass in range(3):
            base = rpass * 1792
            td1 = pr.op("V", lambda v, base=base: v.tensor_scalar(
                spr[:, :], sel_o[:, :], -float(base), None, ALU.add),
                [tsl] + ([tsc_] if tsc_ else []), inc=sC)
            tc1b = pr.op("V", lambda v: v.tensor_single_scalar(
                c1t[:, :], spr[:, :], -0.5, ALU.is_gt), [td1], inc=sC)
            tc2b = pr.op("V", lambda v: v.tensor_single_scalar(
                c2t[:, :], spr[:, :], 1791.5, ALU.is_lt), [td1], inc=sC)
            tcmb = pr.op("V", lambda v: v.tensor_mul(
                cm8[:, :], c1t[:, :], c2t[:, :]), [tc1b, tc2b], inc=sC)
            tslb0 = pr.op("V", lambda v: v.tensor_copy(
                sel2_o[:, :], negt[:, :]), [tng], inc=sC)
            tslb = pr.op("V", lambda v: v.copy_predicated(
                sel2_o[:, :], cm8[:, :], spr[:, :]), [tcmb, tslb0], inc=sC)
            t16b = pr.op("V", lambda v: v.tensor_copy(
                sp16[:, :], sel2_o[:, :]), [tslb], inc=sC)
            tsc_ = pr.op("G", lambda g, base=base: g.local_scatter(
                tbl1[:, base:base + 1792], tok16[:, :], sp16[:, :], P, 1792,
                256), [t16b, tll, tok16_tok] + ([tsc_] if tsc_ else []),
                inc=sG)
        if stop == 52:
            ex.close()
            return dummy_out()
        tcst = pr.op("V", lambda v: v.tensor_copy(tb1f[:, :], tbl1[:, :]),
                     [tsc_], inc=sC)
        tone = pr.op("V", lambda v: v.memset(ones1f[:, :], 1.0), fx, inc=sC)
        tev_ = None
        tevs = []
        for ci in range(11):
            c0 = ci * 512
            cw = 512 if ci < 10 else 256
            wv = [tcst, tone]
            if ci == 0:
                wv += [tke, tfe]
            if tev_ is not None:
                wv.append(tev_)
            tmmc = pr.op("T", lambda t, c0=c0, cw=cw: t.matmul(
                psL[0:1, 0:cw], ones1f[:, :], tb1f[:, c0:c0 + cw],
                start=True, stop=True), wv, inc=sT)
            tev_ = pr.op("V", lambda v, c0=c0, cw=cw:
                         v.tensor_copy(evrow[0:1, c0:c0 + cw],
                                       psL[0:1, 0:cw]), [tmmc], inc=sC)
            tevs.append(tev_)
        if stop == 53:
            ex.close()
            return dummy_out()
        ttr1 = pr.dma("Y", tblrow[0:1, :], evrow[0:1, :], waits=tevs,
                      sem=f"st1{tag}")
        ttr2 = pr.dma("Y", tbl16[:, :],
                      bass.AP(tblrow, 0, [[1, P], [P, CAP // P]]), waits=[ttr1],
                      allow_slow_non_contiguous=True, sem=f"st2{tag}")
        tt32 = pr.op("V", lambda v: v.tensor_copy(
            tblT32[:, :], tbl16[:, :]), [ttr2], inc=sC)

        if stop <= 5:
            ex.close()
            return dummy_out()
        ex.close()
        fx2 = pr.fence()

        me = ExitStack()
        mm = me.enter_context
        w2sb = mm(nc.sbuf_tensor(f"w2sb{tag}", [P, NBLK * DM], BF16))
        dispTM = [mm(nc.sbuf_tensor(f"dtm{tag}{i}", [P, DM], BF16)) for i in range(2)]
        dispF = mm(nc.sbuf_tensor(f"dispF{tag}", [P, EXP * SSPAN], BF16))
        hT = mm(nc.sbuf_tensor(f"hT{tag}", [P, NBLK * SSPAN], BF16))
        zt = [mm(nc.sbuf_tensor(f"zt{tag}{i}", [P, DM], BF16)) for i in range(2)]

        tW2 = pr.dma("G", w2sb[:, :].rearrange("p (m c) -> p m c", c=DM),
                     bass.AP(W2d, 0, [[DM, P], [P * DM, NBLK], [1, DM]]),
                     waits=fx2, sem=f"sw2{tag}")

        t_z = []
        t_gprev = [None, None]
        t_dfprev = None
        t_zdma = [None, None]
        t_relu = [None] * NBLK
        tAGq = []
        for w in range(NSSPAN):
            tdf = []
            for b2 in range(2):
                blk = 2 * w + b2
                wg = tAG1s + [tt32] + fx2
                if t_gprev[b2] is not None:
                    wg.append(t_gprev[b2])
                tg = pr.op("G", lambda g, b2=b2, blk=blk: g.indirect_dma_start(
                    dispTM[b2][:, :], None, xln_all[:, :],
                    IndirectOffsetOnAxis(tblT32[:, blk:blk + 1], 0)),
                    wg, inc=f"sdg{tag}{b2}", amt=16)
                ttr = None
                for c in range(EXP):
                    wtr = [tg]
                    if c == 0 and t_dfprev is not None:
                        wtr.append(t_dfprev)
                    if c == 0 and tdf:
                        wtr.append(tdf[-1])
                    ttr = pr.op("T", lambda t, b2=b2, c=c: t.transpose(
                        psTR[:, c * P:(c + 1) * P],
                        dispTM[b2][:, c * P:(c + 1) * P],
                        identb[:, :]), wtr, inc=sT)
                tdf.append(pr.op("V", lambda v, b2=b2: v.tensor_copy(
                    dispF[:, :].rearrange("p (c f) -> p c f", f=SSPAN)
                    [:, :, b2 * P:(b2 + 1) * P],
                    psTR[:, :].rearrange("p (c f) -> p c f", f=P)),
                    [ttr], inc=sC))
                t_gprev[b2] = tdf[-1]
            for m in range(NBLK):
                pw = psW1[m % 2]
                wv = tdf + [tW1]
                if m >= 2:
                    wv.append(t_relu[m - 2])
                elif t_relu[m - 2 + NBLK] is not None:
                    wv.append(t_relu[m - 2 + NBLK])
                tm = None
                for k in range(EXP):
                    tm = pr.op("T", lambda t, k=k, m=m, pw=pw: t.matmul(
                        pw[:, 0:SSPAN],
                        w1sb[:, k * DFF + m * P:k * DFF + (m + 1) * P],
                        dispF[:, k * SSPAN:(k + 1) * SSPAN],
                        start=(k == 0), stop=(k == EXP - 1)),
                        wv if k == 0 else [], inc=sT)
                if m % 2 == 0:
                    t_relu[m] = pr.op("V", lambda v, m=m, pw=pw: v.tensor_scalar(
                        hT[:, m * SSPAN:(m + 1) * SSPAN], pw[:, 0:SSPAN], 0.0,
                        None, ALU.max), [tm], inc=sC)
                else:
                    t_relu[m] = pr.op("S", lambda s, m=m, pw=pw: s.activation(
                        hT[:, m * SSPAN:(m + 1) * SSPAN], pw[:, 0:SSPAN],
                        AF.Relu), [tm], inc=sS)
            t_dfprev = tm
            for b2 in range(2):
                pz = psW2[b2]
                tz_half = None
                for half in range(2):
                    tmm = None
                    for m in range(NBLK):
                        wv = [t_relu[m], tW2]
                        if half == 1 and tz_half is not None:
                            wv.append(tz_half)
                        if half == 0 and t_zdma[b2] is not None:
                            wv.append(t_zdma[b2])
                        tmm = pr.op("T", lambda t, m=m, b2=b2, pz=pz, half=half:
                                    t.matmul(
                            pz[:, :],
                            hT[:, m * SSPAN + b2 * P:m * SSPAN + (b2 + 1) * P],
                            w2sb[:, m * DM + half * 512:m * DM + (half + 1) * 512],
                            start=(m == 0), stop=(m == NBLK - 1)),
                            wv, inc=sT)
                    if half == 0:
                        tz_half = pr.op("S", lambda s, b2=b2, half=half, pz=pz:
                                        s.copy(
                            zt[b2][:, half * 512:(half + 1) * 512], pz[:, :]),
                            [tmm], inc=sS)
                    else:
                        tz_half = pr.op("V", lambda v, b2=b2, half=half, pz=pz:
                                        v.tensor_copy(
                            zt[b2][:, half * 512:(half + 1) * 512], pz[:, :]),
                            [tmm], inc=sC)
                t_zdma[b2] = pr.dma(
                    "Y", eout_in[(2 * w + b2) * P:(2 * w + b2 + 1) * P, :],
                    zt[b2][:, :], waits=[tz_half], sem=f"szw{tag}{b2}")
                t_z.append(t_zdma[b2])
            if (w + 1) % (NSSPAN // 4) == 0 and w + 1 < NSSPAN:
                qq = (w + 1) // (NSSPAN // 4) - 1
                tAGq.append(pr.op("G", lambda g, qq=qq: g.collective_compute(
                    "AllGather", ALU.bypass, replica_groups=RG,
                    ins=[eout_in[qq * QCAP:(qq + 1) * QCAP, :].opt()],
                    outs=[eout_all[qq * NCORE * QCAP:
                                   (qq + 1) * NCORE * QCAP, :].opt()]),
                    waits=t_z[qq * (NSSPAN // 2):] +
                          ([tAG2] if qq == 0 else []), inc=f"cC{tag}"))

        tAG3 = pr.op("G", lambda g: g.collective_compute(
            "AllGather", ALU.bypass, replica_groups=RG,
            ins=[eout_in[3 * QCAP:CAP, :].opt()],
            outs=[eout_all[3 * NCORE * QCAP:4 * NCORE * QCAP, :].opt()]),
            waits=t_z[3 * (NSSPAN // 2):], inc=f"cC{tag}")

        if stop <= 6:
            me.close()
            return dummy_out()
        me.close()
        fx3 = pr.fence()

        ce = ExitStack()
        cm_ = ce.enter_context
        NCR = 4
        eoutg = [cm_(nc.sbuf_tensor(f"eog{tag}{i}", [P, DM], BF16)) for i in range(NCR)]
        xlt = [cm_(nc.sbuf_tensor(f"xlt{tag}{i}", [P, DM], BF16)) for i in range(NCR)]
        rsb = [cm_(nc.sbuf_tensor(f"rsb{tag}{i}", [P, DM], F32)) for i in range(NCR)]
        ctm = [cm_(nc.sbuf_tensor(f"ctm{tag}{i}", [P, DM], F32)) for i in range(NCR)]
        ott = [cm_(nc.sbuf_tensor(f"ott{tag}{i}", [P, DM], F32)) for i in range(NCR)]
        touts = []
        tprevc = [None] * NCR
        for j in range(NBLK):
            i = j % NCR
            wg = tAGq + [tAG3, tfe, taf, tbf] + fx3
            if tprevc[i] is not None:
                wg.append(tprevc[i])
                wg.append(touts[j - NCR])
            tg = pr.op("G", lambda g, j=j, i=i: g.indirect_dma_start(
                eoutg[i][:, :], None, eout_all[:, :],
                IndirectOffsetOnAxis(flat32[:, j:j + 1], 0)),
                wg, inc=f"seg{tag}{i}", amt=16)
            tx = pr.dma("Y", xlt[i][:, :], xln_in[j * P:(j + 1) * P, :],
                        waits=wg, sem=f"sxl{tag}{i}")
            trs = pr.dma("G", rsb[i][:, :], src[j * P:(j + 1) * P, :], waits=wg,
                         sem=f"srs{tag}{i}")
            t1 = pr.op("V", lambda v, j=j, i=i: v.scalar_tensor_tensor(
                ctm[i][:, :], eoutg[i][:, :], af_t[:, j:j + 1], rsb[i][:, :],
                ALU.mult, ALU.add), [tg, trs], inc=sC)
            t2 = pr.op("V", lambda v, j=j, i=i: v.scalar_tensor_tensor(
                ott[i][:, :], xlt[i][:, :], bf_t[:, j:j + 1], ctm[i][:, :],
                ALU.mult, ALU.add), [t1, tx], inc=sC)
            tprevc[i] = t2
            touts.append(pr.dma("Y", dst[j * P:(j + 1) * P, :], ott[i][:, :],
                                waits=[t2], sem=f"sow{tag}{i}"))
        ce.close()
        fes.close()
        return touts, t_gprev[1] or t_gprev[0]


    def attn_phase(src, dst, deps, src_tok=None, ptr_tok=None):
        sC, sS, sT, sG = "vX", "sX", "tX", "gX"
        ASP = 512
        NASP = NTOK // ASP
        CPA = ASP // P
        aes = ExitStack()
        ae = aes.enter_context
        wqs = ae(nc.sbuf_tensor("wqs", [P, EXP * 512], BF16))
        wks = ae(nc.sbuf_tensor("wks", [P, EXP * 512], BF16))
        wvs = ae(nc.sbuf_tensor("wvs", [P, EXP * 1024], BF16))
        wgs = ae(nc.sbuf_tensor("wgs", [P, EXP * 1024], BF16))
        wos = ae(nc.sbuf_tensor("wos", [P, EXP * 1024], BF16))
        wbs = ae(nc.sbuf_tensor("wbs", [P, EXP * H], BF16))
        xr2, xl2 = xringT, xlnbT
        xaf = xafT
        qf = ae(nc.sbuf_tensor("qf", [P, H * ASP], BF16))
        kfr = ae(nc.sbuf_tensor("kfr", [P, H * ASP], BF16))
        ktok = ae(nc.sbuf_tensor("ktok", [P, CPA * 512], BF16))
        vbt = ae(nc.sbuf_tensor("vbt", [P, CPA * 1024], BF16))
        gf = ae(nc.sbuf_tensor("gf", [P, 2 * H * ASP], BF16))
        betaT = ae(nc.sbuf_tensor("betaT", [P, CPA * H], F32))
        negbT = ae(nc.sbuf_tensor("negbT", [P, CPA * H], F32))
        kss = ae(nc.sbuf_tensor("kss", [P, CPA * H], F32))
        krs = ae(nc.sbuf_tensor("krs", [P, CPA * H], F32))
        kfn2 = [ae(nc.sbuf_tensor(f"kfn{i}", [P, 512], BF16)) for i in range(2)]
        sNB2 = [ae(nc.sbuf_tensor(f"sNB{i}", [P, 512], BF16)) for i in range(2)]
        sNBT2 = [ae(nc.sbuf_tensor(f"sNBT{i}", [P, 512], BF16)) for i in range(2)]
        powA2 = [[ae(nc.sbuf_tensor(f"powA{i}{jj}", [P, 512], BF16))
                  for jj in range(2)] for i in range(2)]
        powT2 = [[ae(nc.sbuf_tensor(f"powT{i}{jj}", [P, 512], BF16))
                  for jj in range(2)] for i in range(2)]
        sX2 = [ae(nc.sbuf_tensor(f"sX{i}", [P, 512], BF16)) for i in range(2)]
        nbk2 = [ae(nc.sbuf_tensor(f"nbk{i}", [P, 512], BF16)) for i in range(2)]
        sW1c2 = [ae(nc.sbuf_tensor(f"sW1c{i}", [P, 1024], BF16)) for i in range(2)]
        sW2cT2 = [ae(nc.sbuf_tensor(f"sW2cT{i}", [P, 512], BF16)) for i in range(2)]
        smk2 = [ae(nc.sbuf_tensor(f"smk{i}", [P, 512], BF16)) for i in range(2)]
        sD = ae(nc.sbuf_tensor("sD", [P, 1024], BF16))
        sS0 = ae(nc.sbuf_tensor("sS0", [P, 1024], BF16))
        sSf = ae(nc.sbuf_tensor("sSf", [P, 1024], F32))
        oTs = [ae(nc.sbuf_tensor(f"oTs{i}", [P, CPA * 512], BF16)) for i in range(2)]
        sqb = ae(nc.sbuf_tensor("sqb", [P, CPA * 512], BF16))
        rr1 = ae(nc.sbuf_tensor("rr1", [1, CPA * 512], F32))
        epsT = ae(nc.sbuf_tensor("epsT", [1, 1], F32))
        scT = ae(nc.sbuf_tensor("scT", [1, 1], F32))
        rrb = ae(nc.sbuf_tensor("rrb", [1, CPA * 512], BF16))
        rbc = ae(nc.sbuf_tensor("rbc", [P, CPA * 512], BF16))
        sgt = [ae(nc.sbuf_tensor(f"sgt{i}", [P, 512], BF16)) for i in range(2)]
        rs2 = [ae(nc.sbuf_tensor(f"rs2{i}", [P, DM], F32)) for i in range(2)]
        yout = [ae(nc.sbuf_tensor(f"yout{i}", [P, DM], F32)) for i in range(2)]
        psS = ae(nc.psum_tensor("psS", [P, 512], F32))
        psX = ae(nc.psum_tensor("psX", [P, 512], F32))
        psDW = ae(nc.psum_tensor("psDW", [P, 512], F32))
        psQT = ae(nc.psum_tensor("psQT", [P, 512], F32))
        psOp = ae(nc.psum_tensor("psOp", [P, 1024], F32))
        psEp = ae(nc.psum_tensor("psEp", [P, 512], F32))
        psT2 = psTRg

        tw = {}
        for nm, sb, dr, cw in [("q", wqs, Wqd, 512), ("k", wks, Wkd, 512),
                               ("v", wvs, Wvd, 1024), ("g", wgs, Wgd, 1024),
                               ("o", wos, Wod, 1024), ("b", wbs, Wbd, H)]:
            tw[nm] = pr.dma(
                "G", sb[:, :].rearrange("p (k c) -> p k c", c=cw),
                bass.AP(dr, 0, [[cw, P], [P * cw, EXP], [1, cw]]),
                waits=deps, sem=f"aw{nm}")

        tS0m = pr.op("V", lambda v: v.memset(sS0[:, :], 0.0), deps, inc=sC)
        teps = pr.op("V", lambda v: v.memset(epsT[:, :], EPS_ON), deps, inc=sC)
        tsc_c = pr.op("V", lambda v: v.memset(scT[:, :], 1.0 / DV), deps, inc=sC)
        tSadd = pr.op("V", lambda v: v.memset(sSf[:, :], 0.0), deps, inc=sC)
        touts = []
        tSupd = [None] * H
        tS0c = tS0m
        tDc = None
        tXc = None
        tOev = None
        tWo = None
        tYev = [None, None]
        tYw = [None, None]
        tRs = [None, None]
        tNBT_prev = None
        tmkm_last = None
        tneglast = None
        tgsil = None
        tvev = None
        tr1_p = tr1b_p = None
        tbc_p = None
        tEfree = tQfree = tXfree = tDWfree = None
        tknG = [None, None]
        tkng = [None] * CPA
        lnring = [None, None]
        prev_evac = [None]

        for a in range(NASP):
            def asink(j, xb, tx, a=a):
                bb = j - 4 * a
                tev = None
                for wv in range(2):
                    trs = None
                    for cc4 in range(4):
                        cch = wv * 4 + cc4
                        wtr = [tx]
                        if cc4 == 0:
                            wtr += [prev_evac[0], ptr_tok]
                        trs = pr.op("T", lambda t, cch=cch, cc4=cc4, xb=xb:
                                    t.transpose(
                            psT2[:, 512 + cc4 * P:512 + (cc4 + 1) * P],
                            xb[:, cch * P:(cch + 1) * P],
                            identb[:, :]), wtr, inc=sT)
                    tev = pr.op("V", lambda v, bb=bb, wv=wv: v.tensor_copy(
                        xaf[:, :].rearrange("p (c f) -> p c f", f=ASP)
                        [:, 4 * wv:4 * wv + 4, bb * P:(bb + 1) * P],
                        psT2[:, 512:1024].rearrange("p (c f) -> p c f", f=P)),
                        [trs], inc=sC)
                    prev_evac[0] = tev
                return tev

            abdeps = None
            if src_tok is not None:
                abdeps = [None] * NBLK
                for jj in range(NBLK):
                    abdeps[jj] = [src_tok[jj]]
            t_ln, t_lns = emit_ln(src, deps, asink, xr2, xl2, sC, sS,
                                  blk0=4 * a, nblk=4, ring_tok=lnring,
                                  blk_deps=abdeps)
            lnring = [t_lns[4 * a + 2], t_lns[4 * a + 3]]
            txaf = t_ln[4 * a + 3]

            projps = [(psDW, 0), (psX, 0), (psOp, 0), (psOp, 512)]
            pv = [[t for t in (tDWfree,) if t is not None],
                  [t for t in (tXfree,) if t is not None],
                  [t for t in (tYev[0], tYev[1], tOev, tr1b_p) if t is not None],
                  [t for t in (tYev[0], tYev[1], tOev, tr1b_p) if t is not None]]
            nps = [0]

            def proj(outsb, outcol, wsb, wcol, scale, toks, slots=(0, 1)):
                slot = slots[nps[0] % len(slots)]
                ps, off = projps[slot]
                nps[0] += 1
                cwid = wsb.shape[1] // EXP
                tm = None
                for k in range(EXP):
                    w = list(toks) + [txaf] + (pv[slot] if k == 0 else [])
                    tm = pr.op("T", lambda t, k=k, ps=ps, off=off: t.matmul(
                        ps[:, off:off + 512],
                        wsb[:, k * cwid + wcol:k * cwid + wcol + P],
                        xaf[:, k * ASP:k * ASP + 512],
                        start=(k == 0), stop=(k == EXP - 1)),
                        w if k == 0 else [], inc=sT)
                if scale is None:
                    if nps[0] % 2 == 0:
                        te = pr.op("S", lambda s, ps=ps, off=off: s.copy(
                            outsb[:, outcol:outcol + 512], ps[:, off:off + 512]),
                            [tm], inc=sS)
                    else:
                        te = pr.op("V", lambda v, ps=ps, off=off: v.tensor_copy(
                            outsb[:, outcol:outcol + 512], ps[:, off:off + 512]),
                            [tm], inc=sC)
                else:
                    te = pr.op("S", lambda s, ps=ps, off=off: s.mul(
                        outsb[:, outcol:outcol + 512], ps[:, off:off + 512],
                        scale), [tm], inc=sS)
                pv[slot] = [te]
                return te, slot

            tqev = tkev = tgev = None
            for m in range(H):
                tqev, _sl = proj(qf, m * ASP, wqs, m * P, SCALE_Q, [tw["q"]])
            for m in range(H):
                tkev, _sl = proj(kfr, m * ASP, wks, m * P, None, [tw["k"]])
            for m in range(2 * H):
                tgev, _sl = proj(gf, m * ASP, wgs, m * P, None,
                                 [tw["g"], tgsil], slots=(2, 3))
            for bb in range(CPA):
                tmb = None
                for k in range(EXP):
                    w = [tw["b"], txaf]
                    if k == 0:
                        w += [tmkm_last, tneglast, tbc_p, tEfree]
                    tmb = pr.op("T", lambda t, k=k, bb=bb: t.matmul(
                        psEp[:, bb * H:(bb + 1) * H],
                        xaf[:, k * ASP + bb * P:k * ASP + (bb + 1) * P],
                        wbs[:, k * H:(k + 1) * H],
                        start=(k == 0), stop=(k == EXP - 1)),
                        w if k == 0 else [], inc=sT)
                tsig = pr.op("S", lambda s, bb=bb: s.activation(
                    betaT[:, bb * H:(bb + 1) * H], psEp[:, bb * H:(bb + 1) * H],
                    AF.Sigmoid), [tmb], inc=sS)
                tneglast = pr.op("V", lambda v, bb=bb: v.tensor_scalar(
                    negbT[:, bb * H:(bb + 1) * H],
                    betaT[:, bb * H:(bb + 1) * H], -1.0, None, ALU.mult),
                    [tsig], inc=sC)
                for half in range(2):
                    slot = nps[0] % 2
                    ps, off = projps[slot]
                    nps[0] += 1
                    tm = None
                    for k in range(EXP):
                        w = [tw["v"], txaf] + (pv[slot] if k == 0 else [])
                        tm = pr.op("T", lambda t, k=k, bb=bb, half=half, ps=ps,
                                   off=off: t.matmul(
                            ps[:, off:off + 512],
                            xaf[:, k * ASP + bb * P:k * ASP + (bb + 1) * P],
                            wvs[:, k * 1024 + half * 512:
                                k * 1024 + (half + 1) * 512],
                            start=(k == 0), stop=(k == EXP - 1)),
                            w if k == 0 else [], inc=sT)
                    for hh in range(2):
                        h = half * 2 + hh
                        tvev = pr.op("V", lambda v, bb=bb, h=h, hh=hh, ps=ps,
                                     off=off: v.tensor_scalar(
                            vbt[:, bb * 1024 + h * 256:bb * 1024 + (h + 1) * 256],
                            ps[:, off + hh * 256:off + (hh + 1) * 256],
                            betaT[:, bb * H + h:bb * H + h + 1], None,
                            ALU.mult), [tm, tsig], inc=sC)
                    pv[slot] = [tvev]

            tkng_new = [None] * CPA
            tDWfree = pv[0][0] if pv[0] else tDWfree
            tXfree = pv[1][0] if pv[1] else tXfree
            for cl in range(CPA):
                c = a * CPA + cl
                par = c % 2
                kfn = kfn2[par]
                sNB, sNBT, sX = sNB2[par], sNBT2[par], sX2[par]
                powA, powT = powA2[par], powT2[par]
                nbk, sW1c, sW2cT, smk = (nbk2[par], sW1c2[par], sW2cT2[par],
                                         smk2[par])
                tts = []
                for h in range(H):
                    w = [tkev]
                    if h == 0:
                        w += [tkng[cl], tNBT_prev, tknG[par]]
                    tts.append(pr.op("T", lambda t, h=h, cl=cl: t.transpose(
                        psT2[:, h * P:(h + 1) * P],
                        kfr[:, h * ASP + cl * P:h * ASP + (cl + 1) * P],
                        identb[:, :]), w if h == 0 else [], inc=sT))
                tka = None
                for h in range(H):
                    tka = pr.op("S", lambda s, h=h, cl=cl: s.activation(
                        lntrash[:, 0:P], psT2[:, h * P:(h + 1) * P], AF.Square,
                        accum_out=kss[:, cl * H + h:cl * H + h + 1]),
                        [tts[h]], inc=sS)
                sl = slice(cl * H, (cl + 1) * H)
                pr.op("V", lambda v, sl=sl: v.tensor_scalar(
                    krs[:, sl], kss[:, sl], 1.0, EPS_K, ALU.mult, ALU.add),
                    [tka], inc=sC)
                tk2 = pr.op("V", lambda v, sl=sl: v.reciprocal(
                    krs[:, sl], krs[:, sl]), inc=sC)
                tk3 = pr.op("S", lambda s, sl=sl: s.activation(
                    krs[:, sl], krs[:, sl], AF.Sqrt), [tk2], inc=sS)
                tktk = None
                for h in range(H):
                    tktk = pr.op("V", lambda v, h=h, cl=cl: v.tensor_scalar(
                        ktok[:, cl * 512 + h * P:cl * 512 + (h + 1) * P],
                        psT2[:, h * P:(h + 1) * P],
                        krs[:, cl * H + h:cl * H + h + 1], None, ALU.mult),
                        [tk3], inc=sC)
                tts2 = []
                for h in range(H):
                    tts2.append(pr.op("T", lambda t, h=h, cl=cl: t.transpose(
                        psT2[:, h * P:(h + 1) * P],
                        ktok[:, cl * 512 + h * P:cl * 512 + (h + 1) * P],
                        identb[:, :]), [tktk] if h == 0 else [], inc=sT))
                tkfn = pr.op("S", lambda s, kfn=kfn: s.copy(
                    kfn[:, :], psT2[:, 0:512]), [tts2[-1]], inc=sS)
                tknG[par] = tkfn
                tG = None
                for h in range(H):
                    w = [tkfn]
                    if h == 0:
                        w += [tneglast, tEfree]
                    tG = pr.op("T", lambda t, h=h, kfn=kfn: t.matmul(
                        psEp[:, h * P:(h + 1) * P], kfn[:, h * P:(h + 1) * P],
                        kfn[:, h * P:(h + 1) * P]), w if h == 0 else [], inc=sT)
                tNB = None
                for h in range(H):
                    tNB = pr.op("V", lambda v, h=h, cl=cl, sNB=sNB:
                                v.scalar_tensor_tensor(
                        sNB[:, h * P:(h + 1) * P], psEp[:, h * P:(h + 1) * P],
                        negbT[:, cl * H + h:cl * H + h + 1], slm[:, :],
                        ALU.mult, ALU.mult), [tG], inc=sC)
                tts3 = []
                for h in range(H):
                    tts3.append(pr.op("T", lambda t, h=h, sNB=sNB: t.transpose(
                        psT2[:, h * P:(h + 1) * P], sNB[:, h * P:(h + 1) * P],
                        identb[:, :]), [tNB, tkfn] if h == 0 else [], inc=sT))
                tNBT = pr.op("S", lambda s, sNBT=sNBT: s.copy(
                    sNBT[:, :], psT2[:, 0:512]), [tts3[-1]], inc=sS)
                tNBT_prev = tNBT
                tX = pr.op("V", lambda v, sX=sX, sNBT=sNBT: v.tensor_add(
                    sX[:, :], sNBT[:, :], id4[:, :]), [tNBT], inc=sC)
                pA, pAT = sNB, sNBT
                tpa = tNBT
                tpat = tNBT
                for j in range(1, 7):
                    tsq = None
                    for h in range(H):
                        w = [tpa, tpat]
                        if h == 0:
                            w.append(tEfree)
                        tsq = pr.op("T", lambda t, h=h, pA=pA, pAT=pAT: t.matmul(
                            psEp[:, h * P:(h + 1) * P],
                            pAT[:, h * P:(h + 1) * P],
                            pA[:, h * P:(h + 1) * P]), w if h == 0 else [],
                            inc=sT)
                    npA = powA[j % 2]
                    tpa_n = pr.op("S", lambda s, npA=npA: s.copy(
                        npA[:, :], psEp[:, 0:512]), [tsq], inc=sS)
                    tEfree = tpa_n
                    if j < 6:
                        tsqt = None
                        for h in range(H):
                            w = [tpa, tpat]
                            if h == 0:
                                w.append(tQfree)
                            tsqt = pr.op("T", lambda t, h=h, pA=pA, pAT=pAT:
                                         t.matmul(
                                psQT[:, h * P:(h + 1) * P],
                                pA[:, h * P:(h + 1) * P],
                                pAT[:, h * P:(h + 1) * P]),
                                w if h == 0 else [], inc=sT)
                        npAT = powT[j % 2]
                        tpat_n = pr.op("S", lambda s, npAT=npAT: s.copy(
                            npAT[:, :], psQT[:, 0:512]), [tsqt], inc=sS)
                        tQfree = tpat_n
                    txm = None
                    for h in range(H):
                        w = [tpa_n, tX]
                        if h == 0:
                            w.append(tXfree)
                        txm = pr.op("T", lambda t, h=h, sX=sX: t.matmul(
                            psX[:, h * P:(h + 1) * P], identb[:, :],
                            sX[:, h * P:(h + 1) * P], start=True, stop=False),
                            w if h == 0 else [], inc=sT)
                        txm = pr.op("T", lambda t, h=h, npA=npA, sX=sX: t.matmul(
                            psX[:, h * P:(h + 1) * P],
                            npA[:, h * P:(h + 1) * P],
                            sX[:, h * P:(h + 1) * P], start=False, stop=True),
                            [], inc=sT)
                    tX = pr.op("S", lambda s, sX=sX: s.copy(
                        sX[:, :], psX[:, 0:512]), [txm], inc=sS)
                    tXfree = tX
                    if j < 6:
                        pA, pAT = npA, npAT
                        tpa, tpat = tpa_n, tpat_n
                    else:
                        tpa = tpa_n
                tXc = tX
                tw1 = None
                for hh in range(2):
                    tw1m = None
                    for h in (2 * hh, 2 * hh + 1):
                        w = [tX, tvev]
                        if h == 2 * hh:
                            w.append(tXfree)
                        tw1m = pr.op("T", lambda t, h=h, hh=hh, cl=cl, sX=sX:
                                     t.matmul(
                            psX[:, (h % 2) * 256:(h % 2) * 256 + 256],
                            sX[:, h * P:(h + 1) * P],
                            vbt[:, cl * 1024 + h * 256:cl * 1024 + (h + 1) * 256]),
                            w if h == 2 * hh else [], inc=sT)
                    tw1 = pr.op("S", lambda s, hh=hh, sW1c=sW1c: s.copy(
                        sW1c[:, hh * 512:(hh + 1) * 512], psX[:, 0:512]),
                        [tw1m] + ([tw1] if tw1 else []), inc=sS)
                    tXfree = tw1
                tnbk = None
                for h in range(H):
                    tnbk = pr.op("V", lambda v, h=h, cl=cl, nbk=nbk:
                                 v.tensor_scalar(
                        nbk[:, h * P:(h + 1) * P],
                        ktok[:, cl * 512 + h * P:cl * 512 + (h + 1) * P],
                        negbT[:, cl * H + h:cl * H + h + 1], None, ALU.mult),
                        [tktk], inc=sC)
                tw2m = None
                for h in range(H):
                    w = [tnbk, tX, tpa, tEfree]
                    tw2m = pr.op("T", lambda t, h=h, nbk=nbk, sX=sX: t.matmul(
                        psEp[:, h * P:(h + 1) * P], nbk[:, h * P:(h + 1) * P],
                        sX[:, h * P:(h + 1) * P]), w if h == 0 else [], inc=sT)
                tw2 = pr.op("S", lambda s, sW2cT=sW2cT: s.copy(
                    sW2cT[:, :], psEp[:, 0:512]), [tw2m], inc=sS)
                tEfree = tw2
                tmkm = None
                for h in range(H):
                    w = [tw2, tkfn, tqev]
                    tmkm = pr.op("T", lambda t, h=h, cl=cl, kfn=kfn: t.matmul(
                        psEp[:, h * P:(h + 1) * P], kfn[:, h * P:(h + 1) * P],
                        qf[:, h * ASP + cl * P:h * ASP + (cl + 1) * P]),
                        w if h == 0 else [], inc=sT)
                tmkm_last = tmkm
                tmk = pr.op("V", lambda v, smk=smk: v.tensor_mul(
                    smk[:, :], psEp[:, 0:512], ut4[:, :]), [tmkm], inc=sC)
                tEfree = tmk
                if c > 0:
                    tS0c = pr.op("S", lambda s: s.copy(
                        sS0[:, :], sSf[:, :]), [tSadd, tS0c, tOev], inc=sS)
                tDc = None
                for hh in range(2):
                    tDm = None
                    for h in (2 * hh, 2 * hh + 1):
                        w = [tS0c, tw1, tw2]
                        if h == 2 * hh:
                            w.append(tDWfree)
                        tDm = pr.op("T", lambda t, h=h, sW1c=sW1c: t.matmul(
                            psDW[:, (h % 2) * 256:(h % 2) * 256 + 256],
                            identb[:, :],
                            sW1c[:, h * 256:(h + 1) * 256], start=True,
                            stop=False), w if h == 2 * hh else [], inc=sT)
                        tDm = pr.op("T", lambda t, h=h, sW2cT=sW2cT: t.matmul(
                            psDW[:, (h % 2) * 256:(h % 2) * 256 + 256],
                            sW2cT[:, h * P:(h + 1) * P],
                            sS0[:, h * 256:(h + 1) * 256], start=False,
                            stop=True), [], inc=sT)
                    tDh = pr.op("V", lambda v, hh=hh: v.tensor_copy(
                        sD[:, hh * 512:(hh + 1) * 512], psDW[:, 0:512]),
                        [tDm] + ([tDc] if tDc else []), inc=sC)
                    tDWfree = tDh
                    tDc = tDh
                for hh in range(2):
                    tSm = None
                    for h in (2 * hh, 2 * hh + 1):
                        w = [tDc, tktk]
                        if h == 2 * hh:
                            w.append(tSadd)
                        tSm = pr.op("T", lambda t, h=h, cl=cl: t.matmul(
                            psS[:, (h % 2) * 256:(h % 2) * 256 + 256],
                            ktok[:, cl * 512 + h * P:cl * 512 + (h + 1) * P],
                            sD[:, h * 256:(h + 1) * 256]),
                            w if h == 2 * hh else [], inc=sT)
                    tSadd = pr.op("V", lambda v, hh=hh: v.tensor_add(
                        sSf[:, hh * 512:(hh + 1) * 512],
                        sSf[:, hh * 512:(hh + 1) * 512], psS[:, 0:512]),
                        [tSm], inc=sC)
                tkng_new[cl] = tSadd
                tOm = None
                for h in range(H):
                    for f in range(2):
                        w = [tS0c, tDc, tmk, tqev]
                        if h == 0 and f == 0:
                            w += [t for t in (tOev, tYev[0], tYev[1], tWo)
                                  if t is not None]
                        tOm = pr.op("T", lambda t, h=h, f=f, cl=cl, smk=smk:
                                    t.matmul(
                            psOp[:, (2 * h + f) * P:(2 * h + f + 1) * P],
                            sS0[:, h * 256 + f * P:h * 256 + (f + 1) * P],
                            qf[:, h * ASP + cl * P:h * ASP + (cl + 1) * P],
                            start=True, stop=False), w if (h == 0 and f == 0)
                            else [], inc=sT)
                        tOm = pr.op("T", lambda t, h=h, f=f, smk=smk: t.matmul(
                            psOp[:, (2 * h + f) * P:(2 * h + f + 1) * P],
                            sD[:, h * 256 + f * P:h * 256 + (f + 1) * P],
                            smk[:, h * P:(h + 1) * P],
                            start=False, stop=True), [], inc=sT)
                for f in range(2):
                    tOev = pr.op("V", lambda v, f=f, cl=cl: v.tensor_copy(
                        oTs[f][:, cl * 512:(cl + 1) * 512]
                        .rearrange("p (c2 t) -> p c2 t", t=P),
                        psOp[:, :].rearrange("p (h2 s f2) -> p h2 s f2",
                                             s=2, f2=P)[:, :, f, :]),
                        [tOm], inc=sC)
            tkng = tkng_new
            tcs = None
            for f in range(2):
                tsq = pr.op("S", lambda s, f=f: s.activation(
                    sqb[:, :], oTs[f][:, :], AF.Square),
                    [tOev] + ([tcs] if tcs else []), inc=sS)
                for q4 in range(CPA):
                    r0 = 32 * (q4 // 2)
                    off = 512 * (q4 % 2)
                    w = [tsq]
                    if f == 0 and q4 == 0:
                        w += [tOev]
                    tcs = pr.op("T", lambda t, q4=q4, f=f, r0=r0, off=off:
                                t.matmul(
                        psOp[r0:r0 + 1, off:off + 512], oncb[:, 0:1],
                        sqb[:, q4 * 512:(q4 + 1) * 512],
                        start=(f == 0), stop=(f == 1)), w, inc=sT)
            tr1 = pr.op("S", lambda s: s.activation(
                rr1[0:1, 0:1024], psOp[0:1, :], AF.Sqrt,
                bias=epsT[0:1, 0:1], scale=scT[0:1, 0:1]),
                [tcs, teps, tsc_c], inc=sS)
            tr1b = pr.op("S", lambda s: s.activation(
                rr1[0:1, 1024:2048], psOp[32:33, :], AF.Sqrt,
                bias=epsT[0:1, 0:1], scale=scT[0:1, 0:1]), [tcs], inc=sS)
            tr2 = pr.op("V", lambda v: v.reciprocal(rr1[0:1, :], rr1[0:1, :]),
                        [tr1, tr1b], inc=sC)
            tr4 = pr.op("S", lambda s: s.copy(rrb[0:1, :], rr1[0:1, :]),
                        [tr2], inc=sS)
            tr1_p, tr1b_p = tr1, tr1b
            tbc = None
            for cl in range(CPA):
                tbm = None
                for h in range(H):
                    w = [tr4]
                    if h == 0:
                        w.append(tbc if tbc is not None else tcs)
                    tbm = pr.op("T", lambda t, cl=cl, h=h: t.matmul(
                        psOp[:, h * P:(h + 1) * P], on1b[:, :],
                        rrb[0:1, cl * 512 + h * P:cl * 512 + (h + 1) * P]),
                        w if h == 0 else [], inc=sT)
                tbc = pr.op("V", lambda v, cl=cl: v.tensor_copy(
                    rbc[:, cl * 512:(cl + 1) * 512], psOp[:, 0:512]),
                    [tbm], inc=sC)
                tbc_p = tbc
            tognew = None
            togs = [None, None]
            for ch2 in range(2 * H):
                h, f = ch2 // 2, ch2 % 2
                i = ch2 % 2
                w = [tgev, tbc]
                if togs[i] is not None:
                    w.append(togs[i])
                tsg = pr.op("S", lambda s, ch2=ch2, i=i: s.activation(
                    sgt[i][:, :], gf[:, ch2 * ASP:(ch2 + 1) * ASP], AF.Silu),
                    w, inc=sS)
                tsg2 = pr.op("V", lambda v, h=h, i=i: v.tensor_mul(
                    sgt[i][:, :].rearrange("p (c t) -> p c t", t=P),
                    sgt[i][:, :].rearrange("p (c t) -> p c t", t=P),
                    rbc[:, :].rearrange("p (c hh t) -> p c hh t", hh=H, t=P)
                    [:, :, h, :]), [tsg, tbc], inc=sC)
                tognew = pr.op("V", lambda v, h=h, f=f, i=i:
                               v.scalar_tensor_tensor(
                    oTs[f][:, :].rearrange("p (c hh t) -> p c hh t", hh=H, t=P)
                    [:, :, h, :],
                    oTs[f][:, :].rearrange("p (c hh t) -> p c hh t", hh=H, t=P)
                    [:, :, h, :],
                    nwsb[:, f:f + 1],
                    sgt[i][:, :].rearrange("p (c t) -> p c t", t=P),
                    ALU.mult, ALU.mult), [tsg2, tOev], inc=sC)
                togs[i] = tognew
            tgsil = tognew
            for cl in range(CPA):
                j = 4 * a + cl
                i = cl % 2
                wv = [tYev[i]] if tYev[i] else []
                if src_tok is not None:
                    wv.append(src_tok[j])
                tRs[i] = pr.dma("Y", rs2[i][:, :], src[j * P:(j + 1) * P, :],
                                waits=wv + [tognew], sem=f"ars{i}")
                tWo = None
                for half in range(2):
                    for ch2 in range(2 * H):
                        h, f = ch2 // 2, ch2 % 2
                        w2 = [tognew, tw["o"]]
                        if ch2 == 0 and half == 0:
                            w2 += [t for t in (tYev[0], tYev[1], tbc, tr1b)
                                   if t is not None]
                        tWo = pr.op("T", lambda t, cl=cl, half=half, h=h, f=f,
                                    ch2=ch2: t.matmul(
                            psOp[:, half * 512:(half + 1) * 512],
                            oTs[f][:, cl * 512 + h * P:cl * 512 + (h + 1) * P],
                            wos[:, ch2 * 1024 + half * 512:
                                ch2 * 1024 + (half + 1) * 512],
                            start=(ch2 == 0), stop=(ch2 == 2 * H - 1)),
                            w2 if (ch2 == 0 and half == 0) else [], inc=sT)
                tye = None
                for half in range(2):
                    w2 = [tWo, tRs[i]]
                    if tYw[i] is not None:
                        w2.append(tYw[i])
                    tye = pr.op("V", lambda v, i=i, half=half: v.tensor_add(
                        yout[i][:, half * 512:(half + 1) * 512],
                        rs2[i][:, half * 512:(half + 1) * 512],
                        psOp[:, half * 512:(half + 1) * 512]), w2, inc=sC)
                tYev[i] = tye
                tYw[i] = pr.dma("Y", dst[j * P:(j + 1) * P, :], yout[i][:, :],
                                waits=[tye], sem=f"ayw{i}")
                touts.append(tYw[i])
        aes.close()
        return touts
    if mode == "ffn1":
        touts, _ = ffn_phase(x_in, out_t, "A", tcon)
    elif mode == "attn":
        t1, tp1 = ffn_phase(x_in, resid1, "A", tcon)
        touts = attn_phase(resid1, out_t, pr.fence(), src_tok=t1, ptr_tok=tp1)
    elif mode == "full":
        t1, tp1 = ffn_phase(x_in, resid1, "A", tcon)
        t2 = attn_phase(resid1, resid2, pr.fence(), src_tok=t1, ptr_tok=tp1)
        touts, _ = ffn_phase(resid2, out_t, "B", pr.fence(), src_tok=t2)
    elif mode.startswith("m"):
        touts, _ = ffn_phase(x_in, out_t, "A", tcon, stop=int(mode[1:]))
    else:
        raise NotImplementedError(mode)

    pr.op("Y", lambda e: None, waits=pr.fence())
    pr.emit()
    es_top.close()
    return nc


def _consts(core):
    ident = np.eye(P).astype(ml_dtypes.bfloat16)
    slmask = np.tril(np.ones((P, P)), -1).astype(ml_dtypes.bfloat16)
    utmask = np.triu(np.ones((P, P)), 0).astype(ml_dtypes.bfloat16)
    ones8f = np.ones((EXP, 1), np.float32)
    ones8b = np.ones((EXP, 1)).astype(ml_dtypes.bfloat16)
    ones1b = np.ones((1, P)).astype(ml_dtypes.bfloat16)
    onescb = np.ones((P, 1)).astype(ml_dtypes.bfloat16)
    ident4 = np.tile(np.eye(P), (1, 4)).astype(ml_dtypes.bfloat16)
    ut4 = np.tile(np.triu(np.ones((P, P)), 0), (1, 4)).astype(ml_dtypes.bfloat16)
    pert8 = (np.arange(EXP).reshape(EXP, 1) * PERT).astype(np.float32)
    eoff8 = (np.arange(EXP).reshape(EXP, 1) * float(CAP // 4)).astype(np.float32)
    selmat = np.zeros((NCORE * EXP, EXP), np.float32)
    for c in range(core):
        for e in range(EXP):
            selmat[8 * c + e, e] = 1.0
    lselmat = np.zeros((NCORE * EXP, NCORE), np.float32)
    for cdst in range(NCORE):
        for csrc in range(cdst):
            lselmat[8 * csrc + core, cdst] = 1.0
    selcol = np.zeros((EXP, P), np.float32)
    for pp in range(P):
        selcol[pp // 16, pp] = 1.0
    pp = np.arange(P).reshape(P, 1)
    tokid = ((pp % 16 // 4) * (NCORE * 1024) + pp // 16 * 1024
             + (pp % 4) * 256
             + np.arange(256).reshape(1, 256)).astype(np.int16)
    coreoff = np.array([[core * NTOK]], np.int32)
    return dict(ident=ident, slmask=slmask, utmask=utmask, ones8f=ones8f,
                ones8b=ones8b, ones1b=ones1b, onescb=onescb, ident4=ident4,
                ut4c=ut4, pert8=pert8, eoff8=eoff8,
                selmat=selmat, lselmat=lselmat, selcol=selcol, tokid=tokid,
                coreoff=coreoff)


_BUILD_CACHE = {}


def kernel(L0_attn_x, L0_attn_Wq, L0_attn_Wk, L0_attn_Wv, L0_attn_Wb,
           L0_attn_Wg, L0_attn_Wo, L0_attn_norm_w, L0_ffn_Wr, L0_ffn_W1,
           L0_ffn_W2, _mode="full"):
    if _mode not in _BUILD_CACHE:
        ncb = build(_mode)
        mybir.codegen_inst_isa_subclasses(ncb)
        _BUILD_CACHE[_mode] = ncb
    ncb = _BUILD_CACHE[_mode]
    x = np.ascontiguousarray(np.asarray(L0_attn_x, np.float32))
    w = {k: np.ascontiguousarray(np.asarray(v, np.float32)) for k, v in dict(
        Wq=L0_attn_Wq, Wk=L0_attn_Wk, Wv=L0_attn_Wv, Wb=L0_attn_Wb,
        Wg=L0_attn_Wg, Wo=L0_attn_Wo, NW=L0_attn_norm_w, Wr=L0_ffn_Wr).items()}
    W1 = np.ascontiguousarray(np.asarray(L0_ffn_W1, np.float32))
    W2 = np.ascontiguousarray(np.asarray(L0_ffn_W2, np.float32))
    in_maps = []
    for c in range(NCORE):
        m = dict(x=x[c], W1e=W1[c], W2e=W2[c], **w, **_consts(c))
        in_maps.append(m)
    res = run_bass_kernel_spmd(ncb, in_maps, core_ids=list(range(NCORE)))
    return np.stack([np.asarray(res.results[c]["out"]) for c in range(NCORE)], 0)


# version 15
# speedup vs baseline: 1.0341x; 1.0165x over previous
import numpy as np
import ml_dtypes
from contextlib import ExitStack

import concourse.bass as bass
import concourse.mybir as mybir
import bass_rust
import concourse.library_config as libcfg
from concourse.bass import IndirectOffsetOnAxis
from concourse.bass_utils import run_bass_kernel_spmd

F32 = mybir.dt.float32
BF16 = mybir.dt.bfloat16
I16 = mybir.dt.int16
I32 = mybir.dt.int32
U8 = mybir.dt.uint8
ALU = mybir.AluOpType
AF = mybir.ActivationFunctionType
AX = mybir.AxisListType.X

P = 128
DM = 1024
NTOK = 4096
NBLK = 32
EXP = 8
CAP = 5120
DFF = 4096
H = 4
DK = 128
DV = 256
NCHUNK = 32
NSPAN = 4
SSPAN = 256
NSSPAN = CAP // SSPAN
NCORE = 8
EPS_LN = 1e-5
EPS_ON = 1e-5
EPS_K = 1e-12
SCALE_Q = float(DK) ** -0.5
PAD_NEG = -3.0e38
PERT = 1e-6


class Prog:

    def __init__(self, nc, es):
        self.nc = nc
        self.es = es
        self.ops = {e: [] for e in "TVSGY"}
        self.sems = {}
        self.cnt = {}
        self.last = {}

    def sem(self, name):
        if name not in self.sems:
            self.sems[name] = self.es.enter_context(self.nc.semaphore(name))
            self.cnt[name] = 0
        return self.sems[name]

    BULK = ("q_y", "q_g", "q_s")

    def op(self, eng, fn, waits=(), inc=None, amt=1):
        ws = []
        for w in waits:
            if w is None:
                continue
            s, v = w
            if s in self.BULK:
                v = self.cnt.get(s, 0)
            ws.append((s, v))
        if eng in "VSG" and amt == 1 and self.last.get(eng) is not None:
            ws.append(self.last[eng])
        tok = None
        if inc is not None:
            self.sem(inc)
            self.cnt[inc] += amt
            tok = (inc, self.cnt[inc])
            if eng in "VSG" and amt == 1:
                self.last[eng] = tok
        self.ops[eng].append((fn, ws, inc, amt))
        return tok

    def dma(self, eng, out, in_, waits=(), sem=None, **kw):
        q = sem or {"Y": "q_y", "G": "q_g", "S": "q_s"}[eng]
        return self.op(eng, lambda e: e.dma_start(out=out, in_=in_, **kw),
                       waits, inc=q, amt=16)

    def fence(self):
        return [(s, self.cnt[s]) for s in self.cnt if self.cnt[s] > 0]

    def emit(self):
        nc = self.nc
        with nc.Block() as blk:
            decos = {"T": blk.tensor, "V": blk.vector, "S": blk.scalar,
                     "G": blk.gpsimd, "Y": blk.sync}
            for eng in "TVSGY":
                if not self.ops[eng]:
                    continue
                ops = self.ops[eng]

                def run(e, ops=ops):
                    seen = {}
                    for fn, ws, inc, amt in ops:
                        for (s, v) in ws:
                            if seen.get(s, -1) < v:
                                e.wait_ge(self.sems[s], v)
                                seen[s] = v
                        inst = fn(e)
                        if inc is not None:
                            inst.then_inc(self.sems[inc], amt)
                decos[eng](run)


def build(mode="full"):
    nc = bass.Bass("TRN2", target_bir_lowering=False)

    x_in = nc.dram_tensor("x", [NTOK, DM], F32, kind="ExternalInput")
    Wqd = nc.dram_tensor("Wq", [DM, H * DK], F32, kind="ExternalInput")
    Wkd = nc.dram_tensor("Wk", [DM, H * DK], F32, kind="ExternalInput")
    Wvd = nc.dram_tensor("Wv", [DM, H * DV], F32, kind="ExternalInput")
    Wbd = nc.dram_tensor("Wb", [DM, H], F32, kind="ExternalInput")
    Wgd = nc.dram_tensor("Wg", [DM, H * DV], F32, kind="ExternalInput")
    Wod = nc.dram_tensor("Wo", [H * DV, DM], F32, kind="ExternalInput")
    NWd = nc.dram_tensor("NW", [DV], F32, kind="ExternalInput")
    Wrd = nc.dram_tensor("Wr", [DM, EXP], F32, kind="ExternalInput")
    W1d = nc.dram_tensor("W1e", [DM, DFF], F32, kind="ExternalInput")
    W2d = nc.dram_tensor("W2e", [DFF, DM], F32, kind="ExternalInput")
    identd = nc.dram_tensor("ident", [P, P], BF16, kind="ExternalInput")
    slmaskd = nc.dram_tensor("slmask", [P, P], BF16, kind="ExternalInput")
    utmaskd = nc.dram_tensor("utmask", [P, P], BF16, kind="ExternalInput")
    ones8fd = nc.dram_tensor("ones8f", [EXP, 1], F32, kind="ExternalInput")
    ones8bd = nc.dram_tensor("ones8b", [EXP, 1], BF16, kind="ExternalInput")
    ones1bd = nc.dram_tensor("ones1b", [1, P], BF16, kind="ExternalInput")
    onescbd = nc.dram_tensor("onescb", [P, 1], BF16, kind="ExternalInput")
    pert8d = nc.dram_tensor("pert8", [EXP, 1], F32, kind="ExternalInput")
    eoff8d = nc.dram_tensor("eoff8", [EXP, 1], F32, kind="ExternalInput")
    selmatd = nc.dram_tensor("selmat", [NCORE * EXP, EXP], F32, kind="ExternalInput")
    lselmatd = nc.dram_tensor("lselmat", [NCORE * EXP, NCORE], F32, kind="ExternalInput")
    selcold = nc.dram_tensor("selcol", [EXP, P], F32, kind="ExternalInput")
    tokidd = nc.dram_tensor("tokid", [P, 256], I16, kind="ExternalInput")
    ident4d = nc.dram_tensor("ident4", [P, 512], BF16, kind="ExternalInput")
    ut4d = nc.dram_tensor("ut4c", [P, 512], BF16, kind="ExternalInput")
    coreoffd = nc.dram_tensor("coreoff", [1, 1], I32, kind="ExternalInput")
    out_t = nc.dram_tensor("out", [NTOK, DM], F32, kind="ExternalOutput")

    resid1 = nc.dram_tensor("resid1", [NTOK, DM], F32)
    resid2 = nc.dram_tensor("resid2", [NTOK, DM], F32)
    xln_in = nc.dram_tensor("xln_in", [NTOK, DM], BF16)
    xln_all = nc.dram_tensor("xln_all", [NCORE * NTOK, DM], BF16, addr_space="Shared")
    spm_in = nc.dram_tensor("spm_in", [EXP, NTOK], F32)
    spm_all = nc.dram_tensor("spm_all", [NCORE * EXP, NTOK], F32, addr_space="Shared")
    eout_in = nc.dram_tensor("eout_in", [CAP, DM], BF16)
    QCAP = CAP // 4
    eout_all = nc.dram_tensor("eout_all", [4 * NCORE * QCAP, DM], BF16, addr_space="Shared")
    tblrow = nc.dram_tensor("tblrow", [1, 5376], I16)
    tblD = nc.dram_tensor("tblD", [P, 5376], I16)
    pinvrow = nc.dram_tensor("pinvrow", [1, NTOK], F32)
    RG = [list(range(NCORE))]

    es_top = ExitStack()
    es = es_top.enter_context
    pr = Prog(nc, es_top)
    greg = es(nc.gpsimd.register("greg"))

    identb = es(nc.sbuf_tensor("identb", [P, P], BF16))
    slm = es(nc.sbuf_tensor("slm", [P, P], BF16))
    utm = es(nc.sbuf_tensor("utm", [P, P], BF16))
    on8f = es(nc.sbuf_tensor("on8f", [EXP, 1], F32))
    on8b = es(nc.sbuf_tensor("on8b", [EXP, 1], BF16))
    on1b = es(nc.sbuf_tensor("on1b", [1, P], BF16))
    oncb = es(nc.sbuf_tensor("oncb", [P, 1], BF16))
    p8 = es(nc.sbuf_tensor("p8", [EXP, 1], F32))
    e8 = es(nc.sbuf_tensor("e8", [EXP, 1], F32))
    selm = es(nc.sbuf_tensor("selm", [NCORE * EXP, EXP], F32))
    lselm = es(nc.sbuf_tensor("lselm", [NCORE * EXP, NCORE], F32))
    selc = es(nc.sbuf_tensor("selc", [EXP, P], F32))
    tok16 = es(nc.sbuf_tensor("tok16", [P, 256], I16))
    id4 = es(nc.sbuf_tensor("id4", [P, 512], BF16))
    ut4 = es(nc.sbuf_tensor("ut4", [P, 512], BF16))
    wrsb = es(nc.sbuf_tensor("wrsb", [P, EXP * EXP], BF16))
    nwsb = es(nc.sbuf_tensor("nwsb", [P, 2], F32))
    co32 = es(nc.sbuf_tensor("co32", [1, 1], I32))
    flat32 = es(nc.sbuf_tensor("flat32", [P, NBLK], I32))
    af_t = es(nc.sbuf_tensor("af_t", [P, NBLK], F32))
    bf_t = es(nc.sbuf_tensor("bf_t", [P, NBLK], F32))
    msum = es(nc.sbuf_tensor("msum", [P, NBLK], F32))
    sqsum = es(nc.sbuf_tensor("sqsum", [P, NBLK], F32))
    mval_t = es(nc.sbuf_tensor("mval_t", [P, NBLK], F32))
    var_t = es(nc.sbuf_tensor("var_t", [P, NBLK], F32))
    rcp_t = es(nc.sbuf_tensor("rcp_t", [P, NBLK], F32))
    rstd_t = es(nc.sbuf_tensor("rstd_t", [P, NBLK], F32))
    nm_t = es(nc.sbuf_tensor("nm_t", [P, NBLK], F32))
    lntrash = es(nc.sbuf_tensor("lntrash", [P, DM], BF16))
    xringT = [es(nc.sbuf_tensor(f"xringT{i}", [P, DM], F32)) for i in range(2)]
    xlnbT = [es(nc.sbuf_tensor(f"xlnbT{i}", [P, DM], BF16)) for i in range(2)]
    xafT = es(nc.sbuf_tensor("xafT", [P, EXP * 512], BF16))
    psTRg = es(nc.psum_tensor("psTRg", [P, 1024], BF16))
    tblT32 = es(nc.sbuf_tensor("tblT32", [P, CAP // P], I32))

    tcon = []
    for dst, src in [(identb, identd), (slm, slmaskd), (utm, utmaskd),
                     (on8f, ones8fd), (on8b, ones8bd), (on1b, ones1bd),
                     (oncb, onescbd),
                     (p8, pert8d), (e8, eoff8d), (selm, selmatd),
                     (lselm, lselmatd), (selc, selcold), (tok16, tokidd),
                     (co32, coreoffd), (id4, ident4d), (ut4, ut4d)]:
        tcon.append(pr.dma("Y", dst[:, :], src[:, :]))
    t_wr = pr.dma("G", wrsb[:, :].rearrange("p (k c) -> p k c", c=EXP),
                  bass.AP(Wrd, 0, [[EXP, P], [P * EXP, EXP], [1, EXP]]),
                  sem="s_wr")
    tcon.append(t_wr)
    t_nw = pr.dma("Y", nwsb[:, :], bass.AP(NWd, 0, [[1, P], [P, 2]]),
                  allow_slow_non_contiguous=True)
    tcon.append(t_nw)
    import os as _os
    if _os.environ.get("SKIPLIB0"):
        t_lib0 = None
    else:
        t_lib0 = pr.op("G", lambda g: (g.load_library(libcfg.attn),
                                       g.nop())[1],
                       pr.fence(), inc="slib")
        tcon.append(t_lib0)
    tok16_tok = tcon[11]
    co_tok = tcon[12]

    def emit_ln(src, deps, sink, xring, xlnb, sC, sS, blk0=0, nblk=NBLK,
                ring_tok=None, blk_deps=None):
        GRP = 2
        toks = [None] * (blk0 + nblk)
        t_s = [None] * (blk0 + nblk)
        t_x = [None] * (blk0 + nblk)
        t_v = [None] * (blk0 + nblk)
        for g in range(blk0 // GRP, (blk0 + nblk) // GRP):
            for j in range(GRP * g, GRP * g + GRP):
                rb = xring[j % GRP]
                w = list(deps) if blk_deps is None else list(blk_deps[j])
                if j - GRP >= blk0:
                    w += [toks[j - GRP], t_s[j - GRP]]
                elif ring_tok is not None and ring_tok[j % GRP] is not None:
                    w += [ring_tok[j % GRP]]
                t_x[j] = pr.dma("Y", rb[:, :], src[j * P:(j + 1) * P, :],
                                waits=w, sem=f"lnx{j % GRP}")
                t_v[j] = pr.op("V", lambda v, j=j, rb=rb: v.reduce_sum(
                    msum[:, j:j + 1], rb[:, :], AX), [t_x[j]], inc=sC)
                t_s[j] = pr.op("S", lambda s, j=j, rb=rb: s.activation(
                    lntrash[:, :], rb[:, :], AF.Square,
                    accum_out=sqsum[:, j:j + 1]),
                    [t_x[j], t_s[j - 1] if j else None], inc=sS)
            j = GRP * g + GRP - 1
            sl = slice(j - GRP + 1, j + 1)
            pr.op("V", lambda v, sl=sl: v.tensor_scalar(
                mval_t[:, sl], msum[:, sl], 1.0 / DM, None, ALU.mult),
                [t_v[j]], inc=sC)
            pr.op("V", lambda v, sl=sl: v.tensor_scalar(
                var_t[:, sl], sqsum[:, sl], 1.0 / DM, EPS_LN, ALU.mult,
                ALU.add), [t_s[j]], inc=sC)
            pr.op("V", lambda v, sl=sl: v.tensor_mul(
                rcp_t[:, sl], mval_t[:, sl], mval_t[:, sl]), inc=sC)
            pr.op("V", lambda v, sl=sl: v.tensor_sub(
                var_t[:, sl], var_t[:, sl], rcp_t[:, sl]), inc=sC)
            tv2 = pr.op("V", lambda v, sl=sl: v.reciprocal(
                rcp_t[:, sl], var_t[:, sl]), inc=sC)
            tq = pr.op("S", lambda s, sl=sl: s.activation(
                rstd_t[:, sl], rcp_t[:, sl], AF.Sqrt), [tv2], inc=sS)
            pr.op("V", lambda v, sl=sl: v.tensor_mul(
                nm_t[:, sl], mval_t[:, sl], rstd_t[:, sl]), [tq], inc=sC)
            tgp = pr.op("V", lambda v, sl=sl: v.tensor_scalar(
                nm_t[:, sl], nm_t[:, sl], -1.0, None, ALU.mult), inc=sC)
            for jj in range(j - GRP + 1, j + 1):
                xb = xlnb[jj % 2]
                w2 = [tgp, t_x[jj]]
                if jj - 2 >= blk0 and toks[jj - 2] is not None:
                    w2.append(toks[jj - 2])
                if jj % 2 == 1:
                    tx = pr.op("S", lambda s, jj=jj, xb=xb: s.activation(
                        xb[:, :], xring[jj % GRP][:, :], AF.Identity,
                        scale=rstd_t[:, jj:jj + 1],
                        bias=nm_t[:, jj:jj + 1]), w2, inc=sS)
                else:
                    tx = pr.op("V", lambda v, jj=jj, xb=xb: v.tensor_scalar(
                        xb[:, :], xring[jj % GRP][:, :], rstd_t[:, jj:jj + 1],
                        nm_t[:, jj:jj + 1], ALU.mult, ALU.add), w2, inc=sC)
                toks[jj] = sink(jj, xb, tx)
        return toks, t_s

    def ffn_phase(src, dst, tag, deps, stop=99, src_tok=None):
        sC, sS, sT, sG, cc = f"v{tag}", f"s{tag}", f"t{tag}", f"g{tag}", "cc"

        def dummy_out():
            tz = pr.op("V", lambda v: v.memset(mval_t[:, :], 0.0), pr.fence(),
                       inc=sC)
            touts = []
            for j in range(NBLK):
                touts.append(pr.dma("Y", dst[j * P:(j + 1) * P, 0:NBLK],
                                    mval_t[:, :], waits=[tz]))
            fes.close()
            return touts, None
        fes = ExitStack()
        fe = fes.enter_context
        w1sb = fe(nc.sbuf_tensor(f"w1sb{tag}", [P, EXP * DFF], BF16))
        xring, xlnb = xringT, xlnbT
        maskA = fe(nc.sbuf_tensor(f"maskA{tag}", [EXP, NTOK], BF16))
        posA = fe(nc.sbuf_tensor(f"posA{tag}", [EXP, NTOK], I16))
        carry8 = fe(nc.sbuf_tensor(f"carry8{tag}", [EXP, 1], F32))
        pinv1 = fe(nc.sbuf_tensor(f"pinv1{tag}", [1, 1024], F32))
        cnt64 = fe(nc.sbuf_tensor(f"cnt64{tag}", [NCORE * EXP, 1], F32))
        ctmp = fe(nc.sbuf_tensor(f"ctmp{tag}", [NCORE * EXP, 1], F32))
        o8pf = fe(nc.sbuf_tensor(f"o8pf{tag}", [EXP, 2], F32))
        o8m1 = fe(nc.sbuf_tensor(f"o8m1{tag}", [EXP, 1], F32))
        kept_f = fe(nc.sbuf_tensor(f"keptf{tag}", [P, NBLK], F32))
        ptok = fe(nc.sbuf_tensor(f"ptok{tag}", [P, NBLK], F32))
        prefm1 = fe(nc.sbuf_tensor(f"prefm1{tag}", [P, 1], F32))
        psL = fe(nc.psum_tensor(f"psL{tag}", [P, 512], F32))
        psTR = psTRg
        psW1 = [fe(nc.psum_tensor(f"psW1{tag}{i}", [P, 512], F32)) for i in range(2)]
        psW2 = [fe(nc.psum_tensor(f"psW2{tag}{i}", [P, 512], F32)) for i in range(2)]

        libw = []
        if tag != "A":
            libw = [pr.op("G", lambda g: (g.load_library(libcfg.attn),
                                          g.nop())[1],
                          pr.fence(), inc=sG)]
        tW1 = pr.dma("G", w1sb[:, :].rearrange("p (k c) -> p k c", c=DFF),
                     bass.AP(W1d, 0, [[DFF, P], [P * DFF, EXP], [1, DFF]]),
                     waits=deps + libw, sem=f"sw1{tag}")
        pr.op("V", lambda v: v.memset(carry8[:, :], 0.0), deps, inc=sC)
        pr.op("V", lambda v: v.memset(cnt64[:, :], 0.0), inc=sC)

        def ln_sink(j, xb, tx):
            return pr.dma("Y", xln_in[j * P:(j + 1) * P, :], xb[:, :],
                          waits=[tx], sem=f"xw{tag}{j % 2}")
        bdeps = None if src_tok is None else [[src_tok[j]] for j in range(NBLK)]
        t_xlnw, _lns = emit_ln(src, deps, ln_sink, xring, xlnb, sC, sS,
                               blk_deps=bdeps)

        if stop <= 1:
            return dummy_out()
        if stop == 15:
            pass
        tAG1 = None
        if stop == 15:
            return dummy_out()

        res = ExitStack()
        re_ = res.enter_context
        xlf = re_(nc.sbuf_tensor(f"xlf{tag}", [P, EXP * 1024], BF16))
        l16 = re_(nc.sbuf_tensor(f"l16{tag}", [2 * EXP, 1024], F32))
        e16 = re_(nc.sbuf_tensor(f"e16{tag}", [2 * EXP, 1024], F32))
        ar16 = re_(nc.sbuf_tensor(f"ar16{tag}", [2 * EXP, 1024], F32))
        cbuf = re_(nc.sbuf_tensor(f"cbuf{tag}", [NCORE * EXP, 1024], F32))
        pr.op("V", lambda v: v.memset(l16[:, :], PAD_NEG), deps, inc=sC)
        pr.op("V", lambda v: v.memset(e16[:, :], 0.0), inc=sC)
        tglib = t_lib0 if tag == "A" else libw[0]
        if stop == 25:
            res.close()
            return dummy_out()
        t_spm = []
        t_pinv = []
        tcp = None
        tex_prev = None
        tpv_prev = None
        tspm_prev = None
        xblk = [re_(nc.sbuf_tensor(f"xblk{tag}{i}", [P, DM], BF16))
                for i in range(2)]
        t_xbe = [None, None]
        tAG1s = []
        for sp in range(NSPAN):
            cols = slice(sp * 1024, (sp + 1) * 1024)
            tAG1s.append(pr.op("G", lambda g, sp=sp: g.collective_compute(
                "AllGather", ALU.bypass, replica_groups=RG,
                ins=[xln_in[sp * 1024:(sp + 1) * 1024, :].opt()],
                outs=[xln_all[sp * NCORE * 1024:
                              (sp + 1) * NCORE * 1024, :].opt()]),
                waits=t_xlnw[sp * 8:(sp + 1) * 8], inc=f"cA{tag}"))
            ttp = []
            for b in range(8):
                i = b % 2
                w = [t_xlnw[sp * 8 + b]] + list(deps)
                if t_xbe[i] is not None:
                    w.append(t_xbe[i])
                if sp > 0 and b < 2:
                    w.append(tcp)
                txb = pr.dma("Y", xblk[i][:, :],
                             xln_in[(sp * 8 + b) * P:(sp * 8 + b + 1) * P, :],
                             waits=w, sem=f"sxb{tag}{i}")
                if stop in (21, 26):
                    t_xbe[i] = txb
                    ttp.append(txb)
                    continue
                ttr = None
                for c in range(EXP):
                    wtr = [txb]
                    if c == 0 and ttp:
                        wtr.append(ttp[-1])
                    ttr = pr.op("T", lambda t, i=i, c=c: t.transpose(
                        psTR[:, c * P:(c + 1) * P],
                        xblk[i][:, c * P:(c + 1) * P], identb[:, :]),
                        wtr, inc=sT)
                if stop == 22:
                    tev = pr.op("V", lambda v: v.tensor_copy(
                        lntrash[:, 0:P], psTR[:, 0:P]), [ttr], inc=sC)
                else:
                    tev = pr.op("V", lambda v, b=b: v.tensor_copy(
                        xlf[:, :].rearrange("p (c f) -> p c f", f=1024)
                        [:, :, b * P:(b + 1) * P],
                        psTR[:, :].rearrange("p (c f) -> p c f", f=P)),
                        [ttr], inc=sC)
                t_xbe[i] = tev
                ttp.append(tev)
            if stop in (16, 21, 22, 26):
                tcp = ttp[-1]
                continue
            for half in range(2):
                tm = None
                for k in range(EXP):
                    tm = pr.op("T", lambda t, k=k, half=half: t.matmul(
                        psL[0:EXP, 0:512],
                        wrsb[:, k * EXP:(k + 1) * EXP],
                        xlf[:, k * 1024 + half * 512:k * 1024 + (half + 1) * 512],
                        start=(k == 0), stop=(k == EXP - 1)),
                        [t_wr, ttp[-1]] + ([tcp] if tcp else []), inc=sT)
                tcp = pr.op("V", lambda v, half=half: v.tensor_copy(
                    l16[0:EXP, half * 512:(half + 1) * 512], psL[0:EXP, 0:512]),
                    [tm] + ([tex_prev] if (half == 0 and tex_prev) else []),
                    inc=sC)
            if stop == 17:
                continue
            tpe = pr.op("V", lambda v: v.tensor_scalar(
                l16[0:EXP, :], l16[0:EXP, :], p8[:, 0:1], None, ALU.subtract),
                [tcp], inc=sC)
            tmx = pr.op("G", lambda g: g.partition_all_reduce(
                ar16[:, :], l16[:, :], 2 * EXP, bass_rust.ReduceOp.max),
                [tpe, tglib] + ([tpv_prev] if tpv_prev else []), inc=sG)
            tsh = pr.op("V", lambda v: v.tensor_sub(
                l16[0:EXP, :], l16[0:EXP, :], ar16[0:EXP, :]), [tmx], inc=sC)
            tmk = pr.op("V", lambda v, cols=cols: v.tensor_single_scalar(
                maskA[:, cols], l16[0:EXP, :], 0.0, ALU.is_equal), [tsh], inc=sC)
            if stop == 18:
                continue
            tex = pr.op("S", lambda s: s.activation(
                e16[0:EXP, :], l16[0:EXP, :], AF.Exp),
                [tsh] + ([tspm_prev] if tspm_prev else []), inc=sS)
            tex_prev = tex
            tse = pr.op("G", lambda g: g.partition_all_reduce(
                ar16[:, :], e16[:, :], 2 * EXP, bass_rust.ReduceOp.add),
                [tex], inc=sG)
            tpv = pr.op("V", lambda v: v.reciprocal(pinv1[:, :], ar16[0:1, :]),
                        [tse] + (t_pinv[-1:] if t_pinv else []), inc=sC)
            tpv_prev = tpv
            t_pinv.append(pr.dma("Y", pinvrow[0:1, sp * 1024:(sp + 1) * 1024],
                                 pinv1[:, :], waits=[tpv], sem=f"spv{tag}"))
            if stop == 19:
                continue
            tsc = pr.op("V", lambda v, cols=cols: v.tensor_tensor_scan(
                posA[:, cols], maskA[:, cols], maskA[:, cols], carry8[:, 0:1],
                ALU.add, ALU.max), [tmk], inc=sC)
            pr.op("V", lambda v, sp=sp: v.tensor_copy(
                carry8[:, :], posA[:, (sp + 1) * 1024 - 1:(sp + 1) * 1024]),
                [tsc], inc=sC)
            tsp = pr.op("V", lambda v, cols=cols: v.tensor_mul(
                e16[0:EXP, :], posA[:, cols], maskA[:, cols]),
                [tsc, tse, tpv], inc=sC)
            tspm_prev = pr.dma("Y", spm_in[:, sp * 1024:(sp + 1) * 1024],
                               e16[0:EXP, :], waits=[tsp], sem=f"ssp{tag}")
            t_spm.append(tspm_prev)

        if stop <= 2 or stop in (16, 17, 18, 19, 21, 22, 26):
            res.close()
            return dummy_out()
        tAG2 = pr.op("G", lambda g: g.collective_compute(
            "AllGather", ALU.bypass, replica_groups=RG,
            ins=[spm_in.ap().opt()], outs=[spm_all.ap().opt()]),
            waits=t_spm, inc=f"cB{tag}")

        tcn = None
        for q in range(4):
            tq = pr.dma("Y", cbuf[:, :], spm_all[:, q * 1024:(q + 1) * 1024],
                        waits=[tAG2] + ([tcn] if tcn else []), sem=f"scb{tag}")
            tr = pr.op("V", lambda v: v.reduce_max(ctmp[:, :], cbuf[:, :], AX),
                       [tq], inc=sC)
            tcn = pr.op("V", lambda v: v.tensor_max(
                cnt64[:, :], cnt64[:, :], ctmp[:, :]), [tr], inc=sC)
        tof = pr.op("T", lambda t: t.matmul(
            psL[0:EXP, 0:1], selm[:, :], cnt64[:, :]), [tcn, tcp], inc=sT)
        tpf = pr.op("T", lambda t: t.matmul(
            psL[0:EXP, 1:2], lselm[:, :], cnt64[:, :]), [tcn], inc=sT)
        tcpo = pr.op("V", lambda v: v.tensor_copy(
            o8pf[:, :], psL[0:EXP, 0:2]), [tof, tpf], inc=sC)
        to8 = pr.op("V", lambda v: v.tensor_scalar(
            o8m1[:, :], o8pf[:, 0:1], -1.0, None, ALU.add), [tcpo], inc=sC)
        tpc = pr.op("T", lambda t: t.matmul(
            psL[:, 2:3], selc[:, :], o8pf[:, 1:2]), [tcpo], inc=sT)
        tpm = pr.op("V", lambda v: v.tensor_scalar(
            prefm1[:, :], psL[:, 2:3], -1.0, None, ALU.add), [tpc], inc=sC)

        if stop <= 3:
            res.close()
            return dummy_out()
        res.close()
        fx = pr.fence()

        ex = ExitStack()
        xe = ex.enter_context
        gpA = xe(nc.sbuf_tensor(f"gpA{tag}", [EXP, NTOK], F32))
        keptmA = xe(nc.sbuf_tensor(f"keptmA{tag}", [EXP, NTOK], BF16))
        wtA = xe(nc.sbuf_tensor(f"wtA{tag}", [EXP, NTOK], F32))
        myexp = xe(nc.sbuf_tensor(f"myexp{tag}", [P, 256], F32))
        spf = xe(nc.sbuf_tensor(f"spf{tag}", [P, 256], F32))
        c1t = xe(nc.sbuf_tensor(f"c1t{tag}", [P, 256], F32))
        c2t = xe(nc.sbuf_tensor(f"c2t{tag}", [P, 256], F32))
        cm8 = xe(nc.sbuf_tensor(f"cm8{tag}", [P, 256], U8))
        negt = xe(nc.sbuf_tensor(f"negt{tag}", [P, 256], F32))
        sel_o = xe(nc.sbuf_tensor(f"selo{tag}", [P, 256], F32))
        sel2_o = xe(nc.sbuf_tensor(f"sel2o{tag}", [P, 256], F32))
        sp16 = xe(nc.sbuf_tensor(f"sp16{tag}", [P, 256], I16))
        tbl1 = xe(nc.sbuf_tensor(f"tbl1{tag}", [P, 5376], I16))
        tb1f = xe(nc.sbuf_tensor(f"tb1f{tag}", [P, 5376], F32))
        evrow = xe(nc.sbuf_tensor(f"evrow{tag}", [1, 5376], I16))
        ones1f = xe(nc.sbuf_tensor(f"ones1f{tag}", [P, 1], F32))
        tbl16 = xe(nc.sbuf_tensor(f"tbl16{tag}", [P, CAP // P], I16))
        spr = xe(nc.sbuf_tensor(f"spr{tag}", [P, 256], F32))

        tgp = pr.op("V", lambda v: v.tensor_scalar(
            gpA[:, :], posA[:, :], o8m1[:, 0:1], None, ALU.add),
            [to8] + fx, inc=sC)
        tk1 = pr.op("V", lambda v: v.tensor_single_scalar(
            keptmA[:, :], gpA[:, :], float(CAP) - 0.5, ALU.is_lt), [tgp], inc=sC)
        tk2 = pr.op("V", lambda v: v.tensor_mul(
            keptmA[:, :], keptmA[:, :], maskA[:, :]), [tk1], inc=sC)
        tw1_ = pr.op("V", lambda v: v.tensor_scalar(
            wtA[:, :], gpA[:, :], e8[:, 0:1], None, ALU.add), [tk1], inc=sC)
        tw1b = tw1_
        for qi in range(1, 4):
            tqm = pr.op("V", lambda v, qi=qi: v.tensor_single_scalar(
                posA[:, :], gpA[:, :], qi * float(CAP // 4) - 0.5, ALU.is_gt),
                [tw1b, tk1], inc=sC)
            tw1b = pr.op("V", lambda v: v.scalar_tensor_tensor(
                wtA[:, :], posA[:, :], float((NCORE - 1) * (CAP // 4)),
                wtA[:, :], ALU.mult, ALU.add), [tqm], inc=sC)
        tw2_ = pr.op("V", lambda v: v.tensor_mul(
            wtA[:, :], wtA[:, :], keptmA[:, :]), [tw1b, tk2], inc=sC)

        tkx = tfx = None
        for j in range(NBLK):
            tkx = pr.op("T", lambda t, j=j: t.matmul(
                psL[:, 32 + j:33 + j], keptmA[:, j * P:(j + 1) * P], on8b[:, :]),
                [tk2, tpm], inc=sT)
            tfx = pr.op("T", lambda t, j=j: t.matmul(
                psL[:, 64 + j:65 + j], wtA[:, j * P:(j + 1) * P], on8f[:, :]),
                [tw2_], inc=sT)
        tke = pr.op("V", lambda v: v.tensor_copy(
            kept_f[:, :], psL[:, 32:32 + NBLK]), [tkx], inc=sC)
        tfe = pr.op("V", lambda v: v.tensor_copy(
            flat32[:, :], psL[:, 64:64 + NBLK]), [tfx], inc=sC)
        tptk = pr.dma("Y", ptok[:, :],
                      bass.AP(pinvrow, 0, [[1, P], [P, NBLK]]), waits=t_pinv,
                      allow_slow_non_contiguous=True, sem=f"spt{tag}")
        taf = pr.op("V", lambda v: v.scalar_tensor_tensor(
            af_t[:, :], ptok[:, :], 0.5, kept_f[:, :], ALU.mult, ALU.mult),
            [tke, tptk], inc=sC)
        tbf = pr.op("V", lambda v: v.tensor_scalar(
            bf_t[:, :], kept_f[:, :], -0.5, 0.5, ALU.mult, ALU.add),
            [tke], inc=sC)

        if stop <= 4:
            ex.close()
            return dummy_out()
        trl = pr.op("G", lambda g: (g.reg_load(greg, co32[0:1, 0:1]),
                                    g.nop())[1],
                    [co_tok, tAG2], inc=sG)
        tme = pr.dma("G", myexp[:, :],
                     bass.AP(spm_all, greg,
                             [[EXP * NTOK, EXP], [256, 16], [1, 256]]),
                     waits=[trl] + fx, sem=f"sme{tag}")
        if stop == 51:
            ex.close()
            return dummy_out()
        tng = pr.op("V", lambda v: v.memset(negt[:, :], -1.0), fx, inc=sC)
        tsf = pr.op("V", lambda v: v.tensor_scalar(
            spf[:, :], myexp[:, :], prefm1[:, 0:1], None, ALU.add),
            [tme, tpm], inc=sC)
        tc1 = pr.op("V", lambda v: v.tensor_single_scalar(
            c1t[:, :], myexp[:, :], 0.5, ALU.is_gt), [tme], inc=sC)
        tc2 = pr.op("V", lambda v: v.tensor_single_scalar(
            c2t[:, :], spf[:, :], float(CAP) - 0.5, ALU.is_lt), [tsf], inc=sC)
        tcm = pr.op("V", lambda v: v.tensor_mul(
            cm8[:, :], c1t[:, :], c2t[:, :]), [tc1, tc2], inc=sC)
        tsl0 = pr.op("V", lambda v: v.tensor_copy(
            sel_o[:, :], negt[:, :]), [tng], inc=sC)
        tsl = pr.op("V", lambda v: v.copy_predicated(
            sel_o[:, :], cm8[:, :], spf[:, :]), [tcm, tsl0], inc=sC)
        tll = pr.op("G", lambda g: (g.load_library(libcfg.local_scatter),
                                    g.nop())[1],
                    pr.fence(), inc=sG)
        tsc_ = None
        for rpass in range(3):
            base = rpass * 1792
            td1 = pr.op("V", lambda v, base=base: v.tensor_scalar(
                spr[:, :], sel_o[:, :], -float(base), None, ALU.add),
                [tsl] + ([tsc_] if tsc_ else []), inc=sC)
            tc1b = pr.op("V", lambda v: v.tensor_single_scalar(
                c1t[:, :], spr[:, :], -0.5, ALU.is_gt), [td1], inc=sC)
            tc2b = pr.op("V", lambda v: v.tensor_single_scalar(
                c2t[:, :], spr[:, :], 1791.5, ALU.is_lt), [td1], inc=sC)
            tcmb = pr.op("V", lambda v: v.tensor_mul(
                cm8[:, :], c1t[:, :], c2t[:, :]), [tc1b, tc2b], inc=sC)
            tslb0 = pr.op("V", lambda v: v.tensor_copy(
                sel2_o[:, :], negt[:, :]), [tng], inc=sC)
            tslb = pr.op("V", lambda v: v.copy_predicated(
                sel2_o[:, :], cm8[:, :], spr[:, :]), [tcmb, tslb0], inc=sC)
            t16b = pr.op("V", lambda v: v.tensor_copy(
                sp16[:, :], sel2_o[:, :]), [tslb], inc=sC)
            tsc_ = pr.op("G", lambda g, base=base: g.local_scatter(
                tbl1[:, base:base + 1792], tok16[:, :], sp16[:, :], P, 1792,
                256), [t16b, tll, tok16_tok] + ([tsc_] if tsc_ else []),
                inc=sG)
        if stop == 52:
            ex.close()
            return dummy_out()
        tcst = pr.op("V", lambda v: v.tensor_copy(tb1f[:, :], tbl1[:, :]),
                     [tsc_], inc=sC)
        tone = pr.op("V", lambda v: v.memset(ones1f[:, :], 1.0), fx, inc=sC)
        tev_ = None
        tevs = []
        for ci in range(11):
            c0 = ci * 512
            cw = 512 if ci < 10 else 256
            wv = [tcst, tone]
            if ci == 0:
                wv += [tke, tfe]
            if tev_ is not None:
                wv.append(tev_)
            tmmc = pr.op("T", lambda t, c0=c0, cw=cw: t.matmul(
                psL[0:1, 0:cw], ones1f[:, :], tb1f[:, c0:c0 + cw],
                start=True, stop=True), wv, inc=sT)
            tev_ = pr.op("V", lambda v, c0=c0, cw=cw:
                         v.tensor_copy(evrow[0:1, c0:c0 + cw],
                                       psL[0:1, 0:cw]), [tmmc], inc=sC)
            tevs.append(tev_)
        if stop == 53:
            ex.close()
            return dummy_out()
        ttr1 = pr.dma("Y", tblrow[0:1, :], evrow[0:1, :], waits=tevs,
                      sem=f"st1{tag}")
        ttr2 = pr.dma("Y", tbl16[:, :],
                      bass.AP(tblrow, 0, [[1, P], [P, CAP // P]]), waits=[ttr1],
                      allow_slow_non_contiguous=True, sem=f"st2{tag}")
        tt32 = pr.op("V", lambda v: v.tensor_copy(
            tblT32[:, :], tbl16[:, :]), [ttr2], inc=sC)

        if stop <= 5:
            ex.close()
            return dummy_out()
        ex.close()
        fx2 = pr.fence()

        me = ExitStack()
        mm = me.enter_context
        w2sb = mm(nc.sbuf_tensor(f"w2sb{tag}", [P, NBLK * DM], BF16))
        dispTM = [mm(nc.sbuf_tensor(f"dtm{tag}{i}", [P, DM], BF16)) for i in range(2)]
        dispF = mm(nc.sbuf_tensor(f"dispF{tag}", [P, EXP * SSPAN], BF16))
        hT = mm(nc.sbuf_tensor(f"hT{tag}", [P, NBLK * SSPAN], BF16))
        zt = [mm(nc.sbuf_tensor(f"zt{tag}{i}", [P, DM], BF16)) for i in range(2)]

        tW2 = pr.dma("G", w2sb[:, :].rearrange("p (m c) -> p m c", c=DM),
                     bass.AP(W2d, 0, [[DM, P], [P * DM, NBLK], [1, DM]]),
                     waits=fx2, sem=f"sw2{tag}")

        t_z = []
        t_gprev = [None, None]
        t_dfprev = None
        t_zdma = [None, None]
        t_relu = [None] * NBLK
        tAGq = []
        for w in range(NSSPAN):
            tdf = []
            for b2 in range(2):
                blk = 2 * w + b2
                wg = tAG1s + [tt32] + fx2
                if t_gprev[b2] is not None:
                    wg.append(t_gprev[b2])
                tg = pr.op("G", lambda g, b2=b2, blk=blk: g.indirect_dma_start(
                    dispTM[b2][:, :], None, xln_all[:, :],
                    IndirectOffsetOnAxis(tblT32[:, blk:blk + 1], 0)),
                    wg, inc=f"sdg{tag}{b2}", amt=16)
                ttr = None
                for c in range(EXP):
                    wtr = [tg]
                    if c == 0 and t_dfprev is not None:
                        wtr.append(t_dfprev)
                    if c == 0 and tdf:
                        wtr.append(tdf[-1])
                    ttr = pr.op("T", lambda t, b2=b2, c=c: t.transpose(
                        psTR[:, c * P:(c + 1) * P],
                        dispTM[b2][:, c * P:(c + 1) * P],
                        identb[:, :]), wtr, inc=sT)
                tdf.append(pr.op("V", lambda v, b2=b2: v.tensor_copy(
                    dispF[:, :].rearrange("p (c f) -> p c f", f=SSPAN)
                    [:, :, b2 * P:(b2 + 1) * P],
                    psTR[:, :].rearrange("p (c f) -> p c f", f=P)),
                    [ttr], inc=sC))
                t_gprev[b2] = tdf[-1]
            for m in range(NBLK):
                pw = psW1[m % 2]
                wv = tdf + [tW1]
                if m >= 2:
                    wv.append(t_relu[m - 2])
                elif t_relu[m - 2 + NBLK] is not None:
                    wv.append(t_relu[m - 2 + NBLK])
                tm = None
                for k in range(EXP):
                    tm = pr.op("T", lambda t, k=k, m=m, pw=pw: t.matmul(
                        pw[:, 0:SSPAN],
                        w1sb[:, k * DFF + m * P:k * DFF + (m + 1) * P],
                        dispF[:, k * SSPAN:(k + 1) * SSPAN],
                        start=(k == 0), stop=(k == EXP - 1)),
                        wv if k == 0 else [], inc=sT)
                if m % 2 == 0:
                    t_relu[m] = pr.op("V", lambda v, m=m, pw=pw: v.tensor_scalar(
                        hT[:, m * SSPAN:(m + 1) * SSPAN], pw[:, 0:SSPAN], 0.0,
                        None, ALU.max), [tm], inc=sC)
                else:
                    t_relu[m] = pr.op("S", lambda s, m=m, pw=pw: s.activation(
                        hT[:, m * SSPAN:(m + 1) * SSPAN], pw[:, 0:SSPAN],
                        AF.Relu), [tm], inc=sS)
            t_dfprev = tm
            for b2 in range(2):
                pz = psW2[b2]
                tz_half = None
                for half in range(2):
                    tmm = None
                    for m in range(NBLK):
                        wv = [t_relu[m], tW2]
                        if half == 1 and tz_half is not None:
                            wv.append(tz_half)
                        if half == 0 and t_zdma[b2] is not None:
                            wv.append(t_zdma[b2])
                        tmm = pr.op("T", lambda t, m=m, b2=b2, pz=pz, half=half:
                                    t.matmul(
                            pz[:, :],
                            hT[:, m * SSPAN + b2 * P:m * SSPAN + (b2 + 1) * P],
                            w2sb[:, m * DM + half * 512:m * DM + (half + 1) * 512],
                            start=(m == 0), stop=(m == NBLK - 1)),
                            wv, inc=sT)
                    if half == 0:
                        tz_half = pr.op("S", lambda s, b2=b2, half=half, pz=pz:
                                        s.copy(
                            zt[b2][:, half * 512:(half + 1) * 512], pz[:, :]),
                            [tmm], inc=sS)
                    else:
                        tz_half = pr.op("V", lambda v, b2=b2, half=half, pz=pz:
                                        v.tensor_copy(
                            zt[b2][:, half * 512:(half + 1) * 512], pz[:, :]),
                            [tmm], inc=sC)
                t_zdma[b2] = pr.dma(
                    "Y", eout_in[(2 * w + b2) * P:(2 * w + b2 + 1) * P, :],
                    zt[b2][:, :], waits=[tz_half], sem=f"szw{tag}{b2}")
                t_z.append(t_zdma[b2])
            if (w + 1) % (NSSPAN // 4) == 0 and w + 1 < NSSPAN:
                qq = (w + 1) // (NSSPAN // 4) - 1
                tAGq.append(pr.op("G", lambda g, qq=qq: g.collective_compute(
                    "AllGather", ALU.bypass, replica_groups=RG,
                    ins=[eout_in[qq * QCAP:(qq + 1) * QCAP, :].opt()],
                    outs=[eout_all[qq * NCORE * QCAP:
                                   (qq + 1) * NCORE * QCAP, :].opt()]),
                    waits=t_z[qq * (NSSPAN // 2):] +
                          ([tAG2] if qq == 0 else []), inc=f"cC{tag}"))

        tAG3 = pr.op("G", lambda g: g.collective_compute(
            "AllGather", ALU.bypass, replica_groups=RG,
            ins=[eout_in[3 * QCAP:CAP, :].opt()],
            outs=[eout_all[3 * NCORE * QCAP:4 * NCORE * QCAP, :].opt()]),
            waits=t_z[3 * (NSSPAN // 2):], inc=f"cC{tag}")

        if stop <= 6:
            me.close()
            return dummy_out()
        me.close()
        fx3 = pr.fence()

        ce = ExitStack()
        cm_ = ce.enter_context
        NCR = 4
        eoutg = [cm_(nc.sbuf_tensor(f"eog{tag}{i}", [P, DM], BF16)) for i in range(NCR)]
        xlt = [cm_(nc.sbuf_tensor(f"xlt{tag}{i}", [P, DM], BF16)) for i in range(NCR)]
        rsb = [cm_(nc.sbuf_tensor(f"rsb{tag}{i}", [P, DM], F32)) for i in range(NCR)]
        ctm = [cm_(nc.sbuf_tensor(f"ctm{tag}{i}", [P, DM], F32)) for i in range(NCR)]
        ott = [cm_(nc.sbuf_tensor(f"ott{tag}{i}", [P, DM], F32)) for i in range(NCR)]
        touts = []
        tprevc = [None] * NCR
        for j in range(NBLK):
            i = j % NCR
            wg = tAGq + [tAG3, tfe, taf, tbf] + fx3
            if tprevc[i] is not None:
                wg.append(tprevc[i])
                wg.append(touts[j - NCR])
            tg = pr.op("G", lambda g, j=j, i=i: g.indirect_dma_start(
                eoutg[i][:, :], None, eout_all[:, :],
                IndirectOffsetOnAxis(flat32[:, j:j + 1], 0)),
                wg, inc=f"seg{tag}{i}", amt=16)
            tx = pr.dma("G", xlt[i][:, :], xln_in[j * P:(j + 1) * P, :],
                        waits=wg, sem=f"sxl{tag}{i}")
            trs = pr.dma("G", rsb[i][:, :], src[j * P:(j + 1) * P, :], waits=wg,
                         sem=f"srs{tag}{i}")
            t1 = pr.op("V", lambda v, j=j, i=i: v.scalar_tensor_tensor(
                ctm[i][:, :], eoutg[i][:, :], af_t[:, j:j + 1], rsb[i][:, :],
                ALU.mult, ALU.add), [tg, trs], inc=sC)
            t2 = pr.op("V", lambda v, j=j, i=i: v.scalar_tensor_tensor(
                ott[i][:, :], xlt[i][:, :], bf_t[:, j:j + 1], ctm[i][:, :],
                ALU.mult, ALU.add), [t1, tx], inc=sC)
            tprevc[i] = t2
            touts.append(pr.dma("Y", dst[j * P:(j + 1) * P, :], ott[i][:, :],
                                waits=[t2], sem=f"sow{tag}{i}"))
        ce.close()
        fes.close()
        return touts, t_gprev[1] or t_gprev[0]


    def attn_phase(src, dst, deps, src_tok=None, ptr_tok=None):
        sC, sS, sT, sG = "vX", "sX", "tX", "gX"
        ASP = 512
        NASP = NTOK // ASP
        CPA = ASP // P
        aes = ExitStack()
        ae = aes.enter_context
        wqs = ae(nc.sbuf_tensor("wqs", [P, EXP * 512], BF16))
        wks = ae(nc.sbuf_tensor("wks", [P, EXP * 512], BF16))
        wvs = ae(nc.sbuf_tensor("wvs", [P, EXP * 1024], BF16))
        wgs = ae(nc.sbuf_tensor("wgs", [P, EXP * 1024], BF16))
        wos = ae(nc.sbuf_tensor("wos", [P, EXP * 1024], BF16))
        wbs = ae(nc.sbuf_tensor("wbs", [P, EXP * H], BF16))
        xr2, xl2 = xringT, xlnbT
        xaf = xafT
        qf = ae(nc.sbuf_tensor("qf", [P, H * ASP], BF16))
        kfr = ae(nc.sbuf_tensor("kfr", [P, H * ASP], BF16))
        ktok = ae(nc.sbuf_tensor("ktok", [P, CPA * 512], BF16))
        vbt = ae(nc.sbuf_tensor("vbt", [P, CPA * 1024], BF16))
        gf = ae(nc.sbuf_tensor("gf", [P, 2 * H * ASP], BF16))
        betaT = ae(nc.sbuf_tensor("betaT", [P, CPA * H], F32))
        negbT = ae(nc.sbuf_tensor("negbT", [P, CPA * H], F32))
        kss = ae(nc.sbuf_tensor("kss", [P, CPA * H], F32))
        krs = ae(nc.sbuf_tensor("krs", [P, CPA * H], F32))
        kfn2 = [ae(nc.sbuf_tensor(f"kfn{i}", [P, 512], BF16)) for i in range(2)]
        sNB2 = [ae(nc.sbuf_tensor(f"sNB{i}", [P, 512], BF16)) for i in range(2)]
        sNBT2 = [ae(nc.sbuf_tensor(f"sNBT{i}", [P, 512], BF16)) for i in range(2)]
        powA2 = [[ae(nc.sbuf_tensor(f"powA{i}{jj}", [P, 512], BF16))
                  for jj in range(2)] for i in range(2)]
        powT2 = [[ae(nc.sbuf_tensor(f"powT{i}{jj}", [P, 512], BF16))
                  for jj in range(2)] for i in range(2)]
        sX2 = [ae(nc.sbuf_tensor(f"sX{i}", [P, 512], BF16)) for i in range(2)]
        nbk2 = [ae(nc.sbuf_tensor(f"nbk{i}", [P, 512], BF16)) for i in range(2)]
        sW1c2 = [ae(nc.sbuf_tensor(f"sW1c{i}", [P, 1024], BF16)) for i in range(2)]
        sW2cT2 = [ae(nc.sbuf_tensor(f"sW2cT{i}", [P, 512], BF16)) for i in range(2)]
        smk2 = [ae(nc.sbuf_tensor(f"smk{i}", [P, 512], BF16)) for i in range(2)]
        sD = ae(nc.sbuf_tensor("sD", [P, 1024], BF16))
        sS0 = ae(nc.sbuf_tensor("sS0", [P, 1024], BF16))
        sSf = ae(nc.sbuf_tensor("sSf", [P, 1024], F32))
        oTs = [ae(nc.sbuf_tensor(f"oTs{i}", [P, CPA * 512], BF16)) for i in range(2)]
        sqb = ae(nc.sbuf_tensor("sqb", [P, CPA * 512], BF16))
        rr1 = ae(nc.sbuf_tensor("rr1", [1, CPA * 512], F32))
        epsT = ae(nc.sbuf_tensor("epsT", [1, 1], F32))
        scT = ae(nc.sbuf_tensor("scT", [1, 1], F32))
        rrb = ae(nc.sbuf_tensor("rrb", [1, CPA * 512], BF16))
        rbc = ae(nc.sbuf_tensor("rbc", [P, CPA * 512], BF16))
        sgt = [ae(nc.sbuf_tensor(f"sgt{i}", [P, 512], BF16)) for i in range(2)]
        rs2 = [ae(nc.sbuf_tensor(f"rs2{i}", [P, DM], F32)) for i in range(2)]
        yout = [ae(nc.sbuf_tensor(f"yout{i}", [P, DM], F32)) for i in range(2)]
        psS = ae(nc.psum_tensor("psS", [P, 512], F32))
        psX = ae(nc.psum_tensor("psX", [P, 512], F32))
        psDW = ae(nc.psum_tensor("psDW", [P, 512], F32))
        psQT = ae(nc.psum_tensor("psQT", [P, 512], F32))
        psOp = ae(nc.psum_tensor("psOp", [P, 1024], F32))
        psEp = ae(nc.psum_tensor("psEp", [P, 512], F32))
        psT2 = psTRg

        tw = {}
        for nm, sb, dr, cw in [("q", wqs, Wqd, 512), ("k", wks, Wkd, 512),
                               ("v", wvs, Wvd, 1024), ("g", wgs, Wgd, 1024),
                               ("o", wos, Wod, 1024), ("b", wbs, Wbd, H)]:
            tw[nm] = pr.dma(
                "G", sb[:, :].rearrange("p (k c) -> p k c", c=cw),
                bass.AP(dr, 0, [[cw, P], [P * cw, EXP], [1, cw]]),
                waits=deps, sem=f"aw{nm}")

        tS0m = pr.op("V", lambda v: v.memset(sS0[:, :], 0.0), deps, inc=sC)
        teps = pr.op("V", lambda v: v.memset(epsT[:, :], EPS_ON), deps, inc=sC)
        tsc_c = pr.op("V", lambda v: v.memset(scT[:, :], 1.0 / DV), deps, inc=sC)
        tSadd = pr.op("V", lambda v: v.memset(sSf[:, :], 0.0), deps, inc=sC)
        touts = []
        tSupd = [None] * H
        tS0c = tS0m
        tDc = None
        tXc = None
        tOev = None
        tWo = None
        tYev = [None, None]
        tYw = [None, None]
        tRs = [None, None]
        tNBT_prev = None
        tmkm_last = None
        tneglast = None
        tgsil = None
        tvev = None
        tr1_p = tr1b_p = None
        tbc_p = None
        tEfree = tQfree = tXfree = tDWfree = None
        tknG = [None, None]
        tkng = [None] * CPA
        lnring = [None, None]
        prev_evac = [None]

        for a in range(NASP):
            def asink(j, xb, tx, a=a):
                bb = j - 4 * a
                tev = None
                for wv in range(2):
                    trs = None
                    for cc4 in range(4):
                        cch = wv * 4 + cc4
                        wtr = [tx]
                        if cc4 == 0:
                            wtr += [prev_evac[0], ptr_tok]
                        trs = pr.op("T", lambda t, cch=cch, cc4=cc4, xb=xb:
                                    t.transpose(
                            psT2[:, 512 + cc4 * P:512 + (cc4 + 1) * P],
                            xb[:, cch * P:(cch + 1) * P],
                            identb[:, :]), wtr, inc=sT)
                    tev = pr.op("V", lambda v, bb=bb, wv=wv: v.tensor_copy(
                        xaf[:, :].rearrange("p (c f) -> p c f", f=ASP)
                        [:, 4 * wv:4 * wv + 4, bb * P:(bb + 1) * P],
                        psT2[:, 512:1024].rearrange("p (c f) -> p c f", f=P)),
                        [trs], inc=sC)
                    prev_evac[0] = tev
                return tev

            abdeps = None
            if src_tok is not None:
                abdeps = [None] * NBLK
                for jj in range(NBLK):
                    abdeps[jj] = [src_tok[jj]]
            t_ln, t_lns = emit_ln(src, deps, asink, xr2, xl2, sC, sS,
                                  blk0=4 * a, nblk=4, ring_tok=lnring,
                                  blk_deps=abdeps)
            lnring = [t_lns[4 * a + 2], t_lns[4 * a + 3]]
            txaf = t_ln[4 * a + 3]

            projps = [(psDW, 0), (psX, 0), (psOp, 0), (psOp, 512)]
            pv = [[t for t in (tDWfree,) if t is not None],
                  [t for t in (tXfree,) if t is not None],
                  [t for t in (tYev[0], tYev[1], tOev, tr1b_p) if t is not None],
                  [t for t in (tYev[0], tYev[1], tOev, tr1b_p) if t is not None]]
            nps = [0]

            def proj(outsb, outcol, wsb, wcol, scale, toks, slots=(0, 1)):
                slot = slots[nps[0] % len(slots)]
                ps, off = projps[slot]
                nps[0] += 1
                cwid = wsb.shape[1] // EXP
                tm = None
                for k in range(EXP):
                    w = list(toks) + [txaf] + (pv[slot] if k == 0 else [])
                    tm = pr.op("T", lambda t, k=k, ps=ps, off=off: t.matmul(
                        ps[:, off:off + 512],
                        wsb[:, k * cwid + wcol:k * cwid + wcol + P],
                        xaf[:, k * ASP:k * ASP + 512],
                        start=(k == 0), stop=(k == EXP - 1)),
                        w if k == 0 else [], inc=sT)
                if scale is None:
                    if nps[0] % 2 == 0:
                        te = pr.op("S", lambda s, ps=ps, off=off: s.copy(
                            outsb[:, outcol:outcol + 512], ps[:, off:off + 512]),
                            [tm], inc=sS)
                    else:
                        te = pr.op("V", lambda v, ps=ps, off=off: v.tensor_copy(
                            outsb[:, outcol:outcol + 512], ps[:, off:off + 512]),
                            [tm], inc=sC)
                else:
                    te = pr.op("S", lambda s, ps=ps, off=off: s.mul(
                        outsb[:, outcol:outcol + 512], ps[:, off:off + 512],
                        scale), [tm], inc=sS)
                pv[slot] = [te]
                return te, slot

            tqev = tkev = tgev = None
            for m in range(H):
                tqev, _sl = proj(qf, m * ASP, wqs, m * P, SCALE_Q, [tw["q"]])
            for m in range(H):
                tkev, _sl = proj(kfr, m * ASP, wks, m * P, None, [tw["k"]])
            for m in range(2 * H):
                tgev, _sl = proj(gf, m * ASP, wgs, m * P, None,
                                 [tw["g"], tgsil], slots=(2, 3))
            for bb in range(CPA):
                tmb = None
                for k in range(EXP):
                    w = [tw["b"], txaf]
                    if k == 0:
                        w += [tmkm_last, tneglast, tbc_p, tEfree]
                    tmb = pr.op("T", lambda t, k=k, bb=bb: t.matmul(
                        psEp[:, bb * H:(bb + 1) * H],
                        xaf[:, k * ASP + bb * P:k * ASP + (bb + 1) * P],
                        wbs[:, k * H:(k + 1) * H],
                        start=(k == 0), stop=(k == EXP - 1)),
                        w if k == 0 else [], inc=sT)
                tsig = pr.op("S", lambda s, bb=bb: s.activation(
                    betaT[:, bb * H:(bb + 1) * H], psEp[:, bb * H:(bb + 1) * H],
                    AF.Sigmoid), [tmb], inc=sS)
                tneglast = pr.op("V", lambda v, bb=bb: v.tensor_scalar(
                    negbT[:, bb * H:(bb + 1) * H],
                    betaT[:, bb * H:(bb + 1) * H], -1.0, None, ALU.mult),
                    [tsig], inc=sC)
                for half in range(2):
                    slot = nps[0] % 2
                    ps, off = projps[slot]
                    nps[0] += 1
                    tm = None
                    for k in range(EXP):
                        w = [tw["v"], txaf] + (pv[slot] if k == 0 else [])
                        tm = pr.op("T", lambda t, k=k, bb=bb, half=half, ps=ps,
                                   off=off: t.matmul(
                            ps[:, off:off + 512],
                            xaf[:, k * ASP + bb * P:k * ASP + (bb + 1) * P],
                            wvs[:, k * 1024 + half * 512:
                                k * 1024 + (half + 1) * 512],
                            start=(k == 0), stop=(k == EXP - 1)),
                            w if k == 0 else [], inc=sT)
                    for hh in range(2):
                        h = half * 2 + hh
                        tvev = pr.op("V", lambda v, bb=bb, h=h, hh=hh, ps=ps,
                                     off=off: v.tensor_scalar(
                            vbt[:, bb * 1024 + h * 256:bb * 1024 + (h + 1) * 256],
                            ps[:, off + hh * 256:off + (hh + 1) * 256],
                            betaT[:, bb * H + h:bb * H + h + 1], None,
                            ALU.mult), [tm, tsig], inc=sC)
                    pv[slot] = [tvev]

            tkng_new = [None] * CPA
            tDWfree = pv[0][0] if pv[0] else tDWfree
            tXfree = pv[1][0] if pv[1] else tXfree
            for cl in range(CPA):
                c = a * CPA + cl
                par = c % 2
                kfn = kfn2[par]
                sNB, sNBT, sX = sNB2[par], sNBT2[par], sX2[par]
                powA, powT = powA2[par], powT2[par]
                nbk, sW1c, sW2cT, smk = (nbk2[par], sW1c2[par], sW2cT2[par],
                                         smk2[par])
                tts = []
                for h in range(H):
                    w = [tkev]
                    if h == 0:
                        w += [tkng[cl], tNBT_prev, tknG[par]]
                    tts.append(pr.op("T", lambda t, h=h, cl=cl: t.transpose(
                        psT2[:, h * P:(h + 1) * P],
                        kfr[:, h * ASP + cl * P:h * ASP + (cl + 1) * P],
                        identb[:, :]), w if h == 0 else [], inc=sT))
                tka = None
                for h in range(H):
                    tka = pr.op("S", lambda s, h=h, cl=cl: s.activation(
                        lntrash[:, 0:P], psT2[:, h * P:(h + 1) * P], AF.Square,
                        accum_out=kss[:, cl * H + h:cl * H + h + 1]),
                        [tts[h]], inc=sS)
                sl = slice(cl * H, (cl + 1) * H)
                pr.op("V", lambda v, sl=sl: v.tensor_scalar(
                    krs[:, sl], kss[:, sl], 1.0, EPS_K, ALU.mult, ALU.add),
                    [tka], inc=sC)
                tk2 = pr.op("V", lambda v, sl=sl: v.reciprocal(
                    krs[:, sl], krs[:, sl]), inc=sC)
                tk3 = pr.op("S", lambda s, sl=sl: s.activation(
                    krs[:, sl], krs[:, sl], AF.Sqrt), [tk2], inc=sS)
                tktk = None
                for h in range(H):
                    tktk = pr.op("V", lambda v, h=h, cl=cl: v.tensor_scalar(
                        ktok[:, cl * 512 + h * P:cl * 512 + (h + 1) * P],
                        psT2[:, h * P:(h + 1) * P],
                        krs[:, cl * H + h:cl * H + h + 1], None, ALU.mult),
                        [tk3], inc=sC)
                tts2 = []
                for h in range(H):
                    tts2.append(pr.op("T", lambda t, h=h, cl=cl: t.transpose(
                        psT2[:, h * P:(h + 1) * P],
                        ktok[:, cl * 512 + h * P:cl * 512 + (h + 1) * P],
                        identb[:, :]), [tktk] if h == 0 else [], inc=sT))
                tkfn = pr.op("S", lambda s, kfn=kfn: s.copy(
                    kfn[:, :], psT2[:, 0:512]), [tts2[-1]], inc=sS)
                tknG[par] = tkfn
                tG = None
                for h in range(H):
                    w = [tkfn]
                    if h == 0:
                        w += [tneglast, tEfree]
                    tG = pr.op("T", lambda t, h=h, kfn=kfn: t.matmul(
                        psEp[:, h * P:(h + 1) * P], kfn[:, h * P:(h + 1) * P],
                        kfn[:, h * P:(h + 1) * P]), w if h == 0 else [], inc=sT)
                tNB = None
                for h in range(H):
                    tNB = pr.op("V", lambda v, h=h, cl=cl, sNB=sNB:
                                v.scalar_tensor_tensor(
                        sNB[:, h * P:(h + 1) * P], psEp[:, h * P:(h + 1) * P],
                        negbT[:, cl * H + h:cl * H + h + 1], slm[:, :],
                        ALU.mult, ALU.mult), [tG], inc=sC)
                tts3 = []
                for h in range(H):
                    tts3.append(pr.op("T", lambda t, h=h, sNB=sNB: t.transpose(
                        psT2[:, h * P:(h + 1) * P], sNB[:, h * P:(h + 1) * P],
                        identb[:, :]), [tNB, tkfn] if h == 0 else [], inc=sT))
                tNBT = pr.op("S", lambda s, sNBT=sNBT: s.copy(
                    sNBT[:, :], psT2[:, 0:512]), [tts3[-1]], inc=sS)
                tNBT_prev = tNBT
                tX = pr.op("V", lambda v, sX=sX, sNBT=sNBT: v.tensor_add(
                    sX[:, :], sNBT[:, :], id4[:, :]), [tNBT], inc=sC)
                pA, pAT = sNB, sNBT
                tpa = tNBT
                tpat = tNBT
                for j in range(1, 7):
                    tsq = None
                    for h in range(H):
                        w = [tpa, tpat]
                        if h == 0:
                            w.append(tEfree)
                        tsq = pr.op("T", lambda t, h=h, pA=pA, pAT=pAT: t.matmul(
                            psEp[:, h * P:(h + 1) * P],
                            pAT[:, h * P:(h + 1) * P],
                            pA[:, h * P:(h + 1) * P]), w if h == 0 else [],
                            inc=sT)
                    npA = powA[j % 2]
                    tpa_n = pr.op("S", lambda s, npA=npA: s.copy(
                        npA[:, :], psEp[:, 0:512]), [tsq], inc=sS)
                    tEfree = tpa_n
                    if j < 6:
                        tsqt = None
                        for h in range(H):
                            w = [tpa, tpat]
                            if h == 0:
                                w.append(tQfree)
                            tsqt = pr.op("T", lambda t, h=h, pA=pA, pAT=pAT:
                                         t.matmul(
                                psQT[:, h * P:(h + 1) * P],
                                pA[:, h * P:(h + 1) * P],
                                pAT[:, h * P:(h + 1) * P]),
                                w if h == 0 else [], inc=sT)
                        npAT = powT[j % 2]
                        tpat_n = pr.op("S", lambda s, npAT=npAT: s.copy(
                            npAT[:, :], psQT[:, 0:512]), [tsqt], inc=sS)
                        tQfree = tpat_n
                    txm = None
                    for h in range(H):
                        w = [tpa_n, tX]
                        if h == 0:
                            w.append(tXfree)
                        txm = pr.op("T", lambda t, h=h, sX=sX: t.matmul(
                            psX[:, h * P:(h + 1) * P], identb[:, :],
                            sX[:, h * P:(h + 1) * P], start=True, stop=False),
                            w if h == 0 else [], inc=sT)
                        txm = pr.op("T", lambda t, h=h, npA=npA, sX=sX: t.matmul(
                            psX[:, h * P:(h + 1) * P],
                            npA[:, h * P:(h + 1) * P],
                            sX[:, h * P:(h + 1) * P], start=False, stop=True),
                            [], inc=sT)
                    tX = pr.op("S", lambda s, sX=sX: s.copy(
                        sX[:, :], psX[:, 0:512]), [txm], inc=sS)
                    tXfree = tX
                    if j < 6:
                        pA, pAT = npA, npAT
                        tpa, tpat = tpa_n, tpat_n
                    else:
                        tpa = tpa_n
                tXc = tX
                tw1 = None
                for hh in range(2):
                    tw1m = None
                    for h in (2 * hh, 2 * hh + 1):
                        w = [tX, tvev]
                        if h == 2 * hh:
                            w.append(tXfree)
                        tw1m = pr.op("T", lambda t, h=h, hh=hh, cl=cl, sX=sX:
                                     t.matmul(
                            psX[:, (h % 2) * 256:(h % 2) * 256 + 256],
                            sX[:, h * P:(h + 1) * P],
                            vbt[:, cl * 1024 + h * 256:cl * 1024 + (h + 1) * 256]),
                            w if h == 2 * hh else [], inc=sT)
                    tw1 = pr.op("S", lambda s, hh=hh, sW1c=sW1c: s.copy(
                        sW1c[:, hh * 512:(hh + 1) * 512], psX[:, 0:512]),
                        [tw1m] + ([tw1] if tw1 else []), inc=sS)
                    tXfree = tw1
                tnbk = None
                for h in range(H):
                    tnbk = pr.op("V", lambda v, h=h, cl=cl, nbk=nbk:
                                 v.tensor_scalar(
                        nbk[:, h * P:(h + 1) * P],
                        ktok[:, cl * 512 + h * P:cl * 512 + (h + 1) * P],
                        negbT[:, cl * H + h:cl * H + h + 1], None, ALU.mult),
                        [tktk], inc=sC)
                tw2m = None
                for h in range(H):
                    w = [tnbk, tX, tpa, tEfree]
                    tw2m = pr.op("T", lambda t, h=h, nbk=nbk, sX=sX: t.matmul(
                        psEp[:, h * P:(h + 1) * P], nbk[:, h * P:(h + 1) * P],
                        sX[:, h * P:(h + 1) * P]), w if h == 0 else [], inc=sT)
                tw2 = pr.op("S", lambda s, sW2cT=sW2cT: s.copy(
                    sW2cT[:, :], psEp[:, 0:512]), [tw2m], inc=sS)
                tEfree = tw2
                tmkm = None
                for h in range(H):
                    w = [tw2, tkfn, tqev]
                    tmkm = pr.op("T", lambda t, h=h, cl=cl, kfn=kfn: t.matmul(
                        psEp[:, h * P:(h + 1) * P], kfn[:, h * P:(h + 1) * P],
                        qf[:, h * ASP + cl * P:h * ASP + (cl + 1) * P]),
                        w if h == 0 else [], inc=sT)
                tmkm_last = tmkm
                tmk = pr.op("V", lambda v, smk=smk: v.tensor_mul(
                    smk[:, :], psEp[:, 0:512], ut4[:, :]), [tmkm], inc=sC)
                tEfree = tmk
                if c > 0:
                    tS0c = pr.op("S", lambda s: s.copy(
                        sS0[:, :], sSf[:, :]), [tSadd, tS0c, tOev], inc=sS)
                tDc = None
                for hh in range(2):
                    tDm = None
                    for h in (2 * hh, 2 * hh + 1):
                        w = [tS0c, tw1, tw2]
                        if h == 2 * hh:
                            w.append(tDWfree)
                        tDm = pr.op("T", lambda t, h=h, sW1c=sW1c: t.matmul(
                            psDW[:, (h % 2) * 256:(h % 2) * 256 + 256],
                            identb[:, :],
                            sW1c[:, h * 256:(h + 1) * 256], start=True,
                            stop=False), w if h == 2 * hh else [], inc=sT)
                        tDm = pr.op("T", lambda t, h=h, sW2cT=sW2cT: t.matmul(
                            psDW[:, (h % 2) * 256:(h % 2) * 256 + 256],
                            sW2cT[:, h * P:(h + 1) * P],
                            sS0[:, h * 256:(h + 1) * 256], start=False,
                            stop=True), [], inc=sT)
                    tDh = pr.op("V", lambda v, hh=hh: v.tensor_copy(
                        sD[:, hh * 512:(hh + 1) * 512], psDW[:, 0:512]),
                        [tDm] + ([tDc] if tDc else []), inc=sC)
                    tDWfree = tDh
                    tDc = tDh
                for hh in range(2):
                    tSm = None
                    for h in (2 * hh, 2 * hh + 1):
                        w = [tDc, tktk]
                        if h == 2 * hh:
                            w.append(tSadd)
                        tSm = pr.op("T", lambda t, h=h, cl=cl: t.matmul(
                            psS[:, (h % 2) * 256:(h % 2) * 256 + 256],
                            ktok[:, cl * 512 + h * P:cl * 512 + (h + 1) * P],
                            sD[:, h * 256:(h + 1) * 256]),
                            w if h == 2 * hh else [], inc=sT)
                    tSadd = pr.op("V", lambda v, hh=hh: v.tensor_add(
                        sSf[:, hh * 512:(hh + 1) * 512],
                        sSf[:, hh * 512:(hh + 1) * 512], psS[:, 0:512]),
                        [tSm], inc=sC)
                tkng_new[cl] = tSadd
                tOm = None
                for h in range(H):
                    for f in range(2):
                        w = [tS0c, tDc, tmk, tqev]
                        if h == 0 and f == 0:
                            w += [t for t in (tOev, tYev[0], tYev[1], tWo)
                                  if t is not None]
                        tOm = pr.op("T", lambda t, h=h, f=f, cl=cl, smk=smk:
                                    t.matmul(
                            psOp[:, (2 * h + f) * P:(2 * h + f + 1) * P],
                            sS0[:, h * 256 + f * P:h * 256 + (f + 1) * P],
                            qf[:, h * ASP + cl * P:h * ASP + (cl + 1) * P],
                            start=True, stop=False), w if (h == 0 and f == 0)
                            else [], inc=sT)
                        tOm = pr.op("T", lambda t, h=h, f=f, smk=smk: t.matmul(
                            psOp[:, (2 * h + f) * P:(2 * h + f + 1) * P],
                            sD[:, h * 256 + f * P:h * 256 + (f + 1) * P],
                            smk[:, h * P:(h + 1) * P],
                            start=False, stop=True), [], inc=sT)
                for f in range(2):
                    tOev = pr.op("V", lambda v, f=f, cl=cl: v.tensor_copy(
                        oTs[f][:, cl * 512:(cl + 1) * 512]
                        .rearrange("p (c2 t) -> p c2 t", t=P),
                        psOp[:, :].rearrange("p (h2 s f2) -> p h2 s f2",
                                             s=2, f2=P)[:, :, f, :]),
                        [tOm], inc=sC)
            tkng = tkng_new
            tcs = None
            for f in range(2):
                tsq = pr.op("S", lambda s, f=f: s.activation(
                    sqb[:, :], oTs[f][:, :], AF.Square),
                    [tOev] + ([tcs] if tcs else []), inc=sS)
                for q4 in range(CPA):
                    r0 = 32 * (q4 // 2)
                    off = 512 * (q4 % 2)
                    w = [tsq]
                    if f == 0 and q4 == 0:
                        w += [tOev]
                    tcs = pr.op("T", lambda t, q4=q4, f=f, r0=r0, off=off:
                                t.matmul(
                        psOp[r0:r0 + 1, off:off + 512], oncb[:, 0:1],
                        sqb[:, q4 * 512:(q4 + 1) * 512],
                        start=(f == 0), stop=(f == 1)), w, inc=sT)
            tr1 = pr.op("S", lambda s: s.activation(
                rr1[0:1, 0:1024], psOp[0:1, :], AF.Sqrt,
                bias=epsT[0:1, 0:1], scale=scT[0:1, 0:1]),
                [tcs, teps, tsc_c], inc=sS)
            tr1b = pr.op("S", lambda s: s.activation(
                rr1[0:1, 1024:2048], psOp[32:33, :], AF.Sqrt,
                bias=epsT[0:1, 0:1], scale=scT[0:1, 0:1]), [tcs], inc=sS)
            tr2 = pr.op("V", lambda v: v.reciprocal(rr1[0:1, :], rr1[0:1, :]),
                        [tr1, tr1b], inc=sC)
            tr4 = pr.op("S", lambda s: s.copy(rrb[0:1, :], rr1[0:1, :]),
                        [tr2], inc=sS)
            tr1_p, tr1b_p = tr1, tr1b
            tbc = None
            for cl in range(CPA):
                tbm = None
                for h in range(H):
                    w = [tr4]
                    if h == 0:
                        w.append(tbc if tbc is not None else tcs)
                    tbm = pr.op("T", lambda t, cl=cl, h=h: t.matmul(
                        psOp[:, h * P:(h + 1) * P], on1b[:, :],
                        rrb[0:1, cl * 512 + h * P:cl * 512 + (h + 1) * P]),
                        w if h == 0 else [], inc=sT)
                tbc = pr.op("V", lambda v, cl=cl: v.tensor_copy(
                    rbc[:, cl * 512:(cl + 1) * 512], psOp[:, 0:512]),
                    [tbm], inc=sC)
                tbc_p = tbc
            tognew = None
            togs = [None, None]
            for ch2 in range(2 * H):
                h, f = ch2 // 2, ch2 % 2
                i = ch2 % 2
                w = [tgev, tbc]
                if togs[i] is not None:
                    w.append(togs[i])
                tsg = pr.op("S", lambda s, ch2=ch2, i=i: s.activation(
                    sgt[i][:, :], gf[:, ch2 * ASP:(ch2 + 1) * ASP], AF.Silu),
                    w, inc=sS)
                tsg2 = pr.op("V", lambda v, h=h, i=i: v.tensor_mul(
                    sgt[i][:, :].rearrange("p (c t) -> p c t", t=P),
                    sgt[i][:, :].rearrange("p (c t) -> p c t", t=P),
                    rbc[:, :].rearrange("p (c hh t) -> p c hh t", hh=H, t=P)
                    [:, :, h, :]), [tsg, tbc], inc=sC)
                tognew = pr.op("V", lambda v, h=h, f=f, i=i:
                               v.scalar_tensor_tensor(
                    oTs[f][:, :].rearrange("p (c hh t) -> p c hh t", hh=H, t=P)
                    [:, :, h, :],
                    oTs[f][:, :].rearrange("p (c hh t) -> p c hh t", hh=H, t=P)
                    [:, :, h, :],
                    nwsb[:, f:f + 1],
                    sgt[i][:, :].rearrange("p (c t) -> p c t", t=P),
                    ALU.mult, ALU.mult), [tsg2, tOev], inc=sC)
                togs[i] = tognew
            tgsil = tognew
            for cl in range(CPA):
                j = 4 * a + cl
                i = cl % 2
                wv = [tYev[i]] if tYev[i] else []
                if src_tok is not None:
                    wv.append(src_tok[j])
                tRs[i] = pr.dma("Y", rs2[i][:, :], src[j * P:(j + 1) * P, :],
                                waits=wv + [tognew], sem=f"ars{i}")
                tWo = None
                for half in range(2):
                    for ch2 in range(2 * H):
                        h, f = ch2 // 2, ch2 % 2
                        w2 = [tognew, tw["o"]]
                        if ch2 == 0 and half == 0:
                            w2 += [t for t in (tYev[0], tYev[1], tbc, tr1b)
                                   if t is not None]
                        tWo = pr.op("T", lambda t, cl=cl, half=half, h=h, f=f,
                                    ch2=ch2: t.matmul(
                            psOp[:, half * 512:(half + 1) * 512],
                            oTs[f][:, cl * 512 + h * P:cl * 512 + (h + 1) * P],
                            wos[:, ch2 * 1024 + half * 512:
                                ch2 * 1024 + (half + 1) * 512],
                            start=(ch2 == 0), stop=(ch2 == 2 * H - 1)),
                            w2 if (ch2 == 0 and half == 0) else [], inc=sT)
                tye = None
                for half in range(2):
                    w2 = [tWo, tRs[i]]
                    if tYw[i] is not None:
                        w2.append(tYw[i])
                    tye = pr.op("V", lambda v, i=i, half=half: v.tensor_add(
                        yout[i][:, half * 512:(half + 1) * 512],
                        rs2[i][:, half * 512:(half + 1) * 512],
                        psOp[:, half * 512:(half + 1) * 512]), w2, inc=sC)
                tYev[i] = tye
                tYw[i] = pr.dma("Y", dst[j * P:(j + 1) * P, :], yout[i][:, :],
                                waits=[tye], sem=f"ayw{i}")
                touts.append(tYw[i])
        aes.close()
        return touts
    if mode == "ffn1":
        touts, _ = ffn_phase(x_in, out_t, "A", tcon)
    elif mode == "attn":
        t1, tp1 = ffn_phase(x_in, resid1, "A", tcon)
        touts = attn_phase(resid1, out_t, pr.fence(), src_tok=t1, ptr_tok=tp1)
    elif mode == "full":
        t1, tp1 = ffn_phase(x_in, resid1, "A", tcon)
        t2 = attn_phase(resid1, resid2, pr.fence(), src_tok=t1, ptr_tok=tp1)
        touts, _ = ffn_phase(resid2, out_t, "B", pr.fence(), src_tok=t2)
    elif mode.startswith("m"):
        touts, _ = ffn_phase(x_in, out_t, "A", tcon, stop=int(mode[1:]))
    else:
        raise NotImplementedError(mode)

    pr.op("Y", lambda e: None, waits=pr.fence())
    pr.emit()
    es_top.close()
    return nc


def _consts(core):
    ident = np.eye(P).astype(ml_dtypes.bfloat16)
    slmask = np.tril(np.ones((P, P)), -1).astype(ml_dtypes.bfloat16)
    utmask = np.triu(np.ones((P, P)), 0).astype(ml_dtypes.bfloat16)
    ones8f = np.ones((EXP, 1), np.float32)
    ones8b = np.ones((EXP, 1)).astype(ml_dtypes.bfloat16)
    ones1b = np.ones((1, P)).astype(ml_dtypes.bfloat16)
    onescb = np.ones((P, 1)).astype(ml_dtypes.bfloat16)
    ident4 = np.tile(np.eye(P), (1, 4)).astype(ml_dtypes.bfloat16)
    ut4 = np.tile(np.triu(np.ones((P, P)), 0), (1, 4)).astype(ml_dtypes.bfloat16)
    pert8 = (np.arange(EXP).reshape(EXP, 1) * PERT).astype(np.float32)
    eoff8 = (np.arange(EXP).reshape(EXP, 1) * float(CAP // 4)).astype(np.float32)
    selmat = np.zeros((NCORE * EXP, EXP), np.float32)
    for c in range(core):
        for e in range(EXP):
            selmat[8 * c + e, e] = 1.0
    lselmat = np.zeros((NCORE * EXP, NCORE), np.float32)
    for cdst in range(NCORE):
        for csrc in range(cdst):
            lselmat[8 * csrc + core, cdst] = 1.0
    selcol = np.zeros((EXP, P), np.float32)
    for pp in range(P):
        selcol[pp // 16, pp] = 1.0
    pp = np.arange(P).reshape(P, 1)
    tokid = ((pp % 16 // 4) * (NCORE * 1024) + pp // 16 * 1024
             + (pp % 4) * 256
             + np.arange(256).reshape(1, 256)).astype(np.int16)
    coreoff = np.array([[core * NTOK]], np.int32)
    return dict(ident=ident, slmask=slmask, utmask=utmask, ones8f=ones8f,
                ones8b=ones8b, ones1b=ones1b, onescb=onescb, ident4=ident4,
                ut4c=ut4, pert8=pert8, eoff8=eoff8,
                selmat=selmat, lselmat=lselmat, selcol=selcol, tokid=tokid,
                coreoff=coreoff)


_BUILD_CACHE = {}


def kernel(L0_attn_x, L0_attn_Wq, L0_attn_Wk, L0_attn_Wv, L0_attn_Wb,
           L0_attn_Wg, L0_attn_Wo, L0_attn_norm_w, L0_ffn_Wr, L0_ffn_W1,
           L0_ffn_W2, _mode="full"):
    if _mode not in _BUILD_CACHE:
        ncb = build(_mode)
        mybir.codegen_inst_isa_subclasses(ncb)
        _BUILD_CACHE[_mode] = ncb
    ncb = _BUILD_CACHE[_mode]
    x = np.ascontiguousarray(np.asarray(L0_attn_x, np.float32))
    w = {k: np.ascontiguousarray(np.asarray(v, np.float32)) for k, v in dict(
        Wq=L0_attn_Wq, Wk=L0_attn_Wk, Wv=L0_attn_Wv, Wb=L0_attn_Wb,
        Wg=L0_attn_Wg, Wo=L0_attn_Wo, NW=L0_attn_norm_w, Wr=L0_ffn_Wr).items()}
    W1 = np.ascontiguousarray(np.asarray(L0_ffn_W1, np.float32))
    W2 = np.ascontiguousarray(np.asarray(L0_ffn_W2, np.float32))
    in_maps = []
    for c in range(NCORE):
        m = dict(x=x[c], W1e=W1[c], W2e=W2[c], **w, **_consts(c))
        in_maps.append(m)
    res = run_bass_kernel_spmd(ncb, in_maps, core_ids=list(range(NCORE)))
    return np.stack([np.asarray(res.results[c]["out"]) for c in range(NCORE)], 0)
